# version 14
# baseline (speedup 1.0000x reference)
import math

import numpy as np
import ml_dtypes
import concourse.bass as bass
import concourse.mybir as mybir
from concourse.bass_utils import run_bass_kernel_spmd
from concourse.hw_specs import get_activation_tables

F32 = mybir.dt.float32
BF16 = mybir.dt.bfloat16
I32 = mybir.dt.int32
AF = mybir.ActivationFunctionType
ALU = mybir.AluOpType
AX = mybir.AxisListType

B, S, D, H, DH, KV, INTER = 4, 4096, 2048, 16, 128, 256, 5632
NT_ROW = S // 128
NT_OWN = 16
ND = D // 128
NI = INTER // 128
HALF_CHUNKS = ([0, 1, 6, 7], [2, 3, 4, 5])
NOWN = 2048
SCL = 1.0 / math.sqrt(DH)
EPS = 1e-5
NEG = -1e30

DEBUG = 0
P4_NH = 16
P4_STAGE = 4
UPTO = 5


def act_set_idx(*need):
    import concourse.bass as _b
    nc = _b.Bass("TRN2", target_bir_lowering=False)
    for i, funcs in enumerate(get_activation_tables(nc.m.arch).values()):
        names = {f.name for f in funcs}
        if set(need) <= names:
            return i
    raise ValueError(need)


SET_LNEXP = 6
SET_SILU = 18


def load_act_set(nc, s, idx):
    s.add_instruction(mybir.InstLoadActFuncSet(
        name=nc.get_next_instruction_name(), act_func_set_id=idx, ins=[], outs=[]))


def host_tables():
    rope_half = DH // 4
    freqs = 1.0 / (10000.0 ** (np.arange(rope_half, dtype=np.float32) / np.float32(rope_half)))
    ang = np.outer(np.arange(S, dtype=np.float32), freqs)
    cos = np.repeat(np.cos(ang), 2, axis=-1)
    sin = np.repeat(np.sin(ang), 2, axis=-1)
    bf = ml_dtypes.bfloat16
    return {
        "cosk": np.ascontiguousarray(cos.T).astype(bf),
        "sink": np.ascontiguousarray(sin.T).astype(bf),
        "srotk": np.ascontiguousarray(np.concatenate([sin[:, 32:], sin[:, :32]], 1).T).astype(bf),
        "cosq_full": np.ascontiguousarray((cos * SCL).T).astype(bf),
        "sinq_full": np.ascontiguousarray((sin * SCL).T).astype(bf),
        "srotq_full": np.ascontiguousarray(
            np.concatenate([sin[:, 32:] * SCL, sin[:, :32] * SCL], 1).T).astype(bf),
        "identbf": np.eye(128, dtype=bf),
        "identf": np.eye(128, dtype=np.float32),
        "maskf": np.triu(np.full((128, 128), NEG, np.float32), 1),
        "masktf": np.tril(np.full((128, 128), NEG, np.float32), -1),
        "onesf": np.ones((1, 128), dtype=np.float32),
    }


def own_rows(half):
    return np.concatenate([np.arange(c * 512, (c + 1) * 512) for c in HALF_CHUNKS[half]])


def sched_for(half):
    out = []
    for c in HALF_CHUNKS[half]:
        L = (c + 1) * 512
        out.append((L, [L - 384 + 128 * q for q in range(4)]))
    return out


class Scope:

    def __init__(self):
        self.cms = []

    def sbuf(self, nc, name, shape, dtype, side=None):
        cm = nc.sbuf_tensor(name, shape, dtype, side=side)
        self.cms.append(cm)
        return cm.__enter__()

    def psum(self, nc, name, shape, dtype):
        cm = nc.psum_tensor(name, shape, dtype)
        self.cms.append(cm)
        return cm.__enter__()

    def close(self):
        for cm in reversed(self.cms):
            cm.__exit__(None, None, None)
        self.cms = []


def build_nc():
    nc = bass.Bass("TRN2", target_bir_lowering=False)

    x_row = nc.dram_tensor("x_row", [S, D], F32, kind="ExternalInput").ap()
    x_own = nc.dram_tensor("x_own", [NOWN, D], F32, kind="ExternalInput").ap()
    xT_own = nc.dram_tensor("xT_own", [D, NOWN], F32, kind="ExternalInput").ap()
    wqT = nc.dram_tensor("wqT", [H, 128, D], F32, kind="ExternalInput").ap()
    wkvT = nc.dram_tensor("wkvT", [D, KV], F32, kind="ExternalInput").ap()
    wkh = nc.dram_tensor("wkh", [H, 128, 128], F32, kind="ExternalInput").ap()
    wvh = nc.dram_tensor("wvh", [H, 128, 128], F32, kind="ExternalInput").ap()
    woT = nc.dram_tensor("woT", [ND, 128, D], F32, kind="ExternalInput").ap()
    gateT = nc.dram_tensor("gateT", [NI, 128, D], F32, kind="ExternalInput").ap()
    upT = nc.dram_tensor("upT", [NI, 128, D], F32, kind="ExternalInput").ap()
    downT = nc.dram_tensor("downT", [ND, 128, INTER], F32, kind="ExternalInput").ap()
    gates = nc.dram_tensor("gates", [NI, 128], F32, kind="ExternalInput").ap()
    ups = nc.dram_tensor("ups", [NI, 128], F32, kind="ExternalInput").ap()
    downs = nc.dram_tensor("downs", [ND, 128], F32, kind="ExternalInput").ap()
    cosk = nc.dram_tensor("cosk", [64, S], BF16, kind="ExternalInput").ap()
    srotk = nc.dram_tensor("srotk", [64, S], BF16, kind="ExternalInput").ap()
    cosq = nc.dram_tensor("cosq", [64, NOWN], BF16, kind="ExternalInput").ap()
    srotq = nc.dram_tensor("srotq", [64, NOWN], BF16, kind="ExternalInput").ap()
    identbf = nc.dram_tensor("identbf", [128, 128], BF16, kind="ExternalInput").ap()
    identf = nc.dram_tensor("identf", [128, 128], F32, kind="ExternalInput").ap()
    maskf = nc.dram_tensor("maskf", [128, 128], F32, kind="ExternalInput").ap()
    masktf = nc.dram_tensor("masktf", [128, 128], F32, kind="ExternalInput").ap()
    onesf = nc.dram_tensor("onesf", [1, 128], F32, kind="ExternalInput").ap()

    out = nc.dram_tensor("out", [D, NOWN], F32, kind="ExternalOutput").ap()

    spillkind = "ExternalOutput" if DEBUG >= 2 else "Internal"
    hdn_spill = nc.dram_tensor("hdn_spill", [INTER, NOWN], BF16, kind=spillkind).ap()
    at_spill = nc.dram_tensor("at_spill", [D, NOWN], BF16, kind=spillkind).ap()

    dbg = {}
    if DEBUG:
        dbg["d_kcT"] = nc.dram_tensor("d_kcT", [128, S], BF16, kind="ExternalOutput").ap()
        dbg["d_vc"] = nc.dram_tensor("d_vc", [128, S], BF16, kind="ExternalOutput").ap()
        dbg["d_xnT"] = nc.dram_tensor("d_xnT", [128, ND * NOWN], BF16, kind="ExternalOutput").ap()

    sb = nc.alloc_sbuf_tensor
    scX = Scope()
    xnT = scX.sbuf(nc, "xnT", [128, ND * NOWN], BF16, side="right")
    kcT = sb("kcT", [128, S], BF16)
    vc = sb("vc", [128, S], BF16)
    s_cosk = sb("s_cosk", [64, S], BF16)
    s_srotk = sb("s_srotk", [64, S], BF16)
    s_cosq = scX.sbuf(nc, "s_cosq", [64, NOWN], BF16, side="right")
    s_srotq = scX.sbuf(nc, "s_srotq", [64, NOWN], BF16, side="right")
    s_idb = sb("s_idb", [128, 128], BF16)
    s_idf = sb("s_idf", [128, 128], F32)
    s_mask = sb("s_mask", [128, 128], F32)
    s_maskt = sb("s_maskt", [128, 128], F32)
    s_onesb = sb("s_onesb", [128, 128], BF16)
    s_ones = sb("s_ones", [1, 128], F32)
    wks = sb("wks", [128, H], F32)
    wvs = sb("wvs", [128, H], F32)
    gsv = sb("gsv", [128, NI], F32)
    usv = sb("usv", [128, NI], F32)
    dsv = sb("dsv", [128, ND], F32)

    sem = nc.alloc_semaphore
    sc1 = Scope()
    wkvb = sc1.sbuf(nc, "wkvb", [128, ND * KV], BF16)

    sc0 = Scope()
    whk = sc0.sbuf(nc, "whk", [128, H * 128], F32)
    scr44 = sc0.sbuf(nc, "scr44", [NI, 128], F32)
    scr44b = sc0.sbuf(nc, "scr44b", [NI, 128], F32)
    scr16 = sc0.sbuf(nc, "scr16", [ND, 128], F32)
    p0a = sc0.psum(nc, "p0a", [128, 512], F32)
    s_c0 = sem("s_c0")
    s_g0 = sem("s_g0")
    s_c1 = sem("s_c1")

    with nc.Block() as blk:
        @blk.sync
        def _(sync):
            for dst, src in (
                (s_cosk[:], cosk), (s_cosq[:], cosq),
                (s_srotk[:], srotk), (s_srotq[:], srotq),
                (s_idb[:], identbf), (s_idf[:], identf),
                (s_mask[:], maskf), (s_maskt[:], masktf), (s_ones[:], onesf),
                (scr44[:], gates), (scr44b[:], ups),
                (scr16[:], downs),
                (whk[:].rearrange("p (h k) -> p h k", k=128),
                 wkh.rearrange("h l k -> l h k")),
            ):
                sync.dma_start(dst, src).then_inc(s_c0, 16)

        @blk.gpsimd
        def _(gp):
            gp.memset(s_onesb[:], 1.0)
            gp.dma_start(wkvb[:].rearrange("p (j k) -> p j k", k=KV),
                         wkvT.rearrange("(j l) k -> l j k", l=128)).then_inc(s_g0, 16)

        @blk.vector
        def _(v):
            v.wait_ge(s_c0, 13 * 16)
            v.wait_ge(s_g0, 16)
            for h in range(H):
                v.reduce_sum(wks[:, h:h + 1], whk[:, h * 128:(h + 1) * 128], axis=AX.X)
            v.nop().then_inc(s_c1, 1)

        @blk.tensor
        def _(t):
            t.wait_ge(s_c0, 13 * 16)
            t.transpose(p0a[0:128, 0:NI], scr44[:], s_idf[0:NI, 0:NI])
            t.transpose(p0a[0:128, 64:64 + NI], scr44b[:], s_idf[0:NI, 0:NI])
            t.transpose(p0a[0:128, 128:128 + ND], scr16[:], s_idf[0:ND, 0:ND]).then_inc(s_c1, 1)

        @blk.scalar
        def _(s):
            load_act_set(nc, s, SET_LNEXP)
            s.wait_ge(s_c1, 2)
            s.copy(gsv[:], p0a[0:128, 0:NI])
            s.copy(usv[:], p0a[0:128, 64:64 + NI])
            s.copy(dsv[:], p0a[0:128, 128:128 + ND])

    with nc.Block() as blk:
        @blk.sync
        def _(sync):
            sync.dma_start(whk[:].rearrange("p (h k) -> p h k", k=128),
                           wvh.rearrange("h l k -> l h k")).then_inc(s_c0, 16)

        @blk.vector
        def _(v):
            v.wait_ge(s_c0, 14 * 16)
            for h in range(H):
                v.reduce_sum(wvs[:, h:h + 1], whk[:, h * 128:(h + 1) * 128], axis=AX.X)

    sc0.close()

    xbuf = [sc1.sbuf(nc, f"xbuf{i}", [128, D], F32) for i in range(3)]
    xsq = sc1.sbuf(nc, "xsq", [128, D], BF16)
    xnb = [sc1.sbuf(nc, f"xnb{i}", [128, D], BF16) for i in range(2)]
    xns = [sc1.sbuf(nc, f"xns{i}", [128, ND * 128], BF16) for i in range(2)]
    kctile = [sc1.sbuf(nc, f"kctile{i}", [128, 128], BF16) for i in range(2)]
    sm_sumx = sc1.sbuf(nc, "sm_sumx", [128, 2], F32)
    sm_sumsq = sc1.sbuf(nc, "sm_sumsq", [128, 2], F32)
    sm_mean = sc1.sbuf(nc, "sm_mean", [128, 2], F32)
    sm_msqe = sc1.sbuf(nc, "sm_msqe", [128, 2], F32)
    sm_varp = sc1.sbuf(nc, "sm_varp", [128, 2], F32)
    sm_lnv = sc1.sbuf(nc, "sm_lnv", [128, 2], F32)
    sm_rstd = sc1.sbuf(nc, "sm_rstd", [128, 2], F32)
    ptr = [sc1.psum(nc, f"ptr{i}", [128, 1024], BF16) for i in range(4)]
    pkv = [sc1.psum(nc, f"pkv{i}", [128, 512], F32) for i in range(2)]
    pkt = [sc1.psum(nc, f"pkt{i}", [128, 1024], BF16) for i in range(2)]

    s_x = [sem(f"s_x{i}") for i in range(3)]
    s_sq = sem("s_sq")
    s_var = sem("s_var")
    s_rstd = sem("s_rstd")
    s_xnb = sem("s_xnb")
    s_tr = sem("s_tr")
    s_xe = sem("s_xe")
    s_kv = sem("s_kv")
    s_kce = sem("s_kce")
    s_vce = sem("s_vce")
    s_kt = sem("s_kt")
    s_kte = sem("s_kte")
    s_vch = sem("s_vch")
    s_sch = sem("s_sch")

    NTOT = NT_ROW + NT_OWN

    with nc.Block() as blk:
        @blk.sync
        def _(sync):
            for ti in range(NTOT):
                if ti >= 3:
                    sync.wait_ge(s_sq, ti - 2)
                    sync.wait_ge(s_xnb, ti - 2)
                src = x_row[ti * 128:(ti + 1) * 128, :] if ti < NT_ROW else \
                    x_own[(ti - NT_ROW) * 128:(ti - NT_ROW + 1) * 128, :]
                sync.dma_start(xbuf[ti % 3][:], src).then_inc(s_x[ti % 3], 16)

        @blk.vector
        def _(v):
            nch = 0
            for tt in range(NTOT + 1):
                if tt < NTOT:
                    ti, p = tt, tt % 2
                    v.wait_ge(s_x[ti % 3], 16 * (ti // 3 + 1))
                    v.reduce_sum(sm_sumx[:, p:p + 1], xbuf[ti % 3][:], axis=AX.X).then_inc(s_vch, 1); nch += 1
                    v.wait_ge(s_vch, nch)
                    v.tensor_scalar(sm_mean[:, p:p + 1], sm_sumx[:, p:p + 1], 1.0 / D,
                                    None, ALU.mult).then_inc(s_vch, 1); nch += 1
                    v.wait_ge(s_vch, nch)
                    v.tensor_scalar(sm_msqe[:, p:p + 1], sm_mean[:, p:p + 1],
                                    sm_mean[:, p:p + 1], EPS, ALU.mult, ALU.subtract).then_inc(s_vch, 1); nch += 1
                    v.wait_ge(s_sq, ti + 1)
                    v.wait_ge(s_vch, nch)
                    v.tensor_scalar(sm_varp[:, p:p + 1], sm_sumsq[:, p:p + 1], 1.0 / D,
                                    sm_msqe[:, p:p + 1], ALU.mult, ALU.subtract).then_inc(s_var, 1)
                if tt >= 1:
                    ti, p = tt - 1, (tt - 1) % 2
                    v.wait_ge(s_rstd, ti + 1)
                    if ti >= 2:
                        v.wait_ge(s_tr, 2 * (ti - 1))
                    v.tensor_scalar(xnb[p][:], xbuf[ti % 3][:], sm_mean[:, p:p + 1],
                                    sm_rstd[:, p:p + 1], ALU.subtract, ALU.mult).then_inc(s_xnb, 1)

        @blk.scalar
        def _(s):
            for tt in range(NTOT + 1):
                if tt < NTOT:
                    ti, p = tt, tt % 2
                    s.wait_ge(s_x[ti % 3], 16 * (ti // 3 + 1))
                    s.activation(xsq[:], xbuf[ti % 3][:], AF.Square,
                                 accum_out=sm_sumsq[:, p:p + 1]).then_inc(s_sq, 1)
                    s.wait_ge(s_var, ti + 1)
                    s.activation(sm_lnv[:, p:p + 1], sm_varp[:, p:p + 1], AF.Ln).then_inc(s_sch, 1)
                    s.wait_ge(s_sch, ti + 1)
                    s.activation(sm_rstd[:, p:p + 1], sm_lnv[:, p:p + 1], AF.Exp,
                                 scale=-0.5).then_inc(s_rstd, 1)
                if tt >= 1:
                    ti, p = tt - 1, (tt - 1) % 2
                    s.wait_ge(s_tr, 2 * ti + 1)
                    if ti < NT_ROW:
                        s.copy(xns[p][:, 0:1024], ptr[(2 * ti) % 4][:]).then_inc(s_xe, 1)
                        s.wait_ge(s_tr, 2 * ti + 2)
                        s.copy(xns[p][:, 1024:2048], ptr[(2 * ti + 1) % 4][:]).then_inc(s_xe, 1)
                        s.wait_ge(s_kv, ti + 1)
                        s.copy(kctile[p][:], pkv[p][:, 0:128]).then_inc(s_kce, 1)
                        s.copy(vc[:, ti * 128:(ti + 1) * 128], pkv[p][:, 128:256]).then_inc(s_vce, 1)
                        s.wait_ge(s_kt, ti + 1)
                        s.copy(kcT[:, ti * 128:(ti + 1) * 128], pkt[p][:, 0:128]).then_inc(s_kte, 1)
                    else:
                        o = ti - NT_ROW
                        xv = xnT[:].rearrange("p (j t) -> p j t", t=NOWN)
                        s.copy(xv[:, 0:8, o * 128:(o + 1) * 128],
                               ptr[(2 * ti) % 4][:].rearrange("p (j t) -> p j t", t=128)).then_inc(s_xe, 1)
                        s.wait_ge(s_tr, 2 * ti + 2)
                        s.copy(xv[:, 8:16, o * 128:(o + 1) * 128],
                               ptr[(2 * ti + 1) % 4][:].rearrange("p (j t) -> p j t", t=128)).then_inc(s_xe, 1)

        @blk.tensor
        def _(t):
            for tt in range(1, NTOT + 1):
                ti, p = tt - 1, (tt - 1) % 2
                t.wait_ge(s_xnb, ti + 1)
                if ti >= 2:
                    t.wait_ge(s_xe, 2 * (ti - 1))
                for half8 in range(2):
                    bank = ptr[(2 * ti + half8) % 4]
                    for j in range(8):
                        jj = half8 * 8 + j
                        ins = t.transpose(bank[:, j * 128:(j + 1) * 128],
                                          xnb[p][:, jj * 128:(jj + 1) * 128], s_idb[:])
                        if j == 7:
                            ins.then_inc(s_tr, 1)
                if ti < NT_ROW:
                    t.wait_ge(s_xe, 2 * (ti + 1))
                    if ti >= 2:
                        t.wait_ge(s_kce, ti - 1)
                        t.wait_ge(s_vce, ti - 1)
                    for j in range(ND):
                        ins = t.matmul(pkv[p][:, 0:KV], xns[p][:, j * 128:(j + 1) * 128],
                                       wkvb[:, j * KV:(j + 1) * KV],
                                       start=(j == 0), stop=(j == ND - 1))
                    ins.then_inc(s_kv, 1)
                    t.wait_ge(s_kce, ti + 1)
                    if ti >= 2:
                        t.wait_ge(s_kte, ti - 1)
                    t.transpose(pkt[p][:, 0:128], kctile[p][:], s_idb[:]).then_inc(s_kt, 1)

    sc1.close()

    if DEBUG >= 1:
        s_dbg = sem("s_dbg")
        with nc.Block() as blk:
            @blk.sync
            def _(sync):
                sync.dma_start(dbg["d_kcT"], kcT[:]).then_inc(s_dbg, 16)
                sync.dma_start(dbg["d_vc"], vc[:]).then_inc(s_dbg, 16)
                sync.dma_start(dbg["d_xnT"], xnT[:]).then_inc(s_dbg, 16)
                sync.wait_ge(s_dbg, 48)
    if UPTO <= 1:
        return nc

    NTK = NOWN // 512
    sc2a = Scope()
    gwb = [sc2a.sbuf(nc, f"gwb{i}", [128, ND * 128], BF16) for i in range(2)]
    uwb = [sc2a.sbuf(nc, f"uwb{i}", [128, ND * 128], BF16) for i in range(2)]
    hgb = [sc2a.sbuf(nc, f"hgb{i}", [128, 512], BF16) for i in range(2)]
    ubf = [sc2a.sbuf(nc, f"ubf{i}", [128, 512], BF16) for i in range(2)]
    hdnb = [sc2a.sbuf(nc, f"hdnb{i}", [128, 512], BF16) for i in range(2)]
    pg = [sc2a.psum(nc, f"pg{i}", [128, 512], F32) for i in range(2)]
    pu = [sc2a.psum(nc, f"pu{i}", [128, 512], F32) for i in range(2)]

    s_gw = [sem("s_gw0"), sem("s_gw1")]
    s_uw = [sem("s_uw0"), sem("s_uw1")]
    s_gmm = sem("s_gmm")
    s_umm = sem("s_umm")
    s_sil = sem("s_sil")
    s_hdn = sem("s_hdn")
    s_hsp = [sem("s_hsp0"), sem("s_hsp1")]
    s_vc2 = sem("s_vc2")



    with nc.Block() as blk:
        @blk.gpsimd
        def _(gp):
            for i in range(NI):
                if i >= 2:
                    gp.wait_ge(s_gmm, NTK * (i - 1))
                    gp.wait_ge(s_umm, NTK * (i - 1))
                gp.dma_start(gwb[i % 2][:], gateT[i]).then_inc(s_gw[i % 2], 16)
                gp.dma_start(uwb[i % 2][:], upT[i]).then_inc(s_uw[i % 2], 16)

        @blk.tensor
        def _(t):
            for i in range(NI):
                t.wait_ge(s_gw[i % 2], 16 * (i // 2 + 1))
                t.wait_ge(s_uw[i % 2], 16 * (i // 2 + 1))
                for tk in range(NTK):
                    seq = i * NTK + tk
                    if seq >= 2:
                        t.wait_ge(s_sil, seq - 1)
                    for j in range(ND):
                        ins = t.matmul(pg[seq % 2][:], gwb[i % 2][:, j * 128:(j + 1) * 128],
                                       xnT[:, j * NOWN + tk * 512:j * NOWN + (tk + 1) * 512],
                                       start=(j == 0), stop=(j == ND - 1))
                    ins.then_inc(s_gmm, 1)
                    if seq >= 2:
                        t.wait_ge(s_hdn, seq - 1)
                    for j in range(ND):
                        ins = t.matmul(pu[seq % 2][:], uwb[i % 2][:, j * 128:(j + 1) * 128],
                                       xnT[:, j * NOWN + tk * 512:j * NOWN + (tk + 1) * 512],
                                       start=(j == 0), stop=(j == ND - 1))
                    ins.then_inc(s_umm, 1)

        @blk.scalar
        def _(s):
            load_act_set(nc, s, SET_SILU)
            for i in range(NI):
                for tk in range(NTK):
                    seq = i * NTK + tk
                    s.wait_ge(s_gmm, seq + 1)
                    if seq >= 2:
                        s.wait_ge(s_hdn, seq - 1)
                    s.activation(hgb[seq % 2][:], pg[seq % 2][:], AF.Silu,
                                 scale=gsv[:, i:i + 1]).then_inc(s_sil, 1)

        @blk.vector
        def _(v):
            nv = 0
            for i in range(NI):
                for tk in range(NTK):
                    seq = i * NTK + tk
                    v.wait_ge(s_umm, seq + 1)
                    if seq >= 2:
                        v.wait_ge(s_hsp[seq % 2], 16 * (seq // 2))
                    v.tensor_scalar(ubf[seq % 2][:], pu[seq % 2][:], usv[:, i:i + 1],
                                    None, ALU.mult).then_inc(s_vc2, 1); nv += 1
                    v.wait_ge(s_sil, seq + 1)
                    v.wait_ge(s_vc2, nv)
                    v.tensor_mul(hdnb[seq % 2][:], hgb[seq % 2][:],
                                 ubf[seq % 2][:]).then_inc(s_hdn, 1)

        @blk.sync
        def _(sync):
            for i in range(NI):
                for tk in range(NTK):
                    seq = i * NTK + tk
                    sync.wait_ge(s_hdn, seq + 1)
                    sync.dma_start(hdn_spill[i * 128:(i + 1) * 128, tk * 512:(tk + 1) * 512],
                                   hdnb[seq % 2][:]).then_inc(s_hsp[seq % 2], 16)

    sc2a.close()
    if UPTO <= 2:
        return nc

    sc2b = Scope()
    hdnT = sc2b.sbuf(nc, "hdnT", [128, NI * 512], BF16)
    dstrip = [sc2b.sbuf(nc, f"dstrip{i}", [128, NI * 128], BF16) for i in range(4)]
    ysb = [sc2b.sbuf(nc, f"ysb{i}", [128, 512], F32) for i in range(2)]
    xtb = [sc2b.sbuf(nc, f"xtb{i}", [128, 512], F32) for i in range(2)]
    py = [sc2b.psum(nc, f"py{i}", [128, 512], F32) for i in range(2)]

    s_hl = sem("s_hl")
    s_dw = [sem(f"s_dw{i}") for i in range(4)]
    s_dmm = sem("s_dmm")
    s_yv = sem("s_yv")
    s_yst = [sem("s_yst0"), sem("s_yst1")]
    s_xl = [sem("s_xl0"), sem("s_xl1")]
    s_vb2 = sem("s_vb2")

    spill3 = hdn_spill.rearrange("(j l) k -> l j k", l=128)

    with nc.Block() as blk:
        @blk.sync
        def _(sync):
            nsp = NI * NTK
            sync.wait_ge(s_hsp[0], 16 * ((nsp + 1) // 2))
            sync.wait_ge(s_hsp[1], 16 * (nsp // 2))
            for tc in range(NTK):
                if tc >= 1:
                    sync.wait_ge(s_dmm, ND * tc)
                for part in range(4):
                    j0 = (NI * part) // 4
                    j1 = (NI * (part + 1)) // 4
                    if j1 > j0:
                        sync.dma_start(
                            hdnT[:].rearrange("p (j k) -> p j k", k=512)[:, j0:j1, :],
                            spill3[:, j0:j1, tc * 512:(tc + 1) * 512]).then_inc(s_hl, 16)
                for hb in range(ND):
                    seq = tc * ND + hb
                    if seq >= 2:
                        sync.wait_ge(s_yv, seq - 1)
                    sync.dma_start(xtb[seq % 2][:],
                                   xT_own[hb * 128:(hb + 1) * 128, tc * 512:(tc + 1) * 512]
                                   ).then_inc(s_xl[seq % 2], 16)
                    sync.wait_ge(s_yv, seq + 1)
                    sync.dma_start(out[hb * 128:(hb + 1) * 128, tc * 512:(tc + 1) * 512],
                                   ysb[seq % 2][:]).then_inc(s_yst[seq % 2], 16)

        @blk.gpsimd
        def _(gp):
            for tc in range(NTK):
                for hb in range(ND):
                    seq = tc * ND + hb
                    if seq >= 4:
                        gp.wait_ge(s_dmm, seq - 3)
                    gp.dma_start(dstrip[seq % 4][:], downT[hb]).then_inc(s_dw[seq % 4], 16)
            gp.wait_ge(s_yst[0], 16 * ((NTK * ND + 1) // 2))
            gp.wait_ge(s_yst[1], 16 * ((NTK * ND) // 2))

        @blk.tensor
        def _(t):
            for tc in range(NTK):
                t.wait_ge(s_hl, 16 * 4 * (tc + 1))
                for hb in range(ND):
                    seq = tc * ND + hb
                    t.wait_ge(s_dw[seq % 4], 16 * (seq // 4 + 1))
                    if seq >= 2:
                        t.wait_ge(s_yv, seq - 1)
                    for jb in range(NI):
                        ins = t.matmul(py[seq % 2][:], dstrip[seq % 4][:, jb * 128:(jb + 1) * 128],
                                       hdnT[:, jb * 512:(jb + 1) * 512],
                                       start=(jb == 0), stop=(jb == NI - 1))
                    ins.then_inc(s_dmm, 1)

        @blk.vector
        def _(v):
            for tc in range(NTK):
                for hb in range(ND):
                    seq = tc * ND + hb
                    v.wait_ge(s_dmm, seq + 1)
                    if seq >= 2:
                        v.wait_ge(s_yst[seq % 2], 16 * (seq // 2))
                    v.tensor_scalar(ysb[seq % 2][:], py[seq % 2][:], dsv[:, hb:hb + 1],
                                    None, ALU.mult).then_inc(s_vb2, 1)
                    v.wait_ge(s_vb2, seq + 1)
                    v.wait_ge(s_xl[seq % 2], 16 * (seq // 2 + 1))
                    v.tensor_add(ysb[seq % 2][:], ysb[seq % 2][:],
                                 xtb[seq % 2][:]).then_inc(s_yv, 1)

    sc2b.close()
    if UPTO <= 3:
        return nc

    scheds = (sched_for(0), sched_for(1))
    NSLOT = len(HALF_CHUNKS[0])
    PQ = max(L for sch in scheds for L, _ in sch)
    NQC = NOWN // 512

    scQ = Scope()
    QT = scQ.sbuf(nc, "QT", [128, H * NOWN], BF16)

    sc3 = Scope()
    wqb = [sc3.sbuf(nc, f"wqb{i}", [128, ND * 128], BF16) for i in range(2)]
    qta = sc3.sbuf(nc, "qta", [32, 512], F32)
    qtb = sc3.sbuf(nc, "qtb", [32, 512], F32)
    qtc = sc3.sbuf(nc, "qtc", [32, 512], F32)
    qtd = sc3.sbuf(nc, "qtd", [32, 512], F32)
    pq3 = [sc3.psum(nc, f"pq3{i}", [128, 512], F32) for i in range(2)]

    s_wq = [sem("s_wq0"), sem("s_wq1")]
    s_qmm = sem("s_qmm")
    s_qe = sem("s_qe")
    s_qrope = sem("s_qrope")
    s_v3 = sem("s_v3")

    with nc.Block() as blk:
        @blk.gpsimd
        def _(gp):
            for h in range(H):
                if h >= 2:
                    gp.wait_ge(s_qmm, NQC * (h - 1))
                gp.dma_start(wqb[h % 2][:], wqT[h]).then_inc(s_wq[h % 2], 16)

        @blk.tensor
        def _(t):
            for h in range(H):
                t.wait_ge(s_wq[h % 2], 16 * (h // 2 + 1))
                for nk in range(NQC):
                    seq = h * NQC + nk
                    if seq >= 2:
                        t.wait_ge(s_qe, seq - 1)
                        t.wait_ge(s_qrope, seq - 1)
                    for j in range(ND):
                        ins = t.matmul(pq3[seq % 2][:], wqb[h % 2][:, j * 128:(j + 1) * 128],
                                       xnT[:, j * NOWN + nk * 512:j * NOWN + (nk + 1) * 512],
                                       start=(j == 0), stop=(j == ND - 1))
                    ins.then_inc(s_qmm, 1)

        @blk.scalar
        def _(sx):
            for h in range(H):
                for nk in range(NQC):
                    seq = h * NQC + nk
                    sx.wait_ge(s_qmm, seq + 1)
                    sx.activation(QT[64:128, h * NOWN + nk * 512:h * NOWN + (nk + 1) * 512],
                                  pq3[seq % 2][64:128, :], AF.Copy,
                                  scale=SCL).then_inc(s_qe, 1)

        @blk.vector
        def _(v):
            nch = 0
            for h in range(H):
                for nk in range(NQC):
                    seq = h * NQC + nk
                    v.wait_ge(s_qmm, seq + 1)
                    v.wait_ge(s_v3, nch)
                    v.wait_ge(s_qrope, seq)
                    pa = pq3[seq % 2]
                    ca, sa = pa[0:32, :], pa[32:64, :]
                    ck = s_cosq[:, nk * 512:(nk + 1) * 512]
                    rq = s_srotq[:, nk * 512:(nk + 1) * 512]
                    qcol = h * NOWN + nk * 512
                    v.tensor_mul(qta[:], ca, ck[0:32, :]).then_inc(s_v3, 1); nch += 1
                    v.tensor_mul(qtb[:], sa, rq[32:64, :]).then_inc(s_v3, 1); nch += 1
                    v.tensor_mul(qtc[:], sa, ck[32:64, :]).then_inc(s_v3, 1); nch += 1
                    v.tensor_mul(qtd[:], ca, rq[0:32, :]).then_inc(s_v3, 1); nch += 1
                    v.wait_ge(s_v3, nch)
                    v.tensor_sub(QT[0:32, qcol:qcol + 512], qta[:], qtb[:]).then_inc(s_v3, 1)
                    nch += 1
                    v.tensor_add(QT[32:64, qcol:qcol + 512], qtc[:], qtd[:]).then_inc(s_qrope, 1)

    sc3.close()
    scX.close()

    sc4 = Scope()
    kbuf = [sc4.sbuf(nc, f"kbuf{i}", [128, S], BF16) for i in range(2)]
    vbuf = [sc4.sbuf(nc, f"vbuf{i}", [128, S], BF16) for i in range(2)]
    ktmp = sc4.sbuf(nc, "ktmp", [64, S], BF16)
    pts = [sc4.sbuf(nc, f"pts{i}", [128, 512], BF16) for i in range(3)]
    atst = [sc4.sbuf(nc, f"atst{i}", [128, 512], BF16) for i in range(2)]
    vrow = sc4.sbuf(nc, "vrow", [1, 128], F32)
    wvb = sc4.sbuf(nc, "wvb", [128, 128], BF16)
    mxc = sc4.sbuf(nc, "mxc", [128, 8], F32)
    nmq = sc4.sbuf(nc, "nmq", [128, 8], F32)
    nmrow = sc4.sbuf(nc, "nmrow", [1, 512], BF16)
    srow = sc4.sbuf(nc, "srow", [1, 512], F32)
    rrow32 = sc4.sbuf(nc, "rrow32", [1, 512], F32)
    rrow = sc4.sbuf(nc, "rrow", [1, 512], BF16)
    rbsb = sc4.sbuf(nc, "rbsb", [128, 512], BF16)

    sp1 = [sc4.psum(nc, f"sp1{i}", [128, 512], F32) for i in range(2)]
    sp2 = [sc4.psum(nc, f"sp2{i}", [128, 512], F32) for i in range(2)]
    pvp = sc4.psum(nc, "pvp", [128, 512], F32)
    pb7 = sc4.psum(nc, "pb7", [128, 512], F32)
    prow = sc4.psum(nc, "prow", [128, 512], F32)
    rbank = sc4.psum(nc, "rbank", [128, 512], F32)

    s_m1 = sem("s_m1")
    s_mx = sem("s_mx")
    s_cmb = sem("s_cmb")
    s_prt = sem("s_prt")
    s_nr = sem("s_nr")
    s_m2 = sem("s_m2")
    s_msk = sem("s_msk")
    s_ex = sem("s_ex")
    s_z = sem("s_z")
    s_pv = sem("s_pv")
    s_sg = sem("s_sg")
    s_sr = sem("s_sr")
    s_rr = sem("s_rr")
    s_rb = sem("s_rb")
    s_rbe = sem("s_rbe")
    s_ae = sem("s_ae")
    s_asp = [sem("s_asp0"), sem("s_asp1")]
    s_wvt = sem("s_wvt")
    s_wve = sem("s_wve")
    s_wvm = sem("s_wvm")
    s_wvb = sem("s_wvb")
    s_kvb = sem("s_kvb")
    s_kb2 = sem("s_kb2")
    s_v4 = sem("s_v4")

    syncq = nc.sync
    for idx in (range(P4_NH * NSLOT) if P4_STAGE >= 4 else ()):
        h, k = idx // NSLOT, idx % NSLOT
        syncq.wait_ge(s_ae, idx + 1)
        syncq.dma_start(at_spill[h * 128:(h + 1) * 128, k * 512:(k + 1) * 512],
                        atst[idx % 2][:]).then_inc(s_asp[idx % 2], 16)

    def emit_p4(sched, cnt):
        t, v, s = nc.tensor, nc.vector, nc.scalar

        def vch(ins):
            ins.then_inc(s_v4, 1)
            cnt["v4"] += 1
            return ins

        def vwait():
            v.wait_ge(s_v4, cnt["v4"])

        a_guard = [list(), list()]
        b_guard = [list(), list()]
        load_act_set(nc, s, SET_LNEXP)
        if DEBUG >= 3:
            v.memset(mxc[:], 0.0)
            vch(v.memset(nmq[:], 0.0))

        def emit_kv(g):
            gp_ = g % 2
            if g >= 1:
                t.wait_ge(s_wve, g)
                t.wait_ge(s_wvb, g)
            t.transpose(pb7[0:1, 0:128], wvs[:, g:g + 1], s_idf[:]).then_inc(s_wvt, 1)
            s.wait_ge(s_wvt, g + 1)
            s.copy(vrow[:], pb7[0:1, 0:128]).then_inc(s_wve, 1)
            t.wait_ge(s_wve, g + 1)
            t.matmul(pb7[:, 128:256], s_ones[0:1, 0:128], vrow[:],
                     start=True, stop=True).then_inc(s_wvm, 1)
            s.wait_ge(s_wvm, g + 1)
            if g >= 1:
                s.wait_ge(s_kvb, g)
            s.copy(wvb[:], pb7[:, 128:256]).then_inc(s_wvb, 1)
            if g >= 2:
                v.wait_ge(s_m1, cnt["m1_head"].get(g - 2, 0))
                v.wait_ge(s_m2, cnt["m2_head"].get(g - 2, 0))
            vch(v.tensor_scalar(kbuf[gp_][:], kcT[:], wks[:, g:g + 1], None, ALU.mult))
            vwait()
            vch(v.tensor_mul(ktmp[0:32, :], kbuf[gp_][32:64, :], s_srotk[32:64, :]))
            vch(v.tensor_mul(ktmp[32:64, :], kbuf[gp_][0:32, :], s_srotk[0:32, :]))
            vwait()
            vch(v.tensor_mul(kbuf[gp_][0:32, :], kbuf[gp_][0:32, :], s_cosk[0:32, :]))
            vch(v.tensor_mul(kbuf[gp_][32:64, :], kbuf[gp_][32:64, :], s_cosk[32:64, :]))
            vwait()
            vch(v.tensor_sub(kbuf[gp_][0:32, :], kbuf[gp_][0:32, :], ktmp[0:32, :]))
            v.tensor_add(kbuf[gp_][32:64, :], kbuf[gp_][32:64, :],
                         ktmp[32:64, :]).then_inc(s_kb2, 1)

        def emit_v(g):
            gp_ = g % 2
            if g >= 2:
                v.wait_ge(s_pv, cnt["pv_head"].get(g - 2, 0))
            v.wait_ge(s_wvb, g + 1)
            v.wait_ge(s_kb2, g + 1)
            wvb_b = wvb[:].rearrange("p (j k) -> p j k", j=1).broadcast_to((128, NT_ROW, 128))
            v.tensor_tensor(vbuf[gp_][:].rearrange("p (j k) -> p j k", k=128),
                            vc[:].rearrange("p (j k) -> p j k", k=128),
                            wvb_b, ALU.mult).then_inc(s_kvb, 1)

        def emit_slot(h, k):
            hp = h % 2
            L, tqs = sched[k]
            ntb = L // 128
            sp = cnt["sl"] % 2
            cnt["sl"] += 1
            qs0 = h * NOWN + k * 512
            for q in range(4):
                tq = tqs[q]
                nch = (tq + 511) // 512
                v.wait_ge(s_cmb, cnt["cmb"])
                for c in range(nch):
                    w = min(512, tq - c * 512)
                    ar = cnt["m1"] % 2
                    for smv in a_guard[ar]:
                        t.wait_ge(*smv)
                    a_guard[ar] = []
                    t.matmul(sp1[ar][:, 0:w], QT[:, qs0 + q * 128:qs0 + (q + 1) * 128],
                             kbuf[hp][:, c * 512:c * 512 + w],
                             start=True, stop=True).then_inc(s_m1, 1)
                    cnt["m1"] += 1
                    v.wait_ge(s_m1, cnt["m1"])
                    v.reduce_max(mxc[:, c:c + 1], sp1[ar][:, 0:w],
                                 axis=AX.X).then_inc(s_mx, 1)
                    cnt["mx"] += 1
                    a_guard[ar] = [(s_mx, cnt["mx"])]
                v.wait_ge(s_prt, cnt["prt_par"][sp])
                v.wait_ge(s_mx, cnt["mx"])
                v.reduce_max(nmq[:, sp * 4 + q:sp * 4 + q + 1], mxc[:, 0:nch],
                             axis=AX.X, negate=True).then_inc(s_cmb, 1)
                cnt["cmb"] += 1
                t.wait_ge(s_cmb, cnt["cmb"])
                if q == 0:
                    t.wait_ge(s_nr, cnt["nr"])
                t.transpose(prow[0:1, q * 128:(q + 1) * 128],
                            nmq[:, sp * 4 + q:sp * 4 + q + 1],
                            s_idf[:]).then_inc(s_prt, 1)
                cnt["prt"] += 1
                yield
            cnt["prt_par"][sp] = cnt["prt"]
            s.wait_ge(s_prt, cnt["prt"])
            s.copy(nmrow[:], prow[0:1, 0:512]).then_inc(s_nr, 1)
            cnt["nr"] += 1
            t.wait_ge(s_nr, cnt["nr"])
            pend = None
            for tb in range(ntb):
                g2 = cnt["m2"] % 2
                for smv in b_guard[g2]:
                    t.wait_ge(*smv)
                b_guard[g2] = []
                t.matmul(sp2[g2][:], kbuf[hp][:, tb * 128:(tb + 1) * 128],
                         QT[:, qs0:qs0 + 512], start=True, stop=False)
                t.matmul(sp2[g2][:], s_onesb[0:1, 0:128], nmrow[:],
                         start=False, stop=True).then_inc(s_m2, 1)
                cnt["m2"] += 1
                dq = tb - (ntb - 4)
                if dq >= 0 and dq < 4:
                    v.wait_ge(s_m2, cnt["m2"])
                    v.tensor_tensor(sp2[g2][:, dq * 128:(dq + 1) * 128],
                                    sp2[g2][:, dq * 128:(dq + 1) * 128],
                                    s_maskt[:], ALU.add).then_inc(s_msk, 1)
                    cnt["msk"] += 1
                g = cnt["gtb"]
                cnt["gtb"] += 1
                ptsb = pts[g % 3]
                s.wait_ge(s_m2, cnt["m2"])
                if dq >= 0 and dq < 4:
                    s.wait_ge(s_msk, cnt["msk"])
                if g >= 3:
                    s.wait_ge(s_pv, cnt["pv_hist"][g - 3])
                s.activation(ptsb[:], sp2[g2][:], AF.Exp).then_inc(s_ex, 1)
                cnt["ex"] += 1
                b_guard[g2] = [(s_ex, cnt["ex"])]
                zq = tb - (ntb - 3)
                if zq >= 0:
                    v.wait_ge(s_ex, cnt["ex"])
                    nz = min(zq, 2) + 1
                    for q in range(nz):
                        ins = v.memset(ptsb[:, q * 128:(q + 1) * 128], 0.0)
                        if q < nz - 1:
                            ins.then_inc(s_v4, 1)
                            cnt["v4"] += 1
                    ins.then_inc(s_z, 1)
                    cnt["z"] += 1
                if pend is not None:
                    emit_sigpv(h, k, ntb, *pend, last=False)
                pend = (tb, cnt["ex"], cnt["z"] if zq >= 0 else None, g)
                if tb % 8 == 7:
                    yield
            emit_sigpv(h, k, ntb, *pend, last=True)
            s.wait_ge(s_sg, cnt["sg"])
            s.copy(srow[:], rbank[0:1, 0:512]).then_inc(s_sr, 1)
            cnt["sr"] += 1
            v.wait_ge(s_sr, cnt["sr"])
            v.reciprocal(rrow32[:], srow[:]).then_inc(s_rr, 1)
            cnt["rr"] += 1
            s.wait_ge(s_rr, cnt["rr"])
            s.copy(rrow[:], rrow32[:]).then_inc(s_rr, 1)
            cnt["rr"] += 1
            t.wait_ge(s_rr, cnt["rr"])
            t.matmul(rbank[:, 0:512], s_onesb[0:1, 0:128], rrow[:],
                     start=True, stop=True).then_inc(s_rb, 1)
            cnt["rb"] += 1
            s.wait_ge(s_rb, cnt["rb"])
            s.copy(rbsb[:], rbank[:, 0:512]).then_inc(s_rbe, 1)
            cnt["rbe"] += 1
            ai = cnt["ae"]
            v.wait_ge(s_rbe, cnt["rbe"])
            v.wait_ge(s_pv, cnt["pv"])
            if ai >= 2:
                v.wait_ge(s_asp[ai % 2], 16 * (ai // 2))
            v.tensor_tensor(atst[ai % 2][:], pvp[:], rbsb[:], ALU.mult).then_inc(s_ae, 1)
            cnt["ae"] += 1

        def emit_sigpv(h, k, ntb, tb, ex_cnt, z_cnt, g, last):
            hp = h % 2
            ptsb = pts[g % 3]
            t.wait_ge(s_ex, ex_cnt)
            if z_cnt is not None:
                t.wait_ge(s_z, z_cnt)
            if tb == 0:
                t.wait_ge(s_rbe, cnt["rbe"])
                t.wait_ge(s_ae, cnt["ae"])
            ins = t.matmul(rbank[0:1, 0:512], s_onesb[:, 0:1], ptsb[:],
                           start=(tb == 0), stop=(tb == ntb - 1))
            if tb == ntb - 1:
                ins.then_inc(s_sg, 1)
                cnt["sg"] += 1
            t.matmul(pvp[:], vbuf[hp][:, tb * 128:(tb + 1) * 128], ptsb[:],
                     start=(tb == 0), stop=(tb == ntb - 1)).then_inc(s_pv, 1)
            cnt["pv"] += 1
            cnt["pv_hist"][g] = cnt["pv"]

        emit_kv(0)
        for h in range(P4_NH):
            emit_v(h)
            if P4_STAGE < 3:
                if h + 1 < P4_NH:
                    emit_kv(h + 1)
                continue
            t.wait_ge(s_kvb, h + 1)
            t.wait_ge(s_kb2, h + 1)
            for k in range(NSLOT):
                sg = emit_slot(h, k)
                first = True
                for _ in sg:
                    if first and k == 0 and h + 1 < P4_NH:
                        emit_kv(h + 1)
                        first = False
            cnt["m1_head"][h] = cnt["m1"]
            cnt["m2_head"][h] = cnt["m2"]
            cnt["pv_head"][h] = cnt["pv"]

    def fresh_cnt():
        return dict(m1=0, mx=0, cmb=0, prt=0, nr=0, m2=0, msk=0, ex=0, z=0,
                    pv=0, sg=0, sr=0, rr=0, rb=0, rbe=0, ae=0, v4=0, sl=0, gtb=0,
                    prt_par=[0, 0], pv_hist={}, m1_head={}, m2_head={}, pv_head={})

    engs = [nc.tensor, nc.vector, nc.scalar]
    idxs = [e.partition_id() % 2 for e in engs]
    for ib in nc.Switch(engines=engs, index=idxs, n=2):
        emit_p4(scheds[ib], fresh_cnt())

    if DEBUG >= 3:
        dbg3 = {}
        for nm, tns in (("d_wks", wks), ("d_wvs", wvs), ("d_wvb", wvb),
                        ("d_vbuf", vbuf[(P4_NH - 1) % 2]), ("d_kbuf", kbuf[(P4_NH - 1) % 2]),
                        ("d_qh", QT), ("d_pbuf", pts[0]),
                        ("d_pbuf1", pts[1]), ("d_pbuf2", pts[2]),
                        ("d_nmrow", nmrow), ("d_srow", srow),
                        ("d_rrow32", rrow32), ("d_rbsb", rbsb),
                        ("d_nmx", nmq), ("d_mxc", mxc)):
            shp = list(tns[:].shape)
            dt = tns[:].dtype
            dbg3[nm] = (nc.dram_tensor(nm, shp, dt, kind="ExternalOutput").ap(), tns)
        s_d3 = sem("s_d3")
        sy = nc.sync
        sy.wait_ge(s_ae, P4_NH * NSLOT)
        nd3 = 0
        for nm, (dap, tns) in dbg3.items():
            sy.dma_start(dap, tns[:]).then_inc(s_d3, 16)
            nd3 += 16
        sy.wait_ge(s_d3, nd3)

    if UPTO <= 4:
        sc4.close()
        scQ.close()
        return nc

    with nc.Block():
        pass
    sc4.close()
    scQ.close()
    sc5 = Scope()
    atsb = sc5.sbuf(nc, "atsb", [128, ND * NOWN], BF16)
    wob = [sc5.sbuf(nc, f"wob{i}", [128, ND * 128], BF16) for i in range(2)]
    oysb = [sc5.sbuf(nc, f"oysb{i}", [128, 512], F32) for i in range(2)]
    pyo = [sc5.psum(nc, f"pyo{i}", [128, 512], F32) for i in range(2)]

    s_al = sem("s_al")
    s_wo = [sem("s_wo0"), sem("s_wo1")]
    s_omm = sem("s_omm")
    s_oye = sem("s_oye")
    s_yo = [sem("s_yo0"), sem("s_yo1")]

    at3 = at_spill.rearrange("(j l) k -> l j k", l=128)

    with nc.Block() as blk:
        @blk.sync
        def _(sync):
            sync.wait_ge(s_asp[0], 16 * ((H * NSLOT + 1) // 2))
            sync.wait_ge(s_asp[1], 16 * ((H * NSLOT) // 2))
            for part in range(4):
                j0, j1 = (ND * part) // 4, (ND * (part + 1)) // 4
                sync.dma_start(atsb[:].rearrange("p (j k) -> p j k", k=NOWN)[:, j0:j1, :],
                               at3[:, j0:j1, :]).then_inc(s_al, 16)

        @blk.gpsimd
        def _(gp):
            NQC5 = NOWN // 512
            nacc = 0

            def emit_acc(gp, seq):
                gp.wait_ge(s_oye, seq + 1)
                ob_, tc_ = seq // NQC5, seq % NQC5
                gp.dma_start(out[ob_ * 128:(ob_ + 1) * 128, tc_ * 512:(tc_ + 1) * 512],
                             oysb[seq % 2][:], accum_op=ALU.add).then_inc(s_yo[seq % 2], 16)

            for ob in range(ND):
                if ob >= 2:
                    while nacc < NQC5 * (ob - 1) - 1:
                        emit_acc(gp, nacc)
                        nacc += 1
                    gp.wait_ge(s_omm, NQC5 * (ob - 1))
                gp.dma_start(wob[ob % 2][:], woT[ob]).then_inc(s_wo[ob % 2], 16)
            while nacc < ND * NQC5:
                emit_acc(gp, nacc)
                nacc += 1
            gp.wait_ge(s_yo[0], 16 * ((ND * NQC5 + 1) // 2))
            gp.wait_ge(s_yo[1], 16 * ((ND * NQC5) // 2))

        @blk.tensor
        def _(t):
            t.wait_ge(s_al, 16 * 4)
            for ob in range(ND):
                t.wait_ge(s_wo[ob % 2], 16 * (ob // 2 + 1))
                for tc in range(NOWN // 512):
                    seq = ob * (NOWN // 512) + tc
                    if seq >= 2:
                        t.wait_ge(s_oye, seq - 1)
                    for j in range(ND):
                        ins = t.matmul(pyo[seq % 2][:], wob[ob % 2][:, j * 128:(j + 1) * 128],
                                       atsb[:, j * NOWN + tc * 512:j * NOWN + (tc + 1) * 512],
                                       start=(j == 0), stop=(j == ND - 1))
                    ins.then_inc(s_omm, 1)

        @blk.scalar
        def _(s):
            for seq in range(ND * (NOWN // 512)):
                s.wait_ge(s_omm, seq + 1)
                if seq >= 2:
                    s.wait_ge(s_yo[seq % 2], 16 * (seq // 2))
                s.copy(oysb[seq % 2][:], pyo[seq % 2][:]).then_inc(s_oye, 1)

    sc5.close()
    return nc



def kernel(L0_attn_x, L0_attn_W_q, L0_attn_W_kv, L0_attn_W_k_heads,
           L0_attn_W_v_heads, L0_attn_W_o, L0_ffn_gate_w, L0_ffn_up_w,
           L0_ffn_down_w, L0_ffn_gate_s, L0_ffn_up_s, L0_ffn_down_s):
    x = np.ascontiguousarray(L0_attn_x, dtype=np.float32)
    tabs = host_tables()
    def tile_blocks(wT, nblk):
        din, dout = wT.shape
        nj = din // 128
        out = np.empty((nblk, 128, din), dtype=wT.dtype)
        w3 = wT.reshape(nj, 128, dout)
        for b in range(nblk):
            out[b] = w3[:, :, b * 128:(b + 1) * 128].transpose(1, 0, 2).reshape(128, din)
        return out

    shared = dict(
        wqT=tile_blocks(np.ascontiguousarray(np.asarray(L0_attn_W_q).T), H),
        wkvT=np.ascontiguousarray(np.asarray(L0_attn_W_kv).T),
        wkh=np.ascontiguousarray(L0_attn_W_k_heads),
        wvh=np.ascontiguousarray(L0_attn_W_v_heads),
        woT=tile_blocks(np.ascontiguousarray(np.asarray(L0_attn_W_o).T), ND),
        gateT=tile_blocks(np.ascontiguousarray(np.asarray(L0_ffn_gate_w).T), NI),
        upT=tile_blocks(np.ascontiguousarray(np.asarray(L0_ffn_up_w).T), NI),
        downT=tile_blocks(np.ascontiguousarray(np.asarray(L0_ffn_down_w).T), ND),
        gates=np.ascontiguousarray(np.asarray(L0_ffn_gate_s).reshape(NI, 128)),
        ups=np.ascontiguousarray(np.asarray(L0_ffn_up_s).reshape(NI, 128)),
        downs=np.ascontiguousarray(np.asarray(L0_ffn_down_s).reshape(ND, 128)),
        cosk=tabs["cosk"], srotk=tabs["srotk"],
        identbf=tabs["identbf"], identf=tabs["identf"], maskf=tabs["maskf"],
        masktf=tabs["masktf"],
        onesf=tabs["onesf"],
    )
    in_maps = []
    for c in range(8):
        b, half = c // 2, c % 2
        rows = own_rows(half)
        m = dict(shared)
        m["x_row"] = np.ascontiguousarray(x[b])
        m["x_own"] = np.ascontiguousarray(x[b][rows])
        m["xT_own"] = np.ascontiguousarray(x[b][rows].T)
        m["cosq"] = np.ascontiguousarray(tabs["cosq_full"][:, rows])
        m["srotq"] = np.ascontiguousarray(tabs["srotq_full"][:, rows])
        in_maps.append(m)

    nc = build_nc()
    res = run_bass_kernel_spmd(nc, in_maps, core_ids=list(range(8)))

    full = np.empty((B, S, D), dtype=np.float32)
    for c in range(8):
        b, half = c // 2, c % 2
        full[b, own_rows(half)] = np.asarray(res.results[c]["out"]).T
    return full


# version 15
# speedup vs baseline: 1.1657x; 1.1657x over previous
import math

import numpy as np
import ml_dtypes
import concourse.bass as bass
import concourse.mybir as mybir
from concourse.bass_utils import run_bass_kernel_spmd
from concourse.hw_specs import get_activation_tables

F32 = mybir.dt.float32
BF16 = mybir.dt.bfloat16
I32 = mybir.dt.int32
AF = mybir.ActivationFunctionType
ALU = mybir.AluOpType
AX = mybir.AxisListType

B, S, D, H, DH, KV, INTER = 4, 4096, 2048, 16, 128, 256, 5632
NT_ROW = S // 128
NT_OWN = 16
ND = D // 128
NI = INTER // 128
HALF_CHUNKS = ([0, 1, 6, 7], [2, 3, 4, 5])
NOWN = 2048
SCL = 1.0 / math.sqrt(DH)
EPS = 1e-5
NEG = -1e30

DEBUG = 0
P4_NH = 16
P4_STAGE = 4
UPTO = 5


def act_set_idx(*need):
    import concourse.bass as _b
    nc = _b.Bass("TRN2", target_bir_lowering=False)
    for i, funcs in enumerate(get_activation_tables(nc.m.arch).values()):
        names = {f.name for f in funcs}
        if set(need) <= names:
            return i
    raise ValueError(need)


SET_LNEXP = 6
SET_SILU = 18


def load_act_set(nc, s, idx):
    s.add_instruction(mybir.InstLoadActFuncSet(
        name=nc.get_next_instruction_name(), act_func_set_id=idx, ins=[], outs=[]))


def host_tables():
    rope_half = DH // 4
    freqs = 1.0 / (10000.0 ** (np.arange(rope_half, dtype=np.float32) / np.float32(rope_half)))
    ang = np.outer(np.arange(S, dtype=np.float32), freqs)
    cos = np.repeat(np.cos(ang), 2, axis=-1)
    sin = np.repeat(np.sin(ang), 2, axis=-1)
    bf = ml_dtypes.bfloat16
    return {
        "cosk": np.ascontiguousarray(cos.T).astype(bf),
        "sink": np.ascontiguousarray(sin.T).astype(bf),
        "srotk": np.ascontiguousarray(np.concatenate([sin[:, 32:], sin[:, :32]], 1).T).astype(bf),
        "cosq_full": np.ascontiguousarray((cos * SCL).T).astype(bf),
        "sinq_full": np.ascontiguousarray((sin * SCL).T).astype(bf),
        "srotq_full": np.ascontiguousarray(
            np.concatenate([sin[:, 32:] * SCL, sin[:, :32] * SCL], 1).T).astype(bf),
        "identbf": np.eye(128, dtype=bf),
        "identf": np.eye(128, dtype=np.float32),
        "maskf": np.triu(np.full((128, 128), NEG, np.float32), 1),
        "masktf": np.tril(np.full((128, 128), NEG, np.float32), -1),
        "onesf": np.ones((1, 128), dtype=np.float32),
    }


def own_rows(half):
    return np.concatenate([np.arange(c * 512, (c + 1) * 512) for c in HALF_CHUNKS[half]])


def sched_for(half):
    out = []
    for c in HALF_CHUNKS[half]:
        L = (c + 1) * 512
        out.append((L, [L - 384 + 128 * q for q in range(4)]))
    return out


class Scope:

    def __init__(self):
        self.cms = []

    def sbuf(self, nc, name, shape, dtype, side=None):
        cm = nc.sbuf_tensor(name, shape, dtype, side=side)
        self.cms.append(cm)
        return cm.__enter__()

    def psum(self, nc, name, shape, dtype):
        cm = nc.psum_tensor(name, shape, dtype)
        self.cms.append(cm)
        return cm.__enter__()

    def close(self):
        for cm in reversed(self.cms):
            cm.__exit__(None, None, None)
        self.cms = []


def build_nc():
    nc = bass.Bass("TRN2", target_bir_lowering=False)

    x_row = nc.dram_tensor("x_row", [S, D], F32, kind="ExternalInput").ap()
    x_own = nc.dram_tensor("x_own", [NOWN, D], F32, kind="ExternalInput").ap()
    xT_own = nc.dram_tensor("xT_own", [D, NOWN], F32, kind="ExternalInput").ap()
    wqT = nc.dram_tensor("wqT", [H, 128, D], F32, kind="ExternalInput").ap()
    wkvT = nc.dram_tensor("wkvT", [D, KV], F32, kind="ExternalInput").ap()
    wkh = nc.dram_tensor("wkh", [H, 128, 128], F32, kind="ExternalInput").ap()
    wvh = nc.dram_tensor("wvh", [H, 128, 128], F32, kind="ExternalInput").ap()
    woT = nc.dram_tensor("woT", [ND, 128, D], F32, kind="ExternalInput").ap()
    gateT = nc.dram_tensor("gateT", [NI, 128, D], F32, kind="ExternalInput").ap()
    upT = nc.dram_tensor("upT", [NI, 128, D], F32, kind="ExternalInput").ap()
    downT = nc.dram_tensor("downT", [ND, 128, INTER], F32, kind="ExternalInput").ap()
    gates = nc.dram_tensor("gates", [NI, 128], F32, kind="ExternalInput").ap()
    ups = nc.dram_tensor("ups", [NI, 128], F32, kind="ExternalInput").ap()
    downs = nc.dram_tensor("downs", [ND, 128], F32, kind="ExternalInput").ap()
    cosk = nc.dram_tensor("cosk", [64, S], BF16, kind="ExternalInput").ap()
    srotk = nc.dram_tensor("srotk", [64, S], BF16, kind="ExternalInput").ap()
    cosq = nc.dram_tensor("cosq", [64, NOWN], BF16, kind="ExternalInput").ap()
    srotq = nc.dram_tensor("srotq", [64, NOWN], BF16, kind="ExternalInput").ap()
    identbf = nc.dram_tensor("identbf", [128, 128], BF16, kind="ExternalInput").ap()
    identf = nc.dram_tensor("identf", [128, 128], F32, kind="ExternalInput").ap()
    maskf = nc.dram_tensor("maskf", [128, 128], F32, kind="ExternalInput").ap()
    masktf = nc.dram_tensor("masktf", [128, 128], F32, kind="ExternalInput").ap()
    onesf = nc.dram_tensor("onesf", [1, 128], F32, kind="ExternalInput").ap()

    out = nc.dram_tensor("out", [D, NOWN], F32, kind="ExternalOutput").ap()

    spillkind = "ExternalOutput" if DEBUG >= 2 else "Internal"
    hdn_spill = nc.dram_tensor("hdn_spill", [INTER, NOWN], BF16, kind=spillkind).ap()
    at_spill = nc.dram_tensor("at_spill", [D, NOWN], BF16, kind=spillkind).ap()

    dbg = {}
    if DEBUG:
        dbg["d_kcT"] = nc.dram_tensor("d_kcT", [128, S], BF16, kind="ExternalOutput").ap()
        dbg["d_vc"] = nc.dram_tensor("d_vc", [128, S], BF16, kind="ExternalOutput").ap()
        dbg["d_xnT"] = nc.dram_tensor("d_xnT", [128, ND * NOWN], BF16, kind="ExternalOutput").ap()

    sb = nc.alloc_sbuf_tensor
    scX = Scope()
    xnT = scX.sbuf(nc, "xnT", [128, ND * NOWN], BF16, side="right")
    kcT = sb("kcT", [128, S], BF16)
    vc = sb("vc", [128, S], BF16)
    s_cosk = sb("s_cosk", [64, S], BF16)
    s_srotk = sb("s_srotk", [64, S], BF16)
    s_cosq = scX.sbuf(nc, "s_cosq", [64, NOWN], BF16, side="right")
    s_srotq = scX.sbuf(nc, "s_srotq", [64, NOWN], BF16, side="right")
    s_idb = sb("s_idb", [128, 128], BF16)
    s_idf = sb("s_idf", [128, 128], F32)
    s_mask = sb("s_mask", [128, 128], F32)
    s_maskt = sb("s_maskt", [128, 128], F32)
    s_onesb = sb("s_onesb", [128, 128], BF16)
    s_ones = sb("s_ones", [1, 128], F32)
    wks = sb("wks", [128, H], F32)
    wvs = sb("wvs", [128, H], F32)
    gsv = sb("gsv", [128, NI], F32)
    usv = sb("usv", [128, NI], F32)
    dsv = sb("dsv", [128, ND], F32)

    sem = nc.alloc_semaphore
    sc1 = Scope()
    wkvb = sc1.sbuf(nc, "wkvb", [128, ND * KV], BF16)

    sc0 = Scope()
    whk = sc0.sbuf(nc, "whk", [128, H * 128], F32)
    scr44 = sc0.sbuf(nc, "scr44", [NI, 128], F32)
    scr44b = sc0.sbuf(nc, "scr44b", [NI, 128], F32)
    scr16 = sc0.sbuf(nc, "scr16", [ND, 128], F32)
    p0a = sc0.psum(nc, "p0a", [128, 512], F32)
    s_c0 = sem("s_c0")
    s_g0 = sem("s_g0")
    s_c1 = sem("s_c1")

    with nc.Block() as blk:
        @blk.sync
        def _(sync):
            for dst, src in (
                (s_cosk[:], cosk), (s_cosq[:], cosq),
                (s_srotk[:], srotk), (s_srotq[:], srotq),
                (s_idb[:], identbf), (s_idf[:], identf),
                (s_mask[:], maskf), (s_maskt[:], masktf), (s_ones[:], onesf),
                (scr44[:], gates), (scr44b[:], ups),
                (scr16[:], downs),
                (whk[:].rearrange("p (h k) -> p h k", k=128),
                 wkh.rearrange("h l k -> l h k")),
            ):
                sync.dma_start(dst, src).then_inc(s_c0, 16)

        @blk.gpsimd
        def _(gp):
            gp.memset(s_onesb[:], 1.0)
            gp.dma_start(wkvb[:].rearrange("p (j k) -> p j k", k=KV),
                         wkvT.rearrange("(j l) k -> l j k", l=128)).then_inc(s_g0, 16)

        @blk.vector
        def _(v):
            v.wait_ge(s_c0, 13 * 16)
            v.wait_ge(s_g0, 16)
            for h in range(H):
                v.reduce_sum(wks[:, h:h + 1], whk[:, h * 128:(h + 1) * 128], axis=AX.X)
            v.nop().then_inc(s_c1, 1)

        @blk.tensor
        def _(t):
            t.wait_ge(s_c0, 13 * 16)
            t.transpose(p0a[0:128, 0:NI], scr44[:], s_idf[0:NI, 0:NI])
            t.transpose(p0a[0:128, 64:64 + NI], scr44b[:], s_idf[0:NI, 0:NI])
            t.transpose(p0a[0:128, 128:128 + ND], scr16[:], s_idf[0:ND, 0:ND]).then_inc(s_c1, 1)

        @blk.scalar
        def _(s):
            load_act_set(nc, s, SET_LNEXP)
            s.wait_ge(s_c1, 2)
            s.copy(gsv[:], p0a[0:128, 0:NI])
            s.copy(usv[:], p0a[0:128, 64:64 + NI])
            s.copy(dsv[:], p0a[0:128, 128:128 + ND])

    with nc.Block() as blk:
        @blk.sync
        def _(sync):
            sync.dma_start(whk[:].rearrange("p (h k) -> p h k", k=128),
                           wvh.rearrange("h l k -> l h k")).then_inc(s_c0, 16)

        @blk.vector
        def _(v):
            v.wait_ge(s_c0, 14 * 16)
            for h in range(H):
                ins = v.reduce_sum(wvs[:, h:h + 1], whk[:, h * 128:(h + 1) * 128], axis=AX.X)
            ins.then_inc(s_c1, 1)

    sc0.close()

    xbuf = [sc1.sbuf(nc, f"xbuf{i}", [128, D], F32) for i in range(3)]
    xsq = sc1.sbuf(nc, "xsq", [128, D], BF16)
    xnb = [sc1.sbuf(nc, f"xnb{i}", [128, D], BF16) for i in range(2)]
    xns = [sc1.sbuf(nc, f"xns{i}", [128, ND * 128], BF16) for i in range(2)]
    kctile = [sc1.sbuf(nc, f"kctile{i}", [128, 128], BF16) for i in range(2)]
    sm_sumx = sc1.sbuf(nc, "sm_sumx", [128, 2], F32)
    sm_sumsq = sc1.sbuf(nc, "sm_sumsq", [128, 2], F32)
    sm_mean = sc1.sbuf(nc, "sm_mean", [128, 2], F32)
    sm_msqe = sc1.sbuf(nc, "sm_msqe", [128, 2], F32)
    sm_varp = sc1.sbuf(nc, "sm_varp", [128, 2], F32)
    sm_lnv = sc1.sbuf(nc, "sm_lnv", [128, 2], F32)
    sm_rstd = sc1.sbuf(nc, "sm_rstd", [128, 2], F32)
    ptr = [sc1.psum(nc, f"ptr{i}", [128, 1024], BF16) for i in range(4)]
    pkv = [sc1.psum(nc, f"pkv{i}", [128, 512], F32) for i in range(2)]
    pkt = [sc1.psum(nc, f"pkt{i}", [128, 1024], BF16) for i in range(2)]

    s_x = [sem(f"s_x{i}") for i in range(3)]
    s_sq = sem("s_sq")
    s_var = sem("s_var")
    s_rstd = sem("s_rstd")
    s_xnb = sem("s_xnb")
    s_tr = sem("s_tr")
    s_xe = sem("s_xe")
    s_kv = sem("s_kv")
    s_kce = sem("s_kce")
    s_vce = sem("s_vce")
    s_kt = sem("s_kt")
    s_kte = sem("s_kte")
    s_vch = sem("s_vch")
    s_sch = sem("s_sch")

    NTOT = NT_ROW + NT_OWN

    with nc.Block() as blk:
        @blk.sync
        def _(sync):
            for ti in range(NTOT):
                if ti >= 3:
                    sync.wait_ge(s_sq, ti - 2)
                    sync.wait_ge(s_xnb, ti - 2)
                src = x_row[ti * 128:(ti + 1) * 128, :] if ti < NT_ROW else \
                    x_own[(ti - NT_ROW) * 128:(ti - NT_ROW + 1) * 128, :]
                sync.dma_start(xbuf[ti % 3][:], src).then_inc(s_x[ti % 3], 16)

        @blk.vector
        def _(v):
            nch = 0
            for tt in range(NTOT + 1):
                if tt < NTOT:
                    ti, p = tt, tt % 2
                    v.wait_ge(s_x[ti % 3], 16 * (ti // 3 + 1))
                    v.reduce_sum(sm_sumx[:, p:p + 1], xbuf[ti % 3][:], axis=AX.X).then_inc(s_vch, 1); nch += 1
                    v.wait_ge(s_vch, nch)
                    v.tensor_scalar(sm_mean[:, p:p + 1], sm_sumx[:, p:p + 1], 1.0 / D,
                                    None, ALU.mult).then_inc(s_vch, 1); nch += 1
                    v.wait_ge(s_vch, nch)
                    v.tensor_scalar(sm_msqe[:, p:p + 1], sm_mean[:, p:p + 1],
                                    sm_mean[:, p:p + 1], EPS, ALU.mult, ALU.subtract).then_inc(s_vch, 1); nch += 1
                    v.wait_ge(s_sq, ti + 1)
                    v.wait_ge(s_vch, nch)
                    v.tensor_scalar(sm_varp[:, p:p + 1], sm_sumsq[:, p:p + 1], 1.0 / D,
                                    sm_msqe[:, p:p + 1], ALU.mult, ALU.subtract).then_inc(s_var, 1)
                if tt >= 1:
                    ti, p = tt - 1, (tt - 1) % 2
                    v.wait_ge(s_rstd, ti + 1)
                    if ti >= 2:
                        v.wait_ge(s_tr, 2 * (ti - 1))
                    v.tensor_scalar(xnb[p][:], xbuf[ti % 3][:], sm_mean[:, p:p + 1],
                                    sm_rstd[:, p:p + 1], ALU.subtract, ALU.mult).then_inc(s_xnb, 1)

        @blk.scalar
        def _(s):
            for tt in range(NTOT + 1):
                if tt < NTOT:
                    ti, p = tt, tt % 2
                    s.wait_ge(s_x[ti % 3], 16 * (ti // 3 + 1))
                    s.activation(xsq[:], xbuf[ti % 3][:], AF.Square,
                                 accum_out=sm_sumsq[:, p:p + 1]).then_inc(s_sq, 1)
                    s.wait_ge(s_var, ti + 1)
                    s.activation(sm_lnv[:, p:p + 1], sm_varp[:, p:p + 1], AF.Ln).then_inc(s_sch, 1)
                    s.wait_ge(s_sch, ti + 1)
                    s.activation(sm_rstd[:, p:p + 1], sm_lnv[:, p:p + 1], AF.Exp,
                                 scale=-0.5).then_inc(s_rstd, 1)
                if tt >= 1:
                    ti, p = tt - 1, (tt - 1) % 2
                    s.wait_ge(s_tr, 2 * ti + 1)
                    if ti < NT_ROW:
                        s.copy(xns[p][:, 0:1024], ptr[(2 * ti) % 4][:]).then_inc(s_xe, 1)
                        s.wait_ge(s_tr, 2 * ti + 2)
                        s.copy(xns[p][:, 1024:2048], ptr[(2 * ti + 1) % 4][:]).then_inc(s_xe, 1)
                        s.wait_ge(s_kv, ti + 1)
                        s.copy(kctile[p][:], pkv[p][:, 0:128]).then_inc(s_kce, 1)
                        s.copy(vc[:, ti * 128:(ti + 1) * 128], pkv[p][:, 128:256]).then_inc(s_vce, 1)
                        s.wait_ge(s_kt, ti + 1)
                        s.copy(kcT[:, ti * 128:(ti + 1) * 128], pkt[p][:, 0:128]).then_inc(s_kte, 1)
                    else:
                        o = ti - NT_ROW
                        xv = xnT[:].rearrange("p (j t) -> p j t", t=NOWN)
                        s.copy(xv[:, 0:8, o * 128:(o + 1) * 128],
                               ptr[(2 * ti) % 4][:].rearrange("p (j t) -> p j t", t=128)).then_inc(s_xe, 1)
                        s.wait_ge(s_tr, 2 * ti + 2)
                        s.copy(xv[:, 8:16, o * 128:(o + 1) * 128],
                               ptr[(2 * ti + 1) % 4][:].rearrange("p (j t) -> p j t", t=128)).then_inc(s_xe, 1)

        @blk.tensor
        def _(t):
            for tt in range(1, NTOT + 1):
                ti, p = tt - 1, (tt - 1) % 2
                t.wait_ge(s_xnb, ti + 1)
                if ti >= 2:
                    t.wait_ge(s_xe, 2 * (ti - 1))
                for half8 in range(2):
                    bank = ptr[(2 * ti + half8) % 4]
                    for j in range(8):
                        jj = half8 * 8 + j
                        ins = t.transpose(bank[:, j * 128:(j + 1) * 128],
                                          xnb[p][:, jj * 128:(jj + 1) * 128], s_idb[:])
                        if j == 7:
                            ins.then_inc(s_tr, 1)
                if ti < NT_ROW:
                    t.wait_ge(s_xe, 2 * (ti + 1))
                    if ti >= 2:
                        t.wait_ge(s_kce, ti - 1)
                        t.wait_ge(s_vce, ti - 1)
                    for j in range(ND):
                        ins = t.matmul(pkv[p][:, 0:KV], xns[p][:, j * 128:(j + 1) * 128],
                                       wkvb[:, j * KV:(j + 1) * KV],
                                       start=(j == 0), stop=(j == ND - 1))
                    ins.then_inc(s_kv, 1)
                    t.wait_ge(s_kce, ti + 1)
                    if ti >= 2:
                        t.wait_ge(s_kte, ti - 1)
                    t.transpose(pkt[p][:, 0:128], kctile[p][:], s_idb[:]).then_inc(s_kt, 1)

    sc1.close()

    if DEBUG >= 1:
        s_dbg = sem("s_dbg")
        with nc.Block() as blk:
            @blk.sync
            def _(sync):
                sync.dma_start(dbg["d_kcT"], kcT[:]).then_inc(s_dbg, 16)
                sync.dma_start(dbg["d_vc"], vc[:]).then_inc(s_dbg, 16)
                sync.dma_start(dbg["d_xnT"], xnT[:]).then_inc(s_dbg, 16)
                sync.wait_ge(s_dbg, 48)
    if UPTO <= 1:
        return nc

    NTK = NOWN // 512
    sc2a = Scope()
    gwb = [sc2a.sbuf(nc, f"gwb{i}", [128, ND * 128], BF16) for i in range(2)]
    uwb = [sc2a.sbuf(nc, f"uwb{i}", [128, ND * 128], BF16) for i in range(2)]
    hgb = [sc2a.sbuf(nc, f"hgb{i}", [128, 512], BF16) for i in range(2)]
    ubf = [sc2a.sbuf(nc, f"ubf{i}", [128, 512], BF16) for i in range(2)]
    hdnb = [sc2a.sbuf(nc, f"hdnb{i}", [128, 512], BF16) for i in range(2)]
    pg = [sc2a.psum(nc, f"pg{i}", [128, 512], F32) for i in range(2)]
    pu = [sc2a.psum(nc, f"pu{i}", [128, 512], F32) for i in range(2)]

    s_gw = [sem("s_gw0"), sem("s_gw1")]
    s_uw = [sem("s_uw0"), sem("s_uw1")]
    s_gmm = sem("s_gmm")
    s_umm = sem("s_umm")
    s_sil = sem("s_sil")
    s_hdn = sem("s_hdn")
    s_hsp = [sem("s_hsp0"), sem("s_hsp1")]
    s_vc2 = sem("s_vc2")



    with nc.Block() as blk:
        @blk.gpsimd
        def _(gp):
            for i in range(NI):
                if i >= 2:
                    gp.wait_ge(s_gmm, NTK * (i - 1))
                    gp.wait_ge(s_umm, NTK * (i - 1))
                gp.dma_start(gwb[i % 2][:], gateT[i]).then_inc(s_gw[i % 2], 16)
                gp.dma_start(uwb[i % 2][:], upT[i]).then_inc(s_uw[i % 2], 16)

        @blk.tensor
        def _(t):
            for i in range(NI):
                t.wait_ge(s_gw[i % 2], 16 * (i // 2 + 1))
                t.wait_ge(s_uw[i % 2], 16 * (i // 2 + 1))
                for tk in range(NTK):
                    seq = i * NTK + tk
                    if seq >= 2:
                        t.wait_ge(s_sil, seq - 1)
                    for j in range(ND):
                        ins = t.matmul(pg[seq % 2][:], gwb[i % 2][:, j * 128:(j + 1) * 128],
                                       xnT[:, j * NOWN + tk * 512:j * NOWN + (tk + 1) * 512],
                                       start=(j == 0), stop=(j == ND - 1))
                    ins.then_inc(s_gmm, 1)
                    if seq >= 2:
                        t.wait_ge(s_hdn, seq - 1)
                    for j in range(ND):
                        ins = t.matmul(pu[seq % 2][:], uwb[i % 2][:, j * 128:(j + 1) * 128],
                                       xnT[:, j * NOWN + tk * 512:j * NOWN + (tk + 1) * 512],
                                       start=(j == 0), stop=(j == ND - 1))
                    ins.then_inc(s_umm, 1)

        @blk.scalar
        def _(s):
            load_act_set(nc, s, SET_SILU)
            for i in range(NI):
                for tk in range(NTK):
                    seq = i * NTK + tk
                    s.wait_ge(s_gmm, seq + 1)
                    if seq >= 2:
                        s.wait_ge(s_hdn, seq - 1)
                    s.activation(hgb[seq % 2][:], pg[seq % 2][:], AF.Silu,
                                 scale=gsv[:, i:i + 1]).then_inc(s_sil, 1)

        @blk.vector
        def _(v):
            nv = 0
            for i in range(NI):
                for tk in range(NTK):
                    seq = i * NTK + tk
                    v.wait_ge(s_umm, seq + 1)
                    if seq >= 2:
                        v.wait_ge(s_hsp[seq % 2], 16 * (seq // 2))
                    v.tensor_scalar(ubf[seq % 2][:], pu[seq % 2][:], usv[:, i:i + 1],
                                    None, ALU.mult).then_inc(s_vc2, 1); nv += 1
                    v.wait_ge(s_sil, seq + 1)
                    v.wait_ge(s_vc2, nv)
                    v.tensor_mul(hdnb[seq % 2][:], hgb[seq % 2][:],
                                 ubf[seq % 2][:]).then_inc(s_hdn, 1)

        @blk.sync
        def _(sync):
            for i in range(NI):
                for tk in range(NTK):
                    seq = i * NTK + tk
                    sync.wait_ge(s_hdn, seq + 1)
                    sync.dma_start(hdn_spill[i * 128:(i + 1) * 128, tk * 512:(tk + 1) * 512],
                                   hdnb[seq % 2][:]).then_inc(s_hsp[seq % 2], 16)

    sc2a.close()
    if UPTO <= 2:
        return nc

    sc2b = Scope()
    hdnT = sc2b.sbuf(nc, "hdnT", [128, NI * 512], BF16)
    dstrip = [sc2b.sbuf(nc, f"dstrip{i}", [128, NI * 128], BF16) for i in range(4)]
    ysb = [sc2b.sbuf(nc, f"ysb{i}", [128, 512], F32) for i in range(2)]
    xtb = [sc2b.sbuf(nc, f"xtb{i}", [128, 512], F32) for i in range(2)]
    py = [sc2b.psum(nc, f"py{i}", [128, 512], F32) for i in range(2)]

    s_hl = sem("s_hl")
    s_dw = [sem(f"s_dw{i}") for i in range(4)]
    s_dmm = sem("s_dmm")
    s_yv = sem("s_yv")
    s_yst = [sem("s_yst0"), sem("s_yst1")]
    s_xl = [sem("s_xl0"), sem("s_xl1")]
    s_vb2 = sem("s_vb2")

    spill3 = hdn_spill.rearrange("(j l) k -> l j k", l=128)

    with nc.Block() as blk:
        @blk.sync
        def _(sync):
            nsp = NI * NTK
            sync.wait_ge(s_hsp[0], 16 * ((nsp + 1) // 2))
            sync.wait_ge(s_hsp[1], 16 * (nsp // 2))
            for tc in range(NTK):
                if tc >= 1:
                    sync.wait_ge(s_dmm, ND * tc)
                for part in range(4):
                    j0 = (NI * part) // 4
                    j1 = (NI * (part + 1)) // 4
                    if j1 > j0:
                        sync.dma_start(
                            hdnT[:].rearrange("p (j k) -> p j k", k=512)[:, j0:j1, :],
                            spill3[:, j0:j1, tc * 512:(tc + 1) * 512]).then_inc(s_hl, 16)
                for hb in range(ND):
                    seq = tc * ND + hb
                    if seq >= 2:
                        sync.wait_ge(s_yv, seq - 1)
                    sync.dma_start(xtb[seq % 2][:],
                                   xT_own[hb * 128:(hb + 1) * 128, tc * 512:(tc + 1) * 512]
                                   ).then_inc(s_xl[seq % 2], 16)
                    sync.wait_ge(s_yv, seq + 1)
                    sync.dma_start(out[hb * 128:(hb + 1) * 128, tc * 512:(tc + 1) * 512],
                                   ysb[seq % 2][:]).then_inc(s_yst[seq % 2], 16)

        @blk.gpsimd
        def _(gp):
            for tc in range(NTK):
                for hb in range(ND):
                    seq = tc * ND + hb
                    if seq >= 4:
                        gp.wait_ge(s_dmm, seq - 3)
                    gp.dma_start(dstrip[seq % 4][:], downT[hb]).then_inc(s_dw[seq % 4], 16)
            gp.wait_ge(s_yst[0], 16 * ((NTK * ND + 1) // 2))
            gp.wait_ge(s_yst[1], 16 * ((NTK * ND) // 2))

        @blk.tensor
        def _(t):
            for tc in range(NTK):
                t.wait_ge(s_hl, 16 * 4 * (tc + 1))
                for hb in range(ND):
                    seq = tc * ND + hb
                    t.wait_ge(s_dw[seq % 4], 16 * (seq // 4 + 1))
                    if seq >= 2:
                        t.wait_ge(s_yv, seq - 1)
                    for jb in range(NI):
                        ins = t.matmul(py[seq % 2][:], dstrip[seq % 4][:, jb * 128:(jb + 1) * 128],
                                       hdnT[:, jb * 512:(jb + 1) * 512],
                                       start=(jb == 0), stop=(jb == NI - 1))
                    ins.then_inc(s_dmm, 1)

        @blk.vector
        def _(v):
            for tc in range(NTK):
                for hb in range(ND):
                    seq = tc * ND + hb
                    v.wait_ge(s_dmm, seq + 1)
                    if seq >= 2:
                        v.wait_ge(s_yst[seq % 2], 16 * (seq // 2))
                    v.tensor_scalar(ysb[seq % 2][:], py[seq % 2][:], dsv[:, hb:hb + 1],
                                    None, ALU.mult).then_inc(s_vb2, 1)
                    v.wait_ge(s_vb2, seq + 1)
                    v.wait_ge(s_xl[seq % 2], 16 * (seq // 2 + 1))
                    v.tensor_add(ysb[seq % 2][:], ysb[seq % 2][:],
                                 xtb[seq % 2][:]).then_inc(s_yv, 1)

    sc2b.close()
    if UPTO <= 3:
        return nc

    scheds = (sched_for(0), sched_for(1))
    NSLOT = len(HALF_CHUNKS[0])
    PQ = max(L for sch in scheds for L, _ in sch)
    NQC = NOWN // 512

    scQ = Scope()
    QT = scQ.sbuf(nc, "QT", [128, H * NOWN], BF16)
    wvball = scQ.sbuf(nc, "wvball", [128, H * 128], BF16)

    sc3 = Scope()
    wqb = [sc3.sbuf(nc, f"wqb{i}", [128, ND * 128], BF16) for i in range(2)]
    qta = sc3.sbuf(nc, "qta", [32, 512], F32)
    qtb = sc3.sbuf(nc, "qtb", [32, 512], F32)
    qtc = sc3.sbuf(nc, "qtc", [32, 512], F32)
    qtd = sc3.sbuf(nc, "qtd", [32, 512], F32)
    vrowq = sc3.sbuf(nc, "vrowq", [1, 128], F32)
    pq3 = [sc3.psum(nc, f"pq3{i}", [128, 512], F32) for i in range(2)]
    pbw = sc3.psum(nc, "pbw", [128, 256], F32)

    s_wq = [sem("s_wq0"), sem("s_wq1")]
    s_qmm = sem("s_qmm")
    s_qe = sem("s_qe")
    s_qrope = sem("s_qrope")
    s_v3 = sem("s_v3")
    s_bt = sem("s_bt")
    s_bv = sem("s_bv")
    s_bm = sem("s_bm")
    s_bw = sem("s_bw")

    with nc.Block() as blk:
        @blk.gpsimd
        def _(gp):
            for h in range(H):
                if h >= 2:
                    gp.wait_ge(s_qmm, NQC * (h - 1))
                gp.dma_start(wqb[h % 2][:], wqT[h]).then_inc(s_wq[h % 2], 16)

        @blk.tensor
        def _(t):
            for h in range(H):
                t.wait_ge(s_wq[h % 2], 16 * (h // 2 + 1))
                for nk in range(NQC):
                    seq = h * NQC + nk
                    if seq >= 2:
                        t.wait_ge(s_qe, seq - 1)
                        t.wait_ge(s_qrope, seq - 1)
                    for j in range(ND):
                        ins = t.matmul(pq3[seq % 2][:], wqb[h % 2][:, j * 128:(j + 1) * 128],
                                       xnT[:, j * NOWN + nk * 512:j * NOWN + (nk + 1) * 512],
                                       start=(j == 0), stop=(j == ND - 1))
                    ins.then_inc(s_qmm, 1)
            t.wait_ge(s_c1, 3)
            for h in range(H):
                if h >= 1:
                    t.wait_ge(s_bv, h)
                    t.wait_ge(s_bw, h)
                t.transpose(pbw[0:1, 0:128], wvs[:, h:h + 1], s_idf[:]).then_inc(s_bt, 1)
                t.wait_ge(s_bv, h + 1)
                if h >= 1:
                    t.wait_ge(s_bw, h)
                t.matmul(pbw[:, 128:256], s_ones[0:1, 0:128], vrowq[:],
                         start=True, stop=True).then_inc(s_bm, 1)

        @blk.scalar
        def _(sx):
            for h in range(H):
                for nk in range(NQC):
                    seq = h * NQC + nk
                    sx.wait_ge(s_qmm, seq + 1)
                    sx.activation(QT[64:128, h * NOWN + nk * 512:h * NOWN + (nk + 1) * 512],
                                  pq3[seq % 2][64:128, :], AF.Copy,
                                  scale=SCL).then_inc(s_qe, 1)
            for h in range(H):
                sx.wait_ge(s_bt, h + 1)
                sx.copy(vrowq[:], pbw[0:1, 0:128]).then_inc(s_bv, 1)
                sx.wait_ge(s_bm, h + 1)
                sx.copy(wvball[:, h * 128:(h + 1) * 128], pbw[:, 128:256]).then_inc(s_bw, 1)

        @blk.vector
        def _(v):
            nch = 0
            for h in range(H):
                for nk in range(NQC):
                    seq = h * NQC + nk
                    v.wait_ge(s_qmm, seq + 1)
                    v.wait_ge(s_v3, nch)
                    v.wait_ge(s_qrope, seq)
                    pa = pq3[seq % 2]
                    ca, sa = pa[0:32, :], pa[32:64, :]
                    ck = s_cosq[:, nk * 512:(nk + 1) * 512]
                    rq = s_srotq[:, nk * 512:(nk + 1) * 512]
                    qcol = h * NOWN + nk * 512
                    v.tensor_mul(qta[:], ca, ck[0:32, :]).then_inc(s_v3, 1); nch += 1
                    v.tensor_mul(qtb[:], sa, rq[32:64, :]).then_inc(s_v3, 1); nch += 1
                    v.tensor_mul(qtc[:], sa, ck[32:64, :]).then_inc(s_v3, 1); nch += 1
                    v.tensor_mul(qtd[:], ca, rq[0:32, :]).then_inc(s_v3, 1); nch += 1
                    v.wait_ge(s_v3, nch)
                    v.tensor_sub(QT[0:32, qcol:qcol + 512], qta[:], qtb[:]).then_inc(s_v3, 1)
                    nch += 1
                    v.tensor_add(QT[32:64, qcol:qcol + 512], qtc[:], qtd[:]).then_inc(s_qrope, 1)

    sc3.close()
    scX.close()

    sc4 = Scope()
    kbuf = [sc4.sbuf(nc, f"kbuf{i}", [128, S], BF16) for i in range(2)]
    vbuf = [sc4.sbuf(nc, f"vbuf{i}", [128, S], BF16) for i in range(2)]
    ktmp = sc4.sbuf(nc, "ktmp", [64, S], BF16)
    pts = [sc4.sbuf(nc, f"pts{i}", [128, 512], BF16) for i in range(3)]
    atst = [sc4.sbuf(nc, f"atst{i}", [128, 512], BF16) for i in range(2)]
    mxc = sc4.sbuf(nc, "mxc", [128, 8], F32)
    nmq = sc4.sbuf(nc, "nmq", [128, 8], F32)
    nmrow = [sc4.sbuf(nc, f"nmrow{i}", [1, 512], BF16) for i in range(2)]
    srow = sc4.sbuf(nc, "srow", [1, 512], F32)
    rrow = sc4.sbuf(nc, "rrow", [1, 512], BF16)
    rbsb = sc4.sbuf(nc, "rbsb", [128, 512], BF16)

    sp1 = [sc4.psum(nc, f"sp1{i}", [128, 512], F32) for i in range(3)]
    sp2 = [sc4.psum(nc, f"sp2{i}", [128, 512], F32) for i in range(2)]
    pvp = sc4.psum(nc, "pvp", [128, 512], F32)
    prow = sc4.psum(nc, "prow", [128, 512], F32)
    rbank = sc4.psum(nc, "rbank", [128, 512], F32)

    s_m1 = sem("s_m1")
    s_mx = sem("s_mx")
    s_cmb = sem("s_cmb")
    s_prt = sem("s_prt")
    s_nr = sem("s_nr")
    s_m2 = sem("s_m2")
    s_msk = sem("s_msk")
    s_ex = sem("s_ex")
    s_z = sem("s_z")
    s_pv = sem("s_pv")
    s_sg = sem("s_sg")
    s_rr = sem("s_rr")
    s_rb = sem("s_rb")
    s_rbe = sem("s_rbe")
    s_ae = sem("s_ae")
    s_asp = [sem("s_asp0"), sem("s_asp1")]
    s_kvb = sem("s_kvb")
    s_kb2 = sem("s_kb2")
    s_v4 = sem("s_v4")

    syncq = nc.sync
    for idx in (range(P4_NH * NSLOT) if P4_STAGE >= 3 else ()):
        syncq.wait_ge(s_ae, idx + 1)
        h, k = idx // NSLOT, idx % NSLOT
        syncq.dma_start(at_spill[h * 128:(h + 1) * 128, k * 512:(k + 1) * 512],
                        atst[idx % 2][:]).then_inc(s_asp[idx % 2], 16)

    def emit_p4(sched, cnt):
        t, v, s = nc.tensor, nc.vector, nc.scalar

        def vch(ins):
            ins.then_inc(s_v4, 1)
            cnt["v4"] += 1
            return ins

        def vwait():
            v.wait_ge(s_v4, cnt["v4"])

        a_guard = [[], [], []]
        b_guard = [[], []]
        load_act_set(nc, s, SET_LNEXP)
        if DEBUG >= 3:
            v.memset(mxc[:], 0.0)
            vch(v.memset(nmq[:], 0.0))

        def emit_kv(g, gm1, gm2):
            gp_ = g % 2
            if g >= 1:
                v.wait_ge(s_m1, gm1)
                v.wait_ge(s_m2, gm2)
            vch(v.tensor_scalar(kbuf[gp_][:], kcT[:], wks[:, g:g + 1], None, ALU.mult))
            yield
            vwait()
            vch(v.tensor_mul(ktmp[0:32, :], kbuf[gp_][32:64, :], s_srotk[32:64, :]))
            yield
            vch(v.tensor_mul(ktmp[32:64, :], kbuf[gp_][0:32, :], s_srotk[0:32, :]))
            yield
            vwait()
            vch(v.tensor_mul(kbuf[gp_][0:32, :], kbuf[gp_][0:32, :], s_cosk[0:32, :]))
            yield
            vch(v.tensor_mul(kbuf[gp_][32:64, :], kbuf[gp_][32:64, :], s_cosk[32:64, :]))
            yield
            vwait()
            vch(v.tensor_sub(kbuf[gp_][0:32, :], kbuf[gp_][0:32, :], ktmp[0:32, :]))
            yield
            v.tensor_add(kbuf[gp_][32:64, :], kbuf[gp_][32:64, :],
                         ktmp[32:64, :]).then_inc(s_kb2, 1)

        def emit_v(g, gpv):
            gp_ = g % 2
            if g == 0:
                v.wait_ge(s_bw, H)
            if g >= 2:
                v.wait_ge(s_pv, gpv)
            wvb_b = wvball[:, g * 128:(g + 1) * 128].rearrange(
                "p (j k) -> p j k", j=1).broadcast_to((128, NT_ROW, 128))
            v.tensor_tensor(vbuf[gp_][:].rearrange("p (j k) -> p j k", k=128),
                            vc[:].rearrange("p (j k) -> p j k", k=128),
                            wvb_b, ALU.mult).then_inc(s_kvb, 1)

        def emit_A(h, k, sp, L, tqs):
            hp = h % 2
            qs0 = h * NOWN + k * 512
            for q in range(4):
                tq = tqs[q]
                nch = (tq + 511) // 512
                v.wait_ge(s_cmb, cnt["cmb"])
                for c in range(nch):
                    w = min(512, tq - c * 512)
                    ar = cnt["m1"] % 3
                    for smv in a_guard[ar]:
                        t.wait_ge(*smv)
                    a_guard[ar] = []
                    t.matmul(sp1[ar][:, 0:w], QT[:, qs0 + q * 128:qs0 + (q + 1) * 128],
                             kbuf[hp][:, c * 512:c * 512 + w],
                             start=True, stop=True).then_inc(s_m1, 1)
                    cnt["m1"] += 1
                    v.wait_ge(s_m1, cnt["m1"])
                    if c == nch - 1:
                        vch(v.tensor_tensor(sp1[ar][:, w - 128:w], sp1[ar][:, w - 128:w],
                                            s_mask[:], ALU.add))
                        vwait()
                    v.reduce_max(mxc[:, c:c + 1], sp1[ar][:, 0:w],
                                 axis=AX.X).then_inc(s_mx, 1)
                    cnt["mx"] += 1
                    a_guard[ar] = [(s_mx, cnt["mx"])]
                    yield
                v.wait_ge(s_prt, cnt["prt_par"][sp])
                v.wait_ge(s_mx, cnt["mx"])
                v.reduce_max(nmq[:, sp * 4 + q:sp * 4 + q + 1], mxc[:, 0:nch],
                             axis=AX.X, negate=True).then_inc(s_cmb, 1)
                cnt["cmb"] += 1
                t.wait_ge(s_cmb, cnt["cmb"])
                if q == 0:
                    t.wait_ge(s_nr, cnt["nr"])
                t.transpose(prow[0:1, q * 128:(q + 1) * 128],
                            nmq[:, sp * 4 + q:sp * 4 + q + 1],
                            s_idf[:]).then_inc(s_prt, 1)
                cnt["prt"] += 1
                yield
            cnt["prt_par"][sp] = cnt["prt"]
            s.wait_ge(s_prt, cnt["prt"])
            s.wait_ge(s_m2, cnt["m2_sl"][sp])
            s.copy(nmrow[sp][:], prow[0:1, 0:512]).then_inc(s_nr, 1)
            cnt["nr"] += 1

        def emit_sigpv(hp, ntb, tb, ex_cnt, z_cnt, g):
            ptsb = pts[g % 3]
            t.wait_ge(s_ex, ex_cnt)
            if z_cnt is not None:
                t.wait_ge(s_z, z_cnt)
            if tb == 0:
                t.wait_ge(s_rbe, cnt["rbe"])
                t.wait_ge(s_ae, cnt["ae"])
            ins = t.matmul(rbank[0:1, 0:512], s_onesb[:, 0:1], ptsb[:],
                           start=(tb == 0), stop=(tb == ntb - 1))
            if tb == ntb - 1:
                ins.then_inc(s_sg, 1)
                cnt["sg"] += 1
            t.matmul(pvp[:], vbuf[hp][:, tb * 128:(tb + 1) * 128], ptsb[:],
                     start=(tb == 0), stop=(tb == ntb - 1)).then_inc(s_pv, 1)
            cnt["pv"] += 1
            cnt["pv_hist"][g] = cnt["pv"]

        def emit_CD(h, k, sp, L, tqs, nrv):
            hp = h % 2
            ntb = L // 128
            qs0 = h * NOWN + k * 512
            if k == 0:
                t.wait_ge(s_kvb, h + 1)
            t.wait_ge(s_nr, nrv)
            pend = None
            for tb in range(ntb):
                g2 = cnt["m2"] % 2
                for smv in b_guard[g2]:
                    t.wait_ge(*smv)
                b_guard[g2] = []
                t.matmul(sp2[g2][:], kbuf[hp][:, tb * 128:(tb + 1) * 128],
                         QT[:, qs0:qs0 + 512], start=True, stop=False)
                t.matmul(sp2[g2][:], s_onesb[0:1, 0:128], nmrow[sp][:],
                         start=False, stop=True).then_inc(s_m2, 1)
                cnt["m2"] += 1
                dq = tb - (ntb - 4)
                if 0 <= dq < 4:
                    v.wait_ge(s_m2, cnt["m2"])
                    v.tensor_tensor(sp2[g2][:, dq * 128:(dq + 1) * 128],
                                    sp2[g2][:, dq * 128:(dq + 1) * 128],
                                    s_maskt[:], ALU.add).then_inc(s_msk, 1)
                    cnt["msk"] += 1
                g = cnt["gtb"]
                cnt["gtb"] += 1
                ptsb = pts[g % 3]
                s.wait_ge(s_m2, cnt["m2"])
                if 0 <= dq < 4:
                    s.wait_ge(s_msk, cnt["msk"])
                if g >= 3:
                    s.wait_ge(s_pv, cnt["pv_hist"][g - 3])
                s.activation(ptsb[:], sp2[g2][:], AF.Exp).then_inc(s_ex, 1)
                cnt["ex"] += 1
                b_guard[g2] = [(s_ex, cnt["ex"])]
                zq = tb - (ntb - 3)
                if zq >= 0:
                    v.wait_ge(s_ex, cnt["ex"])
                    nz = min(zq, 2) + 1
                    for q in range(nz):
                        ins = v.memset(ptsb[:, q * 128:(q + 1) * 128], 0.0)
                        if q < nz - 1:
                            ins.then_inc(s_v4, 1)
                            cnt["v4"] += 1
                    ins.then_inc(s_z, 1)
                    cnt["z"] += 1
                if pend is not None:
                    emit_sigpv(hp, ntb, *pend)
                pend = (tb, cnt["ex"], cnt["z"] if zq >= 0 else None, g)
                yield
            emit_sigpv(hp, ntb, *pend)
            s.wait_ge(s_sg, cnt["sg"])
            s.activation(srow[:], rbank[0:1, 0:512], AF.Ln).then_inc(s_rr, 1)
            cnt["rr"] += 1
            s.wait_ge(s_rr, cnt["rr"])
            s.activation(rrow[:], srow[:], AF.Exp, scale=-1.0).then_inc(s_rr, 1)
            cnt["rr"] += 1
            t.wait_ge(s_rr, cnt["rr"])
            t.matmul(rbank[:, 0:512], s_onesb[0:1, 0:128], rrow[:],
                     start=True, stop=True).then_inc(s_rb, 1)
            cnt["rb"] += 1
            s.wait_ge(s_rb, cnt["rb"])
            s.copy(rbsb[:], rbank[:, 0:512]).then_inc(s_rbe, 1)
            cnt["rbe"] += 1
            ai = cnt["ae"]
            v.wait_ge(s_rbe, cnt["rbe"])
            v.wait_ge(s_pv, cnt["pv"])
            if ai >= 2:
                v.wait_ge(s_asp[ai % 2], 16 * (ai // 2))
            v.tensor_tensor(atst[ai % 2][:], pvp[:], rbsb[:], ALU.mult).then_inc(s_ae, 1)
            cnt["ae"] += 1
            cnt["m2_sl"][sp] = cnt["m2"]

        for _ in emit_kv(0, 0, 0):
            pass
        emit_v(0, 0)
        if P4_STAGE < 3:
            for g in range(1, P4_NH):
                for _ in emit_kv(g, 0, 0):
                    pass
                emit_v(g, 0)
            return

        state = dict(pending=None, kvgen=None, vhead=0)

        def step_bg():
            if state["kvgen"] is not None:
                try:
                    next(state["kvgen"])
                except StopIteration:
                    state["kvgen"] = None
                    emit_v(state["vhead"], cnt["pv"])

        def step_pend():
            if state["pending"] is None:
                return False
            try:
                next(state["pending"])
                return True
            except StopIteration:
                state["pending"] = None
                return False

        kv_created = 0
        for h in range(P4_NH):
            while state["kvgen"] is not None:
                step_bg()
            t.wait_ge(s_kb2, h + 1)
            for k in range(NSLOT):
                L, tqs = sched[k]
                sp = cnt["sl"] % 2
                cnt["sl"] += 1
                gA = emit_A(h, k, sp, L, tqs)
                for _ in gA:
                    step_pend()
                    step_bg()
                while step_pend():
                    step_bg()
                if k == 0 and h + 1 < P4_NH and kv_created == h:
                    state["kvgen"] = emit_kv(h + 1, cnt["m1"], cnt["m2"])
                    state["vhead"] = h + 1
                    kv_created = h + 1
                state["pending"] = emit_CD(h, k, sp, L, tqs, cnt["nr"])
        while state["pending"] is not None or state["kvgen"] is not None:
            step_pend()
            step_bg()

    def fresh_cnt():
        return dict(m1=0, mx=0, cmb=0, prt=0, nr=0, m2=0, msk=0, ex=0, z=0,
                    pv=0, sg=0, rr=0, rb=0, rbe=0, ae=0, v4=0, sl=0, gtb=0,
                    prt_par=[0, 0], m2_sl=[0, 0], pv_hist={})

    engs = [nc.tensor, nc.vector, nc.scalar]
    idxs = [e.partition_id() % 2 for e in engs]
    for ib in nc.Switch(engines=engs, index=idxs, n=2):
        emit_p4(scheds[ib], fresh_cnt())

    if DEBUG >= 3:
        dbg3 = {}
        for nm, tns in (("d_wks", wks), ("d_wvs", wvs),
                        ("d_vbuf", vbuf[(P4_NH - 1) % 2]), ("d_kbuf", kbuf[(P4_NH - 1) % 2]),
                        ("d_qh", QT), ("d_pbuf", pts[0]),
                        ("d_pbuf1", pts[1]), ("d_pbuf2", pts[2]),
                        ("d_nmrow", nmrow[0]), ("d_srow", srow),
                        ("d_rbsb", rbsb),
                        ("d_nmx", nmq), ("d_mxc", mxc)):
            shp = list(tns[:].shape)
            dt = tns[:].dtype
            dbg3[nm] = (nc.dram_tensor(nm, shp, dt, kind="ExternalOutput").ap(), tns)
        s_d3 = sem("s_d3")
        sy = nc.sync
        sy.wait_ge(s_ae, P4_NH * NSLOT)
        nd3 = 0
        for nm, (dap, tns) in dbg3.items():
            sy.dma_start(dap, tns[:]).then_inc(s_d3, 16)
            nd3 += 16
        sy.wait_ge(s_d3, nd3)

    if UPTO <= 4:
        sc4.close()
        scQ.close()
        return nc

    with nc.Block():
        pass
    sc4.close()
    scQ.close()
    sc5 = Scope()
    atsb = sc5.sbuf(nc, "atsb", [128, ND * NOWN], BF16)
    wob = [sc5.sbuf(nc, f"wob{i}", [128, ND * 128], BF16) for i in range(2)]
    oysb = [sc5.sbuf(nc, f"oysb{i}", [128, 512], F32) for i in range(2)]
    pyo = [sc5.psum(nc, f"pyo{i}", [128, 512], F32) for i in range(2)]

    s_al = sem("s_al")
    s_wo = [sem("s_wo0"), sem("s_wo1")]
    s_omm = sem("s_omm")
    s_oye = sem("s_oye")
    s_yo = [sem("s_yo0"), sem("s_yo1")]

    at3 = at_spill.rearrange("(j l) k -> l j k", l=128)

    with nc.Block() as blk:
        @blk.sync
        def _(sync):
            sync.wait_ge(s_asp[0], 16 * ((H * NSLOT + 1) // 2))
            sync.wait_ge(s_asp[1], 16 * ((H * NSLOT) // 2))
            for part in range(4):
                j0, j1 = (ND * part) // 4, (ND * (part + 1)) // 4
                sync.dma_start(atsb[:].rearrange("p (j k) -> p j k", k=NOWN)[:, j0:j1, :],
                               at3[:, j0:j1, :]).then_inc(s_al, 16)

        @blk.gpsimd
        def _(gp):
            NQC5 = NOWN // 512
            nacc = 0

            def emit_acc(gp, seq):
                gp.wait_ge(s_oye, seq + 1)
                ob_, tc_ = seq // NQC5, seq % NQC5
                gp.dma_start(out[ob_ * 128:(ob_ + 1) * 128, tc_ * 512:(tc_ + 1) * 512],
                             oysb[seq % 2][:], accum_op=ALU.add).then_inc(s_yo[seq % 2], 16)

            for ob in range(ND):
                if ob >= 2:
                    while nacc < NQC5 * (ob - 1) - 1:
                        emit_acc(gp, nacc)
                        nacc += 1
                    gp.wait_ge(s_omm, NQC5 * (ob - 1))
                gp.dma_start(wob[ob % 2][:], woT[ob]).then_inc(s_wo[ob % 2], 16)
            while nacc < ND * NQC5:
                emit_acc(gp, nacc)
                nacc += 1
            gp.wait_ge(s_yo[0], 16 * ((ND * NQC5 + 1) // 2))
            gp.wait_ge(s_yo[1], 16 * ((ND * NQC5) // 2))

        @blk.tensor
        def _(t):
            t.wait_ge(s_al, 16 * 4)
            for ob in range(ND):
                t.wait_ge(s_wo[ob % 2], 16 * (ob // 2 + 1))
                for tc in range(NOWN // 512):
                    seq = ob * (NOWN // 512) + tc
                    if seq >= 2:
                        t.wait_ge(s_oye, seq - 1)
                    for j in range(ND):
                        ins = t.matmul(pyo[seq % 2][:], wob[ob % 2][:, j * 128:(j + 1) * 128],
                                       atsb[:, j * NOWN + tc * 512:j * NOWN + (tc + 1) * 512],
                                       start=(j == 0), stop=(j == ND - 1))
                    ins.then_inc(s_omm, 1)

        @blk.scalar
        def _(s):
            for seq in range(ND * (NOWN // 512)):
                s.wait_ge(s_omm, seq + 1)
                if seq >= 2:
                    s.wait_ge(s_yo[seq % 2], 16 * (seq // 2))
                s.copy(oysb[seq % 2][:], pyo[seq % 2][:]).then_inc(s_oye, 1)

    sc5.close()
    return nc



def kernel(L0_attn_x, L0_attn_W_q, L0_attn_W_kv, L0_attn_W_k_heads,
           L0_attn_W_v_heads, L0_attn_W_o, L0_ffn_gate_w, L0_ffn_up_w,
           L0_ffn_down_w, L0_ffn_gate_s, L0_ffn_up_s, L0_ffn_down_s):
    x = np.ascontiguousarray(L0_attn_x, dtype=np.float32)
    tabs = host_tables()
    def tile_blocks(wT, nblk):
        din, dout = wT.shape
        nj = din // 128
        out = np.empty((nblk, 128, din), dtype=wT.dtype)
        w3 = wT.reshape(nj, 128, dout)
        for b in range(nblk):
            out[b] = w3[:, :, b * 128:(b + 1) * 128].transpose(1, 0, 2).reshape(128, din)
        return out

    shared = dict(
        wqT=tile_blocks(np.ascontiguousarray(np.asarray(L0_attn_W_q).T), H),
        wkvT=np.ascontiguousarray(np.asarray(L0_attn_W_kv).T),
        wkh=np.ascontiguousarray(L0_attn_W_k_heads),
        wvh=np.ascontiguousarray(L0_attn_W_v_heads),
        woT=tile_blocks(np.ascontiguousarray(np.asarray(L0_attn_W_o).T), ND),
        gateT=tile_blocks(np.ascontiguousarray(np.asarray(L0_ffn_gate_w).T), NI),
        upT=tile_blocks(np.ascontiguousarray(np.asarray(L0_ffn_up_w).T), NI),
        downT=tile_blocks(np.ascontiguousarray(np.asarray(L0_ffn_down_w).T), ND),
        gates=np.ascontiguousarray(np.asarray(L0_ffn_gate_s).reshape(NI, 128)),
        ups=np.ascontiguousarray(np.asarray(L0_ffn_up_s).reshape(NI, 128)),
        downs=np.ascontiguousarray(np.asarray(L0_ffn_down_s).reshape(ND, 128)),
        cosk=tabs["cosk"], srotk=tabs["srotk"],
        identbf=tabs["identbf"], identf=tabs["identf"], maskf=tabs["maskf"],
        masktf=tabs["masktf"],
        onesf=tabs["onesf"],
    )
    in_maps = []
    for c in range(8):
        b, half = c // 2, c % 2
        rows = own_rows(half)
        m = dict(shared)
        m["x_row"] = np.ascontiguousarray(x[b])
        m["x_own"] = np.ascontiguousarray(x[b][rows])
        m["xT_own"] = np.ascontiguousarray(x[b][rows].T)
        m["cosq"] = np.ascontiguousarray(tabs["cosq_full"][:, rows])
        m["srotq"] = np.ascontiguousarray(tabs["srotq_full"][:, rows])
        in_maps.append(m)

    nc = build_nc()
    res = run_bass_kernel_spmd(nc, in_maps, core_ids=list(range(8)))

    full = np.empty((B, S, D), dtype=np.float32)
    for c in range(8):
        b, half = c // 2, c % 2
        full[b, own_rows(half)] = np.asarray(res.results[c]["out"]).T
    return full


# version 16
# speedup vs baseline: 1.1968x; 1.0267x over previous
import math

import numpy as np
import ml_dtypes
import concourse.bass as bass
import concourse.mybir as mybir
from concourse.bass_utils import run_bass_kernel_spmd
from concourse.hw_specs import get_activation_tables

F32 = mybir.dt.float32
BF16 = mybir.dt.bfloat16
I32 = mybir.dt.int32
AF = mybir.ActivationFunctionType
ALU = mybir.AluOpType
AX = mybir.AxisListType

B, S, D, H, DH, KV, INTER = 4, 4096, 2048, 16, 128, 256, 5632
NT_ROW = S // 128
NT_OWN = 16
ND = D // 128
NI = INTER // 128
HALF_CHUNKS = ([0, 1, 6, 7], [2, 3, 4, 5])
NOWN = 2048
SCL = 1.0 / math.sqrt(DH)
EPS = 1e-5
NEG = -1e30

DEBUG = 0
P4_NH = 16
P4_STAGE = 4
UPTO = 5


def act_set_idx(*need):
    import concourse.bass as _b
    nc = _b.Bass("TRN2", target_bir_lowering=False)
    for i, funcs in enumerate(get_activation_tables(nc.m.arch).values()):
        names = {f.name for f in funcs}
        if set(need) <= names:
            return i
    raise ValueError(need)


SET_LNEXP = 6
SET_SILU = 18


def load_act_set(nc, s, idx):
    s.add_instruction(mybir.InstLoadActFuncSet(
        name=nc.get_next_instruction_name(), act_func_set_id=idx, ins=[], outs=[]))


def host_tables():
    rope_half = DH // 4
    freqs = 1.0 / (10000.0 ** (np.arange(rope_half, dtype=np.float32) / np.float32(rope_half)))
    ang = np.outer(np.arange(S, dtype=np.float32), freqs)
    cos = np.repeat(np.cos(ang), 2, axis=-1)
    sin = np.repeat(np.sin(ang), 2, axis=-1)
    bf = ml_dtypes.bfloat16
    return {
        "cosk": np.ascontiguousarray(cos.T).astype(bf),
        "sink": np.ascontiguousarray(sin.T).astype(bf),
        "srotk": np.ascontiguousarray(np.concatenate([sin[:, 32:], sin[:, :32]], 1).T).astype(bf),
        "cosq_full": np.ascontiguousarray((cos * SCL).T).astype(bf),
        "sinq_full": np.ascontiguousarray((sin * SCL).T).astype(bf),
        "srotq_full": np.ascontiguousarray(
            np.concatenate([sin[:, 32:] * SCL, sin[:, :32] * SCL], 1).T).astype(bf),
        "identbf": np.eye(128, dtype=bf),
        "identf": np.eye(128, dtype=np.float32),
        "maskf": np.triu(np.full((128, 128), NEG, np.float32), 1),
        "masktf": np.tril(np.full((128, 128), NEG, np.float32), -1),
        "onesf": np.ones((1, 128), dtype=np.float32),
    }


def own_rows(half):
    return np.concatenate([np.arange(c * 512, (c + 1) * 512) for c in HALF_CHUNKS[half]])


def sched_for(half):
    out = []
    for c in HALF_CHUNKS[half]:
        L = (c + 1) * 512
        out.append((L, [L - 384 + 128 * q for q in range(4)]))
    return out


class Scope:

    def __init__(self):
        self.cms = []

    def sbuf(self, nc, name, shape, dtype, side=None):
        cm = nc.sbuf_tensor(name, shape, dtype, side=side)
        self.cms.append(cm)
        return cm.__enter__()

    def psum(self, nc, name, shape, dtype):
        cm = nc.psum_tensor(name, shape, dtype)
        self.cms.append(cm)
        return cm.__enter__()

    def close(self):
        for cm in reversed(self.cms):
            cm.__exit__(None, None, None)
        self.cms = []


def build_nc():
    nc = bass.Bass("TRN2", target_bir_lowering=False)

    x_row = nc.dram_tensor("x_row", [S, D], F32, kind="ExternalInput").ap()
    x_own = nc.dram_tensor("x_own", [NOWN, D], F32, kind="ExternalInput").ap()
    xT_own = nc.dram_tensor("xT_own", [D, NOWN], F32, kind="ExternalInput").ap()
    wqT = nc.dram_tensor("wqT", [H, 128, D], F32, kind="ExternalInput").ap()
    wkvT = nc.dram_tensor("wkvT", [D, KV], F32, kind="ExternalInput").ap()
    wkh = nc.dram_tensor("wkh", [H, 128, 128], F32, kind="ExternalInput").ap()
    wvh = nc.dram_tensor("wvh", [H, 128, 128], F32, kind="ExternalInput").ap()
    woT = nc.dram_tensor("woT", [ND, 128, D], F32, kind="ExternalInput").ap()
    gateT = nc.dram_tensor("gateT", [NI, 128, D], F32, kind="ExternalInput").ap()
    upT = nc.dram_tensor("upT", [NI, 128, D], F32, kind="ExternalInput").ap()
    downT = nc.dram_tensor("downT", [ND, 128, INTER], F32, kind="ExternalInput").ap()
    gates = nc.dram_tensor("gates", [NI, 128], F32, kind="ExternalInput").ap()
    ups = nc.dram_tensor("ups", [NI, 128], F32, kind="ExternalInput").ap()
    downs = nc.dram_tensor("downs", [ND, 128], F32, kind="ExternalInput").ap()
    cosk = nc.dram_tensor("cosk", [64, S], BF16, kind="ExternalInput").ap()
    srotk = nc.dram_tensor("srotk", [64, S], BF16, kind="ExternalInput").ap()
    cosq = nc.dram_tensor("cosq", [64, NOWN], BF16, kind="ExternalInput").ap()
    srotq = nc.dram_tensor("srotq", [64, NOWN], BF16, kind="ExternalInput").ap()
    identbf = nc.dram_tensor("identbf", [128, 128], BF16, kind="ExternalInput").ap()
    identf = nc.dram_tensor("identf", [128, 128], F32, kind="ExternalInput").ap()
    maskf = nc.dram_tensor("maskf", [128, 128], F32, kind="ExternalInput").ap()
    masktf = nc.dram_tensor("masktf", [128, 128], F32, kind="ExternalInput").ap()
    onesf = nc.dram_tensor("onesf", [1, 128], F32, kind="ExternalInput").ap()

    out = nc.dram_tensor("out", [D, NOWN], F32, kind="ExternalOutput").ap()

    spillkind = "ExternalOutput" if DEBUG >= 2 else "Internal"
    hdn_spill = nc.dram_tensor("hdn_spill", [INTER, NOWN], BF16, kind=spillkind).ap()
    at_spill = nc.dram_tensor("at_spill", [D, NOWN], BF16, kind=spillkind).ap()

    dbg = {}
    if DEBUG:
        dbg["d_kcT"] = nc.dram_tensor("d_kcT", [128, S], BF16, kind="ExternalOutput").ap()
        dbg["d_vc"] = nc.dram_tensor("d_vc", [128, S], BF16, kind="ExternalOutput").ap()
        dbg["d_xnT"] = nc.dram_tensor("d_xnT", [128, ND * NOWN], BF16, kind="ExternalOutput").ap()

    sb = nc.alloc_sbuf_tensor
    scX = Scope()
    xnT = scX.sbuf(nc, "xnT", [128, ND * NOWN], BF16, side="right")
    kcT = sb("kcT", [128, S], BF16)
    vc = sb("vc", [128, S], BF16)
    s_cosk = sb("s_cosk", [64, S], BF16)
    s_srotk = sb("s_srotk", [64, S], BF16)
    s_cosq = scX.sbuf(nc, "s_cosq", [64, NOWN], BF16, side="right")
    s_srotq = scX.sbuf(nc, "s_srotq", [64, NOWN], BF16, side="right")
    s_idb = sb("s_idb", [128, 128], BF16)
    s_idf = sb("s_idf", [128, 128], F32)
    s_mask = sb("s_mask", [128, 128], F32)
    s_maskt = sb("s_maskt", [128, 128], F32)
    s_onesb = sb("s_onesb", [128, 128], BF16)
    s_ones = sb("s_ones", [1, 128], F32)
    wks = sb("wks", [128, H], F32)
    wvs = sb("wvs", [128, H], F32)
    gsv = sb("gsv", [128, NI], F32)
    usv = sb("usv", [128, NI], F32)
    dsv = sb("dsv", [128, ND], F32)

    sem = nc.alloc_semaphore
    sc1 = Scope()
    wkvb = sc1.sbuf(nc, "wkvb", [128, ND * KV], BF16)

    sc0 = Scope()
    whk = sc0.sbuf(nc, "whk", [128, H * 128], F32)
    scr44 = sc0.sbuf(nc, "scr44", [NI, 128], F32)
    scr44b = sc0.sbuf(nc, "scr44b", [NI, 128], F32)
    scr16 = sc0.sbuf(nc, "scr16", [ND, 128], F32)
    p0a = sc0.psum(nc, "p0a", [128, 512], F32)
    s_c0 = sem("s_c0")
    s_g0 = sem("s_g0")
    s_c1 = sem("s_c1")

    with nc.Block() as blk:
        @blk.sync
        def _(sync):
            for dst, src in (
                (s_cosk[:], cosk), (s_cosq[:], cosq),
                (s_srotk[:], srotk), (s_srotq[:], srotq),
                (s_idb[:], identbf), (s_idf[:], identf),
                (s_mask[:], maskf), (s_maskt[:], masktf), (s_ones[:], onesf),
                (scr44[:], gates), (scr44b[:], ups),
                (scr16[:], downs),
                (whk[:].rearrange("p (h k) -> p h k", k=128),
                 wkh.rearrange("h l k -> l h k")),
            ):
                sync.dma_start(dst, src).then_inc(s_c0, 16)

        @blk.gpsimd
        def _(gp):
            gp.memset(s_onesb[:], 1.0)
            gp.dma_start(wkvb[:].rearrange("p (j k) -> p j k", k=KV),
                         wkvT.rearrange("(j l) k -> l j k", l=128)).then_inc(s_g0, 16)

        @blk.vector
        def _(v):
            v.wait_ge(s_c0, 13 * 16)
            v.wait_ge(s_g0, 16)
            for h in range(H):
                v.reduce_sum(wks[:, h:h + 1], whk[:, h * 128:(h + 1) * 128], axis=AX.X)
            v.nop().then_inc(s_c1, 1)

        @blk.tensor
        def _(t):
            t.wait_ge(s_c0, 13 * 16)
            t.transpose(p0a[0:128, 0:NI], scr44[:], s_idf[0:NI, 0:NI])
            t.transpose(p0a[0:128, 64:64 + NI], scr44b[:], s_idf[0:NI, 0:NI])
            t.transpose(p0a[0:128, 128:128 + ND], scr16[:], s_idf[0:ND, 0:ND]).then_inc(s_c1, 1)

        @blk.scalar
        def _(s):
            load_act_set(nc, s, SET_LNEXP)
            s.wait_ge(s_c1, 2)
            s.copy(gsv[:], p0a[0:128, 0:NI])
            s.copy(usv[:], p0a[0:128, 64:64 + NI])
            s.copy(dsv[:], p0a[0:128, 128:128 + ND])

    with nc.Block() as blk:
        @blk.sync
        def _(sync):
            sync.dma_start(whk[:].rearrange("p (h k) -> p h k", k=128),
                           wvh.rearrange("h l k -> l h k")).then_inc(s_c0, 16)

        @blk.vector
        def _(v):
            v.wait_ge(s_c0, 14 * 16)
            for h in range(H):
                ins = v.reduce_sum(wvs[:, h:h + 1], whk[:, h * 128:(h + 1) * 128], axis=AX.X)
            ins.then_inc(s_c1, 1)

    sc0.close()

    xbuf = [sc1.sbuf(nc, f"xbuf{i}", [128, D], F32) for i in range(3)]
    xsq = sc1.sbuf(nc, "xsq", [128, D], BF16)
    xnb = [sc1.sbuf(nc, f"xnb{i}", [128, D], BF16) for i in range(2)]
    xns = [sc1.sbuf(nc, f"xns{i}", [128, ND * 128], BF16) for i in range(2)]
    kctile = [sc1.sbuf(nc, f"kctile{i}", [128, 128], BF16) for i in range(2)]
    sm_sumx = sc1.sbuf(nc, "sm_sumx", [128, 2], F32)
    sm_sumsq = sc1.sbuf(nc, "sm_sumsq", [128, 2], F32)
    sm_mean = sc1.sbuf(nc, "sm_mean", [128, 2], F32)
    sm_msqe = sc1.sbuf(nc, "sm_msqe", [128, 2], F32)
    sm_varp = sc1.sbuf(nc, "sm_varp", [128, 2], F32)
    sm_lnv = sc1.sbuf(nc, "sm_lnv", [128, 2], F32)
    sm_rstd = sc1.sbuf(nc, "sm_rstd", [128, 2], F32)
    ptr = [sc1.psum(nc, f"ptr{i}", [128, 1024], BF16) for i in range(4)]
    pkv = [sc1.psum(nc, f"pkv{i}", [128, 512], F32) for i in range(2)]
    pkt = [sc1.psum(nc, f"pkt{i}", [128, 1024], BF16) for i in range(2)]

    s_x = [sem(f"s_x{i}") for i in range(3)]
    s_sq = sem("s_sq")
    s_var = sem("s_var")
    s_rstd = sem("s_rstd")
    s_xnb = sem("s_xnb")
    s_tr = sem("s_tr")
    s_xe = sem("s_xe")
    s_kv = sem("s_kv")
    s_kce = sem("s_kce")
    s_vce = sem("s_vce")
    s_kt = sem("s_kt")
    s_kte = sem("s_kte")
    s_vch = sem("s_vch")
    s_sch = sem("s_sch")

    NTOT = NT_ROW + NT_OWN

    with nc.Block() as blk:
        @blk.sync
        def _(sync):
            for ti in range(NTOT):
                if ti >= 3:
                    sync.wait_ge(s_sq, ti - 2)
                    sync.wait_ge(s_xnb, ti - 2)
                src = x_row[ti * 128:(ti + 1) * 128, :] if ti < NT_ROW else \
                    x_own[(ti - NT_ROW) * 128:(ti - NT_ROW + 1) * 128, :]
                sync.dma_start(xbuf[ti % 3][:], src).then_inc(s_x[ti % 3], 16)

        @blk.vector
        def _(v):
            nch = 0
            for tt in range(NTOT + 1):
                if tt < NTOT:
                    ti, p = tt, tt % 2
                    v.wait_ge(s_x[ti % 3], 16 * (ti // 3 + 1))
                    v.reduce_sum(sm_sumx[:, p:p + 1], xbuf[ti % 3][:], axis=AX.X).then_inc(s_vch, 1); nch += 1
                    v.wait_ge(s_vch, nch)
                    v.tensor_scalar(sm_mean[:, p:p + 1], sm_sumx[:, p:p + 1], 1.0 / D,
                                    None, ALU.mult).then_inc(s_vch, 1); nch += 1
                    v.wait_ge(s_vch, nch)
                    v.tensor_scalar(sm_msqe[:, p:p + 1], sm_mean[:, p:p + 1],
                                    sm_mean[:, p:p + 1], EPS, ALU.mult, ALU.subtract).then_inc(s_vch, 1); nch += 1
                    v.wait_ge(s_sq, ti + 1)
                    v.wait_ge(s_vch, nch)
                    v.tensor_scalar(sm_varp[:, p:p + 1], sm_sumsq[:, p:p + 1], 1.0 / D,
                                    sm_msqe[:, p:p + 1], ALU.mult, ALU.subtract).then_inc(s_var, 1)
                if tt >= 1:
                    ti, p = tt - 1, (tt - 1) % 2
                    v.wait_ge(s_rstd, ti + 1)
                    if ti >= 2:
                        v.wait_ge(s_tr, 2 * (ti - 1))
                    v.tensor_scalar(xnb[p][:], xbuf[ti % 3][:], sm_mean[:, p:p + 1],
                                    sm_rstd[:, p:p + 1], ALU.subtract, ALU.mult).then_inc(s_xnb, 1)

        @blk.scalar
        def _(s):
            for tt in range(NTOT + 1):
                if tt < NTOT:
                    ti, p = tt, tt % 2
                    s.wait_ge(s_x[ti % 3], 16 * (ti // 3 + 1))
                    s.activation(xsq[:], xbuf[ti % 3][:], AF.Square,
                                 accum_out=sm_sumsq[:, p:p + 1]).then_inc(s_sq, 1)
                    s.wait_ge(s_var, ti + 1)
                    s.activation(sm_lnv[:, p:p + 1], sm_varp[:, p:p + 1], AF.Ln).then_inc(s_sch, 1)
                    s.wait_ge(s_sch, ti + 1)
                    s.activation(sm_rstd[:, p:p + 1], sm_lnv[:, p:p + 1], AF.Exp,
                                 scale=-0.5).then_inc(s_rstd, 1)
                if tt >= 1:
                    ti, p = tt - 1, (tt - 1) % 2
                    s.wait_ge(s_tr, 2 * ti + 1)
                    if ti < NT_ROW:
                        s.copy(xns[p][:, 0:1024], ptr[(2 * ti) % 4][:]).then_inc(s_xe, 1)
                        s.wait_ge(s_tr, 2 * ti + 2)
                        s.copy(xns[p][:, 1024:2048], ptr[(2 * ti + 1) % 4][:]).then_inc(s_xe, 1)
                        s.wait_ge(s_kv, ti + 1)
                        s.copy(kctile[p][:], pkv[p][:, 0:128]).then_inc(s_kce, 1)
                        s.copy(vc[:, ti * 128:(ti + 1) * 128], pkv[p][:, 128:256]).then_inc(s_vce, 1)
                        s.wait_ge(s_kt, ti + 1)
                        s.copy(kcT[:, ti * 128:(ti + 1) * 128], pkt[p][:, 0:128]).then_inc(s_kte, 1)
                    else:
                        o = ti - NT_ROW
                        xv = xnT[:].rearrange("p (j t) -> p j t", t=NOWN)
                        s.copy(xv[:, 0:8, o * 128:(o + 1) * 128],
                               ptr[(2 * ti) % 4][:].rearrange("p (j t) -> p j t", t=128)).then_inc(s_xe, 1)
                        s.wait_ge(s_tr, 2 * ti + 2)
                        s.copy(xv[:, 8:16, o * 128:(o + 1) * 128],
                               ptr[(2 * ti + 1) % 4][:].rearrange("p (j t) -> p j t", t=128)).then_inc(s_xe, 1)

        @blk.tensor
        def _(t):
            for tt in range(1, NTOT + 1):
                ti, p = tt - 1, (tt - 1) % 2
                t.wait_ge(s_xnb, ti + 1)
                if ti >= 2:
                    t.wait_ge(s_xe, 2 * (ti - 1))
                for half8 in range(2):
                    bank = ptr[(2 * ti + half8) % 4]
                    for j in range(8):
                        jj = half8 * 8 + j
                        ins = t.transpose(bank[:, j * 128:(j + 1) * 128],
                                          xnb[p][:, jj * 128:(jj + 1) * 128], s_idb[:])
                        if j == 7:
                            ins.then_inc(s_tr, 1)
                if ti < NT_ROW:
                    t.wait_ge(s_xe, 2 * (ti + 1))
                    if ti >= 2:
                        t.wait_ge(s_kce, ti - 1)
                        t.wait_ge(s_vce, ti - 1)
                    for j in range(ND):
                        ins = t.matmul(pkv[p][:, 0:KV], xns[p][:, j * 128:(j + 1) * 128],
                                       wkvb[:, j * KV:(j + 1) * KV],
                                       start=(j == 0), stop=(j == ND - 1))
                    ins.then_inc(s_kv, 1)
                    t.wait_ge(s_kce, ti + 1)
                    if ti >= 2:
                        t.wait_ge(s_kte, ti - 1)
                    t.transpose(pkt[p][:, 0:128], kctile[p][:], s_idb[:]).then_inc(s_kt, 1)

    sc1.close()

    if DEBUG >= 1:
        s_dbg = sem("s_dbg")
        with nc.Block() as blk:
            @blk.sync
            def _(sync):
                sync.dma_start(dbg["d_kcT"], kcT[:]).then_inc(s_dbg, 16)
                sync.dma_start(dbg["d_vc"], vc[:]).then_inc(s_dbg, 16)
                sync.dma_start(dbg["d_xnT"], xnT[:]).then_inc(s_dbg, 16)
                sync.wait_ge(s_dbg, 48)
    if UPTO <= 1:
        return nc

    NTK = NOWN // 512
    sc2a = Scope()
    gwb = [sc2a.sbuf(nc, f"gwb{i}", [128, ND * 128], BF16) for i in range(2)]
    uwb = [sc2a.sbuf(nc, f"uwb{i}", [128, ND * 128], BF16) for i in range(2)]
    hgb = [sc2a.sbuf(nc, f"hgb{i}", [128, 512], BF16) for i in range(2)]
    ubf = [sc2a.sbuf(nc, f"ubf{i}", [128, 512], BF16) for i in range(2)]
    hdnb = [sc2a.sbuf(nc, f"hdnb{i}", [128, 512], BF16) for i in range(2)]
    pg = [sc2a.psum(nc, f"pg{i}", [128, 512], F32) for i in range(2)]
    pu = [sc2a.psum(nc, f"pu{i}", [128, 512], F32) for i in range(2)]

    s_gw = [sem("s_gw0"), sem("s_gw1")]
    s_uw = [sem("s_uw0"), sem("s_uw1")]
    s_gmm = sem("s_gmm")
    s_umm = sem("s_umm")
    s_sil = sem("s_sil")
    s_hdn = sem("s_hdn")
    s_hsp = [sem("s_hsp0"), sem("s_hsp1")]
    s_vc2 = sem("s_vc2")



    with nc.Block() as blk:
        @blk.gpsimd
        def _(gp):
            for i in range(NI):
                if i >= 2:
                    gp.wait_ge(s_gmm, NTK * (i - 1))
                    gp.wait_ge(s_umm, NTK * (i - 1))
                gp.dma_start(gwb[i % 2][:], gateT[i]).then_inc(s_gw[i % 2], 16)
                gp.dma_start(uwb[i % 2][:], upT[i]).then_inc(s_uw[i % 2], 16)

        @blk.tensor
        def _(t):
            for i in range(NI):
                t.wait_ge(s_gw[i % 2], 16 * (i // 2 + 1))
                t.wait_ge(s_uw[i % 2], 16 * (i // 2 + 1))
                for tk in range(NTK):
                    seq = i * NTK + tk
                    if seq >= 2:
                        t.wait_ge(s_sil, seq - 1)
                    for j in range(ND):
                        ins = t.matmul(pg[seq % 2][:], gwb[i % 2][:, j * 128:(j + 1) * 128],
                                       xnT[:, j * NOWN + tk * 512:j * NOWN + (tk + 1) * 512],
                                       start=(j == 0), stop=(j == ND - 1))
                    ins.then_inc(s_gmm, 1)
                    if seq >= 2:
                        t.wait_ge(s_hdn, seq - 1)
                    for j in range(ND):
                        ins = t.matmul(pu[seq % 2][:], uwb[i % 2][:, j * 128:(j + 1) * 128],
                                       xnT[:, j * NOWN + tk * 512:j * NOWN + (tk + 1) * 512],
                                       start=(j == 0), stop=(j == ND - 1))
                    ins.then_inc(s_umm, 1)

        @blk.scalar
        def _(s):
            load_act_set(nc, s, SET_SILU)
            for i in range(NI):
                for tk in range(NTK):
                    seq = i * NTK + tk
                    s.wait_ge(s_gmm, seq + 1)
                    if seq >= 2:
                        s.wait_ge(s_hdn, seq - 1)
                    s.activation(hgb[seq % 2][:], pg[seq % 2][:], AF.Silu,
                                 scale=gsv[:, i:i + 1]).then_inc(s_sil, 1)

        @blk.vector
        def _(v):
            nv = 0
            for i in range(NI):
                for tk in range(NTK):
                    seq = i * NTK + tk
                    v.wait_ge(s_umm, seq + 1)
                    if seq >= 2:
                        v.wait_ge(s_hsp[seq % 2], 16 * (seq // 2))
                    v.tensor_scalar(ubf[seq % 2][:], pu[seq % 2][:], usv[:, i:i + 1],
                                    None, ALU.mult).then_inc(s_vc2, 1); nv += 1
                    v.wait_ge(s_sil, seq + 1)
                    v.wait_ge(s_vc2, nv)
                    v.tensor_mul(hdnb[seq % 2][:], hgb[seq % 2][:],
                                 ubf[seq % 2][:]).then_inc(s_hdn, 1)

        @blk.sync
        def _(sync):
            for i in range(NI):
                for tk in range(NTK):
                    seq = i * NTK + tk
                    sync.wait_ge(s_hdn, seq + 1)
                    sync.dma_start(hdn_spill[i * 128:(i + 1) * 128, tk * 512:(tk + 1) * 512],
                                   hdnb[seq % 2][:]).then_inc(s_hsp[seq % 2], 16)

    sc2a.close()
    if UPTO <= 2:
        return nc

    sc2b = Scope()
    hdnT = sc2b.sbuf(nc, "hdnT", [128, NI * 512], BF16)
    dstrip = [sc2b.sbuf(nc, f"dstrip{i}", [128, NI * 128], BF16) for i in range(4)]
    ysb = [sc2b.sbuf(nc, f"ysb{i}", [128, 512], F32) for i in range(2)]
    xtb = [sc2b.sbuf(nc, f"xtb{i}", [128, 512], F32) for i in range(2)]
    py = [sc2b.psum(nc, f"py{i}", [128, 512], F32) for i in range(2)]

    s_hl = sem("s_hl")
    s_dw = [sem(f"s_dw{i}") for i in range(4)]
    s_dmm = sem("s_dmm")
    s_yv = sem("s_yv")
    s_yst = [sem("s_yst0"), sem("s_yst1")]
    s_xl = [sem("s_xl0"), sem("s_xl1")]
    s_vb2 = sem("s_vb2")

    spill3 = hdn_spill.rearrange("(j l) k -> l j k", l=128)

    with nc.Block() as blk:
        @blk.sync
        def _(sync):
            nsp = NI * NTK
            sync.wait_ge(s_hsp[0], 16 * ((nsp + 1) // 2))
            sync.wait_ge(s_hsp[1], 16 * (nsp // 2))
            for tc in range(NTK):
                if tc >= 1:
                    sync.wait_ge(s_dmm, ND * tc)
                for part in range(4):
                    j0 = (NI * part) // 4
                    j1 = (NI * (part + 1)) // 4
                    if j1 > j0:
                        sync.dma_start(
                            hdnT[:].rearrange("p (j k) -> p j k", k=512)[:, j0:j1, :],
                            spill3[:, j0:j1, tc * 512:(tc + 1) * 512]).then_inc(s_hl, 16)
                for hb in range(ND):
                    seq = tc * ND + hb
                    if seq >= 2:
                        sync.wait_ge(s_yv, seq - 1)
                    sync.dma_start(xtb[seq % 2][:],
                                   xT_own[hb * 128:(hb + 1) * 128, tc * 512:(tc + 1) * 512]
                                   ).then_inc(s_xl[seq % 2], 16)
                    sync.wait_ge(s_yv, seq + 1)
                    sync.dma_start(out[hb * 128:(hb + 1) * 128, tc * 512:(tc + 1) * 512],
                                   ysb[seq % 2][:]).then_inc(s_yst[seq % 2], 16)

        @blk.gpsimd
        def _(gp):
            for tc in range(NTK):
                for hb in range(ND):
                    seq = tc * ND + hb
                    if seq >= 4:
                        gp.wait_ge(s_dmm, seq - 3)
                    gp.dma_start(dstrip[seq % 4][:], downT[hb]).then_inc(s_dw[seq % 4], 16)
            gp.wait_ge(s_yst[0], 16 * ((NTK * ND + 1) // 2))
            gp.wait_ge(s_yst[1], 16 * ((NTK * ND) // 2))

        @blk.tensor
        def _(t):
            for tc in range(NTK):
                t.wait_ge(s_hl, 16 * 4 * (tc + 1))
                for hb in range(ND):
                    seq = tc * ND + hb
                    t.wait_ge(s_dw[seq % 4], 16 * (seq // 4 + 1))
                    if seq >= 2:
                        t.wait_ge(s_yv, seq - 1)
                    for jb in range(NI):
                        ins = t.matmul(py[seq % 2][:], dstrip[seq % 4][:, jb * 128:(jb + 1) * 128],
                                       hdnT[:, jb * 512:(jb + 1) * 512],
                                       start=(jb == 0), stop=(jb == NI - 1))
                    ins.then_inc(s_dmm, 1)

        @blk.vector
        def _(v):
            for tc in range(NTK):
                for hb in range(ND):
                    seq = tc * ND + hb
                    v.wait_ge(s_dmm, seq + 1)
                    if seq >= 2:
                        v.wait_ge(s_yst[seq % 2], 16 * (seq // 2))
                    v.tensor_scalar(ysb[seq % 2][:], py[seq % 2][:], dsv[:, hb:hb + 1],
                                    None, ALU.mult).then_inc(s_vb2, 1)
                    v.wait_ge(s_vb2, seq + 1)
                    v.wait_ge(s_xl[seq % 2], 16 * (seq // 2 + 1))
                    v.tensor_add(ysb[seq % 2][:], ysb[seq % 2][:],
                                 xtb[seq % 2][:]).then_inc(s_yv, 1)

    sc2b.close()
    if UPTO <= 3:
        return nc

    scheds = (sched_for(0), sched_for(1))
    NSLOT = len(HALF_CHUNKS[0])
    PQ = max(L for sch in scheds for L, _ in sch)
    NQC = NOWN // 512

    scQ = Scope()
    QT = scQ.sbuf(nc, "QT", [128, H * NOWN], BF16)
    wvball = scQ.sbuf(nc, "wvball", [128, H * 128], BF16)

    sc3 = Scope()
    wqb = [sc3.sbuf(nc, f"wqb{i}", [128, ND * 128], BF16) for i in range(2)]
    qta = sc3.sbuf(nc, "qta", [32, 512], F32)
    qtb = sc3.sbuf(nc, "qtb", [32, 512], F32)
    qtc = sc3.sbuf(nc, "qtc", [32, 512], F32)
    qtd = sc3.sbuf(nc, "qtd", [32, 512], F32)
    vrowq = sc3.sbuf(nc, "vrowq", [1, 128], F32)
    pq3 = [sc3.psum(nc, f"pq3{i}", [128, 512], F32) for i in range(2)]
    pbw = sc3.psum(nc, "pbw", [128, 256], F32)

    s_wq = [sem("s_wq0"), sem("s_wq1")]
    s_qmm = sem("s_qmm")
    s_qe = sem("s_qe")
    s_qrope = sem("s_qrope")
    s_v3 = sem("s_v3")
    s_bt = sem("s_bt")
    s_bv = sem("s_bv")
    s_bm = sem("s_bm")
    s_bw = sem("s_bw")

    with nc.Block() as blk:
        @blk.gpsimd
        def _(gp):
            for h in range(H):
                if h >= 2:
                    gp.wait_ge(s_qmm, NQC * (h - 1))
                gp.dma_start(wqb[h % 2][:], wqT[h]).then_inc(s_wq[h % 2], 16)

        @blk.tensor
        def _(t):
            for h in range(H):
                t.wait_ge(s_wq[h % 2], 16 * (h // 2 + 1))
                for nk in range(NQC):
                    seq = h * NQC + nk
                    if seq >= 2:
                        t.wait_ge(s_qe, seq - 1)
                        t.wait_ge(s_qrope, seq - 1)
                    for j in range(ND):
                        ins = t.matmul(pq3[seq % 2][:], wqb[h % 2][:, j * 128:(j + 1) * 128],
                                       xnT[:, j * NOWN + nk * 512:j * NOWN + (nk + 1) * 512],
                                       start=(j == 0), stop=(j == ND - 1))
                    ins.then_inc(s_qmm, 1)
            t.wait_ge(s_c1, 3)
            for h in range(H):
                if h >= 1:
                    t.wait_ge(s_bv, h)
                    t.wait_ge(s_bw, h)
                t.transpose(pbw[0:1, 0:128], wvs[:, h:h + 1], s_idf[:]).then_inc(s_bt, 1)
                t.wait_ge(s_bv, h + 1)
                if h >= 1:
                    t.wait_ge(s_bw, h)
                t.matmul(pbw[:, 128:256], s_ones[0:1, 0:128], vrowq[:],
                         start=True, stop=True).then_inc(s_bm, 1)

        @blk.scalar
        def _(sx):
            for h in range(H):
                for nk in range(NQC):
                    seq = h * NQC + nk
                    sx.wait_ge(s_qmm, seq + 1)
                    sx.activation(QT[64:128, h * NOWN + nk * 512:h * NOWN + (nk + 1) * 512],
                                  pq3[seq % 2][64:128, :], AF.Copy,
                                  scale=SCL).then_inc(s_qe, 1)
            for h in range(H):
                sx.wait_ge(s_bt, h + 1)
                sx.copy(vrowq[:], pbw[0:1, 0:128]).then_inc(s_bv, 1)
                sx.wait_ge(s_bm, h + 1)
                sx.copy(wvball[:, h * 128:(h + 1) * 128], pbw[:, 128:256]).then_inc(s_bw, 1)

        @blk.vector
        def _(v):
            nch = 0
            for h in range(H):
                for nk in range(NQC):
                    seq = h * NQC + nk
                    v.wait_ge(s_qmm, seq + 1)
                    v.wait_ge(s_v3, nch)
                    v.wait_ge(s_qrope, seq)
                    pa = pq3[seq % 2]
                    ca, sa = pa[0:32, :], pa[32:64, :]
                    ck = s_cosq[:, nk * 512:(nk + 1) * 512]
                    rq = s_srotq[:, nk * 512:(nk + 1) * 512]
                    qcol = h * NOWN + nk * 512
                    v.tensor_mul(qta[:], ca, ck[0:32, :]).then_inc(s_v3, 1); nch += 1
                    v.tensor_mul(qtb[:], sa, rq[32:64, :]).then_inc(s_v3, 1); nch += 1
                    v.tensor_mul(qtc[:], sa, ck[32:64, :]).then_inc(s_v3, 1); nch += 1
                    v.tensor_mul(qtd[:], ca, rq[0:32, :]).then_inc(s_v3, 1); nch += 1
                    v.wait_ge(s_v3, nch)
                    v.tensor_sub(QT[0:32, qcol:qcol + 512], qta[:], qtb[:]).then_inc(s_v3, 1)
                    nch += 1
                    v.tensor_add(QT[32:64, qcol:qcol + 512], qtc[:], qtd[:]).then_inc(s_qrope, 1)

    sc3.close()
    scX.close()

    sc4 = Scope()
    kbuf = [sc4.sbuf(nc, f"kbuf{i}", [128, S], BF16) for i in range(2)]
    vbuf = [sc4.sbuf(nc, f"vbuf{i}", [128, S], BF16) for i in range(2)]
    ktmp = sc4.sbuf(nc, "ktmp", [64, S], BF16)
    pts = [sc4.sbuf(nc, f"pts{i}", [128, 512], BF16) for i in range(3)]
    atst = [sc4.sbuf(nc, f"atst{i}", [128, 512], BF16) for i in range(2)]
    mxc = sc4.sbuf(nc, "mxc", [128, 8], F32)
    nmq = sc4.sbuf(nc, "nmq", [128, 8], F32)
    nmrow = [sc4.sbuf(nc, f"nmrow{i}", [1, 512], BF16) for i in range(2)]
    srow = sc4.sbuf(nc, "srow", [1, 512], F32)
    rrow = sc4.sbuf(nc, "rrow", [1, 512], BF16)
    rbsb = sc4.sbuf(nc, "rbsb", [128, 512], BF16)

    sp1 = [sc4.psum(nc, f"sp1{i}", [128, 512], F32) for i in range(3)]
    sp2 = [sc4.psum(nc, f"sp2{i}", [128, 512], F32) for i in range(2)]
    pvp = sc4.psum(nc, "pvp", [128, 512], F32)
    prow = sc4.psum(nc, "prow", [128, 512], F32)
    rbank = sc4.psum(nc, "rbank", [128, 512], F32)

    s_m1 = sem("s_m1")
    s_mx = sem("s_mx")
    s_cmb = sem("s_cmb")
    s_prt = sem("s_prt")
    s_nr = sem("s_nr")
    s_m2 = sem("s_m2")
    s_msk = sem("s_msk")
    s_ex = sem("s_ex")
    s_z = sem("s_z")
    s_pv = sem("s_pv")
    s_sg = sem("s_sg")
    s_rr = sem("s_rr")
    s_rb = sem("s_rb")
    s_rbe = sem("s_rbe")
    s_ae = sem("s_ae")
    s_asp = [sem("s_asp0"), sem("s_asp1")]
    s_kvb = sem("s_kvb")
    s_kb2 = sem("s_kb2")
    s_v4 = sem("s_v4")

    syncq = nc.sync
    for idx in (range(P4_NH * NSLOT) if P4_STAGE >= 3 else ()):
        syncq.wait_ge(s_ae, idx + 1)
        h, k = idx // NSLOT, idx % NSLOT
        syncq.dma_start(at_spill[h * 128:(h + 1) * 128, k * 512:(k + 1) * 512],
                        atst[idx % 2][:]).then_inc(s_asp[idx % 2], 16)

    def emit_p4(sched, cnt):
        t, v, s = nc.tensor, nc.vector, nc.scalar

        def vch(ins):
            ins.then_inc(s_v4, 1)
            cnt["v4"] += 1
            return ins

        def vwait():
            v.wait_ge(s_v4, cnt["v4"])

        a_guard = [[], [], []]
        b_guard = [[], []]
        load_act_set(nc, s, SET_LNEXP)
        if DEBUG >= 3:
            v.memset(mxc[:], 0.0)
            vch(v.memset(nmq[:], 0.0))

        def emit_kv(g, gm1, gm2):
            gp_ = g % 2
            if g >= 1:
                v.wait_ge(s_m1, gm1)
                v.wait_ge(s_m2, gm2)
            vch(v.tensor_scalar(kbuf[gp_][:], kcT[:], wks[:, g:g + 1], None, ALU.mult))
            yield
            vwait()
            vch(v.tensor_mul(ktmp[0:32, :], kbuf[gp_][32:64, :], s_srotk[32:64, :]))
            yield
            vch(v.tensor_mul(ktmp[32:64, :], kbuf[gp_][0:32, :], s_srotk[0:32, :]))
            yield
            vwait()
            vch(v.tensor_mul(kbuf[gp_][0:32, :], kbuf[gp_][0:32, :], s_cosk[0:32, :]))
            yield
            vch(v.tensor_mul(kbuf[gp_][32:64, :], kbuf[gp_][32:64, :], s_cosk[32:64, :]))
            yield
            vwait()
            vch(v.tensor_sub(kbuf[gp_][0:32, :], kbuf[gp_][0:32, :], ktmp[0:32, :]))
            yield
            v.tensor_add(kbuf[gp_][32:64, :], kbuf[gp_][32:64, :],
                         ktmp[32:64, :]).then_inc(s_kb2, 1)

        def emit_v(g, gpv):
            gp_ = g % 2
            if g == 0:
                v.wait_ge(s_bw, H)
            if g >= 2:
                v.wait_ge(s_pv, gpv)
            wvb_b = wvball[:, g * 128:(g + 1) * 128].rearrange(
                "p (j k) -> p j k", j=1).broadcast_to((128, NT_ROW, 128))
            v.tensor_tensor(vbuf[gp_][:].rearrange("p (j k) -> p j k", k=128),
                            vc[:].rearrange("p (j k) -> p j k", k=128),
                            wvb_b, ALU.mult).then_inc(s_kvb, 1)

        def emit_A(h, k, sp, L, tqs):
            hp = h % 2
            qs0 = h * NOWN + k * 512
            for q in range(4):
                tq = tqs[q]
                nch = (tq + 511) // 512
                v.wait_ge(s_cmb, cnt["cmb"])
                for c in range(nch):
                    w = min(512, tq - c * 512)
                    ar = cnt["m1"] % 3
                    for smv in a_guard[ar]:
                        t.wait_ge(*smv)
                    a_guard[ar] = []
                    t.matmul(sp1[ar][:, 0:w], QT[:, qs0 + q * 128:qs0 + (q + 1) * 128],
                             kbuf[hp][:, c * 512:c * 512 + w],
                             start=True, stop=True).then_inc(s_m1, 1)
                    cnt["m1"] += 1
                    v.wait_ge(s_m1, cnt["m1"])
                    if c == nch - 1:
                        vch(v.tensor_tensor(sp1[ar][:, w - 128:w], sp1[ar][:, w - 128:w],
                                            s_mask[:], ALU.add))
                        vwait()
                    v.reduce_max(mxc[:, c:c + 1], sp1[ar][:, 0:w],
                                 axis=AX.X).then_inc(s_mx, 1)
                    cnt["mx"] += 1
                    a_guard[ar] = [(s_mx, cnt["mx"])]
                    yield
                v.wait_ge(s_prt, cnt["prt_par"][sp])
                v.wait_ge(s_mx, cnt["mx"])
                v.reduce_max(nmq[:, sp * 4 + q:sp * 4 + q + 1], mxc[:, 0:nch],
                             axis=AX.X, negate=True).then_inc(s_cmb, 1)
                cnt["cmb"] += 1
                t.wait_ge(s_cmb, cnt["cmb"])
                if q == 0:
                    t.wait_ge(s_nr, cnt["nr"])
                t.transpose(prow[0:1, q * 128:(q + 1) * 128],
                            nmq[:, sp * 4 + q:sp * 4 + q + 1],
                            s_idf[:]).then_inc(s_prt, 1)
                cnt["prt"] += 1
                yield
            cnt["prt_par"][sp] = cnt["prt"]

        def emit_sigpv(hp, ntb, tb, ex_cnt, z_cnt, g):
            ptsb = pts[g % 3]
            t.wait_ge(s_ex, ex_cnt)
            if z_cnt is not None:
                t.wait_ge(s_z, z_cnt)
            if tb == 0:
                t.wait_ge(s_rbe, cnt["rbe"])
                t.wait_ge(s_ae, cnt["ae"])
            ins = t.matmul(rbank[0:1, 0:512], s_onesb[:, 0:1], ptsb[:],
                           start=(tb == 0), stop=(tb == ntb - 1))
            if tb == ntb - 1:
                ins.then_inc(s_sg, 1)
                cnt["sg"] += 1
            t.matmul(pvp[:], vbuf[hp][:, tb * 128:(tb + 1) * 128], ptsb[:],
                     start=(tb == 0), stop=(tb == ntb - 1)).then_inc(s_pv, 1)
            cnt["pv"] += 1
            cnt["pv_hist"][g] = cnt["pv"]

        def emit_CD(h, k, sp, L, tqs, prtv):
            hp = h % 2
            ntb = L // 128
            qs0 = h * NOWN + k * 512
            s.wait_ge(s_prt, prtv)
            s.wait_ge(s_m2, cnt["m2_sl"][sp])
            s.copy(nmrow[sp][:], prow[0:1, 0:512]).then_inc(s_nr, 1)
            cnt["nr"] += 1
            if k == 0:
                t.wait_ge(s_kvb, h + 1)
            t.wait_ge(s_nr, cnt["nr"])
            pend = None
            for tb in range(ntb):
                g2 = cnt["m2"] % 2
                for smv in b_guard[g2]:
                    t.wait_ge(*smv)
                b_guard[g2] = []
                t.matmul(sp2[g2][:], kbuf[hp][:, tb * 128:(tb + 1) * 128],
                         QT[:, qs0:qs0 + 512], start=True, stop=False)
                t.matmul(sp2[g2][:], s_onesb[0:1, 0:128], nmrow[sp][:],
                         start=False, stop=True).then_inc(s_m2, 1)
                cnt["m2"] += 1
                dq = tb - (ntb - 4)
                if 0 <= dq < 4:
                    v.wait_ge(s_m2, cnt["m2"])
                    v.tensor_tensor(sp2[g2][:, dq * 128:(dq + 1) * 128],
                                    sp2[g2][:, dq * 128:(dq + 1) * 128],
                                    s_maskt[:], ALU.add).then_inc(s_msk, 1)
                    cnt["msk"] += 1
                g = cnt["gtb"]
                cnt["gtb"] += 1
                ptsb = pts[g % 3]
                s.wait_ge(s_m2, cnt["m2"])
                if 0 <= dq < 4:
                    s.wait_ge(s_msk, cnt["msk"])
                if g >= 3:
                    s.wait_ge(s_pv, cnt["pv_hist"][g - 3])
                s.activation(ptsb[:], sp2[g2][:], AF.Exp).then_inc(s_ex, 1)
                cnt["ex"] += 1
                b_guard[g2] = [(s_ex, cnt["ex"])]
                zq = tb - (ntb - 3)
                if zq >= 0:
                    v.wait_ge(s_ex, cnt["ex"])
                    nz = min(zq, 2) + 1
                    for q in range(nz):
                        ins = v.memset(ptsb[:, q * 128:(q + 1) * 128], 0.0)
                        if q < nz - 1:
                            ins.then_inc(s_v4, 1)
                            cnt["v4"] += 1
                    ins.then_inc(s_z, 1)
                    cnt["z"] += 1
                if pend is not None:
                    emit_sigpv(hp, ntb, *pend)
                pend = (tb, cnt["ex"], cnt["z"] if zq >= 0 else None, g)
                yield
            emit_sigpv(hp, ntb, *pend)
            s.wait_ge(s_sg, cnt["sg"])
            s.activation(srow[:], rbank[0:1, 0:512], AF.Ln).then_inc(s_rr, 1)
            cnt["rr"] += 1
            s.wait_ge(s_rr, cnt["rr"])
            s.activation(rrow[:], srow[:], AF.Exp, scale=-1.0).then_inc(s_rr, 1)
            cnt["rr"] += 1
            t.wait_ge(s_rr, cnt["rr"])
            t.matmul(rbank[:, 0:512], s_onesb[0:1, 0:128], rrow[:],
                     start=True, stop=True).then_inc(s_rb, 1)
            cnt["rb"] += 1
            s.wait_ge(s_rb, cnt["rb"])
            s.copy(rbsb[:], rbank[:, 0:512]).then_inc(s_rbe, 1)
            cnt["rbe"] += 1
            ai = cnt["ae"]
            v.wait_ge(s_rbe, cnt["rbe"])
            v.wait_ge(s_pv, cnt["pv"])
            if ai >= 2:
                v.wait_ge(s_asp[ai % 2], 16 * (ai // 2))
            v.tensor_tensor(atst[ai % 2][:], pvp[:], rbsb[:], ALU.mult).then_inc(s_ae, 1)
            cnt["ae"] += 1
            cnt["m2_sl"][sp] = cnt["m2"]

        for _ in emit_kv(0, 0, 0):
            pass
        emit_v(0, 0)
        if P4_STAGE < 3:
            for g in range(1, P4_NH):
                for _ in emit_kv(g, 0, 0):
                    pass
                emit_v(g, 0)
            return

        state = dict(pending=None, kvgen=None, vhead=0)

        def step_bg():
            if state["kvgen"] is not None:
                try:
                    next(state["kvgen"])
                except StopIteration:
                    state["kvgen"] = None
                    emit_v(state["vhead"], cnt["pv"])

        def step_pend():
            if state["pending"] is None:
                return False
            try:
                next(state["pending"])
                return True
            except StopIteration:
                state["pending"] = None
                return False

        kv_created = 0
        for h in range(P4_NH):
            while state["kvgen"] is not None:
                step_bg()
            t.wait_ge(s_kb2, h + 1)
            for k in range(NSLOT):
                L, tqs = sched[k]
                sp = cnt["sl"] % 2
                cnt["sl"] += 1
                gA = emit_A(h, k, sp, L, tqs)
                for _ in gA:
                    step_pend()
                    step_bg()
                while step_pend():
                    step_bg()
                if k == 0 and h + 1 < P4_NH and kv_created == h:
                    state["kvgen"] = emit_kv(h + 1, cnt["m1"], cnt["m2"])
                    state["vhead"] = h + 1
                    kv_created = h + 1
                state["pending"] = emit_CD(h, k, sp, L, tqs, cnt["prt"])
        while state["pending"] is not None or state["kvgen"] is not None:
            step_pend()
            step_bg()

    def fresh_cnt():
        return dict(m1=0, mx=0, cmb=0, prt=0, nr=0, m2=0, msk=0, ex=0, z=0,
                    pv=0, sg=0, rr=0, rb=0, rbe=0, ae=0, v4=0, sl=0, gtb=0,
                    prt_par=[0, 0], m2_sl=[0, 0], pv_hist={})

    engs = [nc.tensor, nc.vector, nc.scalar]
    idxs = [e.partition_id() % 2 for e in engs]
    for ib in nc.Switch(engines=engs, index=idxs, n=2):
        emit_p4(scheds[ib], fresh_cnt())

    if DEBUG >= 3:
        dbg3 = {}
        for nm, tns in (("d_wks", wks), ("d_wvs", wvs),
                        ("d_vbuf", vbuf[(P4_NH - 1) % 2]), ("d_kbuf", kbuf[(P4_NH - 1) % 2]),
                        ("d_qh", QT), ("d_pbuf", pts[0]),
                        ("d_pbuf1", pts[1]), ("d_pbuf2", pts[2]),
                        ("d_nmrow", nmrow[0]), ("d_srow", srow),
                        ("d_rbsb", rbsb),
                        ("d_nmx", nmq), ("d_mxc", mxc)):
            shp = list(tns[:].shape)
            dt = tns[:].dtype
            dbg3[nm] = (nc.dram_tensor(nm, shp, dt, kind="ExternalOutput").ap(), tns)
        s_d3 = sem("s_d3")
        sy = nc.sync
        sy.wait_ge(s_ae, P4_NH * NSLOT)
        nd3 = 0
        for nm, (dap, tns) in dbg3.items():
            sy.dma_start(dap, tns[:]).then_inc(s_d3, 16)
            nd3 += 16
        sy.wait_ge(s_d3, nd3)

    if UPTO <= 4:
        sc4.close()
        scQ.close()
        return nc

    with nc.Block():
        pass
    sc4.close()
    scQ.close()
    sc5 = Scope()
    atsb = sc5.sbuf(nc, "atsb", [128, ND * NOWN], BF16)
    wob = [sc5.sbuf(nc, f"wob{i}", [128, ND * 128], BF16) for i in range(2)]
    oysb = [sc5.sbuf(nc, f"oysb{i}", [128, 512], F32) for i in range(2)]
    pyo = [sc5.psum(nc, f"pyo{i}", [128, 512], F32) for i in range(2)]

    s_al = sem("s_al")
    s_wo = [sem("s_wo0"), sem("s_wo1")]
    s_omm = sem("s_omm")
    s_oye = sem("s_oye")
    s_yo = [sem("s_yo0"), sem("s_yo1")]

    at3 = at_spill.rearrange("(j l) k -> l j k", l=128)

    with nc.Block() as blk:
        @blk.sync
        def _(sync):
            sync.wait_ge(s_asp[0], 16 * ((H * NSLOT + 1) // 2))
            sync.wait_ge(s_asp[1], 16 * ((H * NSLOT) // 2))
            for part in range(4):
                j0, j1 = (ND * part) // 4, (ND * (part + 1)) // 4
                sync.dma_start(atsb[:].rearrange("p (j k) -> p j k", k=NOWN)[:, j0:j1, :],
                               at3[:, j0:j1, :]).then_inc(s_al, 16)

        @blk.gpsimd
        def _(gp):
            NQC5 = NOWN // 512
            nacc = 0

            def emit_acc(gp, seq):
                gp.wait_ge(s_oye, seq + 1)
                ob_, tc_ = seq // NQC5, seq % NQC5
                gp.dma_start(out[ob_ * 128:(ob_ + 1) * 128, tc_ * 512:(tc_ + 1) * 512],
                             oysb[seq % 2][:], accum_op=ALU.add).then_inc(s_yo[seq % 2], 16)

            for ob in range(ND):
                if ob >= 2:
                    while nacc < NQC5 * (ob - 1) - 1:
                        emit_acc(gp, nacc)
                        nacc += 1
                    gp.wait_ge(s_omm, NQC5 * (ob - 1))
                gp.dma_start(wob[ob % 2][:], woT[ob]).then_inc(s_wo[ob % 2], 16)
            while nacc < ND * NQC5:
                emit_acc(gp, nacc)
                nacc += 1
            gp.wait_ge(s_yo[0], 16 * ((ND * NQC5 + 1) // 2))
            gp.wait_ge(s_yo[1], 16 * ((ND * NQC5) // 2))

        @blk.tensor
        def _(t):
            t.wait_ge(s_al, 16 * 4)
            for ob in range(ND):
                t.wait_ge(s_wo[ob % 2], 16 * (ob // 2 + 1))
                for tc in range(NOWN // 512):
                    seq = ob * (NOWN // 512) + tc
                    if seq >= 2:
                        t.wait_ge(s_oye, seq - 1)
                    for j in range(ND):
                        ins = t.matmul(pyo[seq % 2][:], wob[ob % 2][:, j * 128:(j + 1) * 128],
                                       atsb[:, j * NOWN + tc * 512:j * NOWN + (tc + 1) * 512],
                                       start=(j == 0), stop=(j == ND - 1))
                    ins.then_inc(s_omm, 1)

        @blk.scalar
        def _(s):
            for seq in range(ND * (NOWN // 512)):
                s.wait_ge(s_omm, seq + 1)
                if seq >= 2:
                    s.wait_ge(s_yo[seq % 2], 16 * (seq // 2))
                s.copy(oysb[seq % 2][:], pyo[seq % 2][:]).then_inc(s_oye, 1)

    sc5.close()
    return nc



def kernel(L0_attn_x, L0_attn_W_q, L0_attn_W_kv, L0_attn_W_k_heads,
           L0_attn_W_v_heads, L0_attn_W_o, L0_ffn_gate_w, L0_ffn_up_w,
           L0_ffn_down_w, L0_ffn_gate_s, L0_ffn_up_s, L0_ffn_down_s):
    x = np.ascontiguousarray(L0_attn_x, dtype=np.float32)
    tabs = host_tables()
    def tile_blocks(wT, nblk):
        din, dout = wT.shape
        nj = din // 128
        out = np.empty((nblk, 128, din), dtype=wT.dtype)
        w3 = wT.reshape(nj, 128, dout)
        for b in range(nblk):
            out[b] = w3[:, :, b * 128:(b + 1) * 128].transpose(1, 0, 2).reshape(128, din)
        return out

    shared = dict(
        wqT=tile_blocks(np.ascontiguousarray(np.asarray(L0_attn_W_q).T), H),
        wkvT=np.ascontiguousarray(np.asarray(L0_attn_W_kv).T),
        wkh=np.ascontiguousarray(L0_attn_W_k_heads),
        wvh=np.ascontiguousarray(L0_attn_W_v_heads),
        woT=tile_blocks(np.ascontiguousarray(np.asarray(L0_attn_W_o).T), ND),
        gateT=tile_blocks(np.ascontiguousarray(np.asarray(L0_ffn_gate_w).T), NI),
        upT=tile_blocks(np.ascontiguousarray(np.asarray(L0_ffn_up_w).T), NI),
        downT=tile_blocks(np.ascontiguousarray(np.asarray(L0_ffn_down_w).T), ND),
        gates=np.ascontiguousarray(np.asarray(L0_ffn_gate_s).reshape(NI, 128)),
        ups=np.ascontiguousarray(np.asarray(L0_ffn_up_s).reshape(NI, 128)),
        downs=np.ascontiguousarray(np.asarray(L0_ffn_down_s).reshape(ND, 128)),
        cosk=tabs["cosk"], srotk=tabs["srotk"],
        identbf=tabs["identbf"], identf=tabs["identf"], maskf=tabs["maskf"],
        masktf=tabs["masktf"],
        onesf=tabs["onesf"],
    )
    in_maps = []
    for c in range(8):
        b, half = c // 2, c % 2
        rows = own_rows(half)
        m = dict(shared)
        m["x_row"] = np.ascontiguousarray(x[b])
        m["x_own"] = np.ascontiguousarray(x[b][rows])
        m["xT_own"] = np.ascontiguousarray(x[b][rows].T)
        m["cosq"] = np.ascontiguousarray(tabs["cosq_full"][:, rows])
        m["srotq"] = np.ascontiguousarray(tabs["srotq_full"][:, rows])
        in_maps.append(m)

    nc = build_nc()
    res = run_bass_kernel_spmd(nc, in_maps, core_ids=list(range(8)))

    full = np.empty((B, S, D), dtype=np.float32)
    for c in range(8):
        b, half = c // 2, c % 2
        full[b, own_rows(half)] = np.asarray(res.results[c]["out"]).T
    return full


# version 17
# speedup vs baseline: 1.2234x; 1.0222x over previous
import math

import numpy as np
import ml_dtypes
import concourse.bass as bass
import concourse.mybir as mybir
from concourse.bass_utils import run_bass_kernel_spmd
from concourse.hw_specs import get_activation_tables

F32 = mybir.dt.float32
BF16 = mybir.dt.bfloat16
I32 = mybir.dt.int32
AF = mybir.ActivationFunctionType
ALU = mybir.AluOpType
AX = mybir.AxisListType

B, S, D, H, DH, KV, INTER = 4, 4096, 2048, 16, 128, 256, 5632
NT_ROW = S // 128
NT_OWN = 16
ND = D // 128
NI = INTER // 128
HALF_CHUNKS = ([0, 1, 6, 7], [2, 3, 4, 5])
NOWN = 2048
SCL = 1.0 / math.sqrt(DH)
EPS = 1e-5
NEG = -1e30

DEBUG = 0
P4_NH = 16
P4_STAGE = 4
UPTO = 5


def act_set_idx(*need):
    import concourse.bass as _b
    nc = _b.Bass("TRN2", target_bir_lowering=False)
    for i, funcs in enumerate(get_activation_tables(nc.m.arch).values()):
        names = {f.name for f in funcs}
        if set(need) <= names:
            return i
    raise ValueError(need)


SET_LNEXP = 6
SET_SILU = 18


def load_act_set(nc, s, idx):
    s.add_instruction(mybir.InstLoadActFuncSet(
        name=nc.get_next_instruction_name(), act_func_set_id=idx, ins=[], outs=[]))


def host_tables():
    rope_half = DH // 4
    freqs = 1.0 / (10000.0 ** (np.arange(rope_half, dtype=np.float32) / np.float32(rope_half)))
    ang = np.outer(np.arange(S, dtype=np.float32), freqs)
    cos = np.repeat(np.cos(ang), 2, axis=-1)
    sin = np.repeat(np.sin(ang), 2, axis=-1)
    bf = ml_dtypes.bfloat16
    return {
        "cosk": np.ascontiguousarray(cos.T).astype(bf),
        "sink": np.ascontiguousarray(sin.T).astype(bf),
        "srotk": np.ascontiguousarray(np.concatenate([sin[:, 32:], sin[:, :32]], 1).T).astype(bf),
        "cosq_full": np.ascontiguousarray((cos * SCL).T).astype(bf),
        "sinq_full": np.ascontiguousarray((sin * SCL).T).astype(bf),
        "srotq_full": np.ascontiguousarray(
            np.concatenate([sin[:, 32:] * SCL, sin[:, :32] * SCL], 1).T).astype(bf),
        "identbf": np.eye(128, dtype=bf),
        "identf": np.eye(128, dtype=np.float32),
        "maskf": np.triu(np.full((128, 128), NEG, np.float32), 1),
        "masktf": np.tril(np.full((128, 128), NEG, np.float32), -1),
        "onesf": np.ones((1, 128), dtype=np.float32),
    }


def own_rows(half):
    return np.concatenate([np.arange(c * 512, (c + 1) * 512) for c in HALF_CHUNKS[half]])


def sched_for(half):
    out = []
    for c in HALF_CHUNKS[half]:
        L = (c + 1) * 512
        out.append((L, [L - 384 + 128 * q for q in range(4)]))
    return out


class Scope:

    def __init__(self):
        self.cms = []

    def sbuf(self, nc, name, shape, dtype, side=None):
        cm = nc.sbuf_tensor(name, shape, dtype, side=side)
        self.cms.append(cm)
        return cm.__enter__()

    def psum(self, nc, name, shape, dtype):
        cm = nc.psum_tensor(name, shape, dtype)
        self.cms.append(cm)
        return cm.__enter__()

    def close(self):
        for cm in reversed(self.cms):
            cm.__exit__(None, None, None)
        self.cms = []


def build_nc():
    nc = bass.Bass("TRN2", target_bir_lowering=False)

    x_row = nc.dram_tensor("x_row", [S, D], F32, kind="ExternalInput").ap()
    x_own = nc.dram_tensor("x_own", [NOWN, D], F32, kind="ExternalInput").ap()
    xT_own = nc.dram_tensor("xT_own", [D, NOWN], F32, kind="ExternalInput").ap()
    wqT = nc.dram_tensor("wqT", [H, 128, D], F32, kind="ExternalInput").ap()
    wkvT = nc.dram_tensor("wkvT", [D, KV], F32, kind="ExternalInput").ap()
    wkh = nc.dram_tensor("wkh", [H, 128, 128], F32, kind="ExternalInput").ap()
    wvh = nc.dram_tensor("wvh", [H, 128, 128], F32, kind="ExternalInput").ap()
    woT = nc.dram_tensor("woT", [ND, 128, D], F32, kind="ExternalInput").ap()
    gateT = nc.dram_tensor("gateT", [NI, 128, D], F32, kind="ExternalInput").ap()
    upT = nc.dram_tensor("upT", [NI, 128, D], F32, kind="ExternalInput").ap()
    downT = nc.dram_tensor("downT", [ND, 128, INTER], F32, kind="ExternalInput").ap()
    gates = nc.dram_tensor("gates", [NI, 128], F32, kind="ExternalInput").ap()
    ups = nc.dram_tensor("ups", [NI, 128], F32, kind="ExternalInput").ap()
    downs = nc.dram_tensor("downs", [ND, 128], F32, kind="ExternalInput").ap()
    cosk = nc.dram_tensor("cosk", [64, S], BF16, kind="ExternalInput").ap()
    srotk = nc.dram_tensor("srotk", [64, S], BF16, kind="ExternalInput").ap()
    cosq = nc.dram_tensor("cosq", [64, NOWN], BF16, kind="ExternalInput").ap()
    srotq = nc.dram_tensor("srotq", [64, NOWN], BF16, kind="ExternalInput").ap()
    identbf = nc.dram_tensor("identbf", [128, 128], BF16, kind="ExternalInput").ap()
    identf = nc.dram_tensor("identf", [128, 128], F32, kind="ExternalInput").ap()
    maskf = nc.dram_tensor("maskf", [128, 128], F32, kind="ExternalInput").ap()
    masktf = nc.dram_tensor("masktf", [128, 128], F32, kind="ExternalInput").ap()
    onesf = nc.dram_tensor("onesf", [1, 128], F32, kind="ExternalInput").ap()

    out = nc.dram_tensor("out", [D, NOWN], F32, kind="ExternalOutput").ap()

    spillkind = "ExternalOutput" if DEBUG >= 2 else "Internal"
    hdn_spill = nc.dram_tensor("hdn_spill", [INTER, NOWN], BF16, kind=spillkind).ap()
    at_spill = nc.dram_tensor("at_spill", [D, NOWN], BF16, kind=spillkind).ap()

    dbg = {}
    if DEBUG:
        dbg["d_kcT"] = nc.dram_tensor("d_kcT", [128, S], BF16, kind="ExternalOutput").ap()
        dbg["d_vc"] = nc.dram_tensor("d_vc", [128, S], BF16, kind="ExternalOutput").ap()
        dbg["d_xnT"] = nc.dram_tensor("d_xnT", [128, ND * NOWN], BF16, kind="ExternalOutput").ap()

    sb = nc.alloc_sbuf_tensor
    scX = Scope()
    xnT = scX.sbuf(nc, "xnT", [128, ND * NOWN], BF16, side="right")
    kcT = sb("kcT", [128, S], BF16)
    vc = sb("vc", [128, S], BF16)
    s_cosk = sb("s_cosk", [64, S], BF16)
    s_srotk = sb("s_srotk", [64, S], BF16)
    s_cosq = scX.sbuf(nc, "s_cosq", [64, NOWN], BF16, side="right")
    s_srotq = scX.sbuf(nc, "s_srotq", [64, NOWN], BF16, side="right")
    s_idb = sb("s_idb", [128, 128], BF16)
    s_idf = sb("s_idf", [128, 128], F32)
    s_mask = sb("s_mask", [128, 128], F32)
    s_maskt = sb("s_maskt", [128, 128], F32)
    s_onesb = sb("s_onesb", [128, 128], BF16)
    s_ones = sb("s_ones", [1, 128], F32)
    wks = sb("wks", [128, H], F32)
    wvs = sb("wvs", [128, H], F32)
    gsv = sb("gsv", [128, NI], F32)
    usv = sb("usv", [128, NI], F32)
    dsv = sb("dsv", [128, ND], F32)

    sem = nc.alloc_semaphore
    sc1 = Scope()
    wkvb = sc1.sbuf(nc, "wkvb", [128, ND * KV], BF16)

    sc0 = Scope()
    whk = sc0.sbuf(nc, "whk", [128, H * 128], F32)
    scr44 = sc0.sbuf(nc, "scr44", [NI, 128], F32)
    scr44b = sc0.sbuf(nc, "scr44b", [NI, 128], F32)
    scr16 = sc0.sbuf(nc, "scr16", [ND, 128], F32)
    p0a = sc0.psum(nc, "p0a", [128, 512], F32)
    s_c0 = sem("s_c0")
    s_g0 = sem("s_g0")
    s_c1 = sem("s_c1")

    with nc.Block() as blk:
        @blk.sync
        def _(sync):
            for dst, src in (
                (s_cosk[:], cosk), (s_cosq[:], cosq),
                (s_srotk[:], srotk), (s_srotq[:], srotq),
                (s_idb[:], identbf), (s_idf[:], identf),
                (s_mask[:], maskf), (s_maskt[:], masktf), (s_ones[:], onesf),
                (scr44[:], gates), (scr44b[:], ups),
                (scr16[:], downs),
                (whk[:].rearrange("p (h k) -> p h k", k=128),
                 wkh.rearrange("h l k -> l h k")),
            ):
                sync.dma_start(dst, src).then_inc(s_c0, 16)

        @blk.gpsimd
        def _(gp):
            gp.memset(s_onesb[:], 1.0)
            gp.dma_start(wkvb[:].rearrange("p (j k) -> p j k", k=KV),
                         wkvT.rearrange("(j l) k -> l j k", l=128)).then_inc(s_g0, 16)

        @blk.vector
        def _(v):
            v.wait_ge(s_c0, 13 * 16)
            v.wait_ge(s_g0, 16)
            for h in range(H):
                v.reduce_sum(wks[:, h:h + 1], whk[:, h * 128:(h + 1) * 128], axis=AX.X)
            v.nop().then_inc(s_c1, 1)

        @blk.tensor
        def _(t):
            t.wait_ge(s_c0, 13 * 16)
            t.transpose(p0a[0:128, 0:NI], scr44[:], s_idf[0:NI, 0:NI])
            t.transpose(p0a[0:128, 64:64 + NI], scr44b[:], s_idf[0:NI, 0:NI])
            t.transpose(p0a[0:128, 128:128 + ND], scr16[:], s_idf[0:ND, 0:ND]).then_inc(s_c1, 1)

        @blk.scalar
        def _(s):
            load_act_set(nc, s, SET_LNEXP)
            s.wait_ge(s_c1, 2)
            s.copy(gsv[:], p0a[0:128, 0:NI])
            s.copy(usv[:], p0a[0:128, 64:64 + NI])
            s.copy(dsv[:], p0a[0:128, 128:128 + ND])

    with nc.Block() as blk:
        @blk.sync
        def _(sync):
            sync.dma_start(whk[:].rearrange("p (h k) -> p h k", k=128),
                           wvh.rearrange("h l k -> l h k")).then_inc(s_c0, 16)

        @blk.vector
        def _(v):
            v.wait_ge(s_c0, 14 * 16)
            for h in range(H):
                ins = v.reduce_sum(wvs[:, h:h + 1], whk[:, h * 128:(h + 1) * 128], axis=AX.X)
            ins.then_inc(s_c1, 1)

    sc0.close()

    xbuf = [sc1.sbuf(nc, f"xbuf{i}", [128, D], F32) for i in range(3)]
    xsq = sc1.sbuf(nc, "xsq", [128, D], BF16)
    xnb = [sc1.sbuf(nc, f"xnb{i}", [128, D], BF16) for i in range(2)]
    xns = [sc1.sbuf(nc, f"xns{i}", [128, ND * 128], BF16) for i in range(2)]
    kctile = [sc1.sbuf(nc, f"kctile{i}", [128, 128], BF16) for i in range(2)]
    sm_sumx = sc1.sbuf(nc, "sm_sumx", [128, 2], F32)
    sm_sumsq = sc1.sbuf(nc, "sm_sumsq", [128, 2], F32)
    sm_mean = sc1.sbuf(nc, "sm_mean", [128, 2], F32)
    sm_msqe = sc1.sbuf(nc, "sm_msqe", [128, 2], F32)
    sm_varp = sc1.sbuf(nc, "sm_varp", [128, 2], F32)
    sm_lnv = sc1.sbuf(nc, "sm_lnv", [128, 2], F32)
    sm_rstd = sc1.sbuf(nc, "sm_rstd", [128, 2], F32)
    ptr = [sc1.psum(nc, f"ptr{i}", [128, 1024], BF16) for i in range(4)]
    pkv = [sc1.psum(nc, f"pkv{i}", [128, 512], F32) for i in range(2)]
    pkt = [sc1.psum(nc, f"pkt{i}", [128, 1024], BF16) for i in range(2)]

    s_x = [sem(f"s_x{i}") for i in range(3)]
    s_sq = sem("s_sq")
    s_var = sem("s_var")
    s_rstd = sem("s_rstd")
    s_xnb = sem("s_xnb")
    s_tr = sem("s_tr")
    s_xe = sem("s_xe")
    s_kv = sem("s_kv")
    s_kce = sem("s_kce")
    s_vce = sem("s_vce")
    s_kt = sem("s_kt")
    s_kte = sem("s_kte")
    s_vch = sem("s_vch")
    s_sch = sem("s_sch")

    NTOT = NT_ROW + NT_OWN

    with nc.Block() as blk:
        @blk.sync
        def _(sync):
            for ti in range(NTOT):
                if ti >= 3:
                    sync.wait_ge(s_sq, ti - 2)
                    sync.wait_ge(s_xnb, ti - 2)
                src = x_row[ti * 128:(ti + 1) * 128, :] if ti < NT_ROW else \
                    x_own[(ti - NT_ROW) * 128:(ti - NT_ROW + 1) * 128, :]
                sync.dma_start(xbuf[ti % 3][:], src).then_inc(s_x[ti % 3], 16)

        @blk.vector
        def _(v):
            nch = 0
            for tt in range(NTOT + 1):
                if tt < NTOT:
                    ti, p = tt, tt % 2
                    v.wait_ge(s_x[ti % 3], 16 * (ti // 3 + 1))
                    v.reduce_sum(sm_sumx[:, p:p + 1], xbuf[ti % 3][:], axis=AX.X).then_inc(s_vch, 1); nch += 1
                    v.wait_ge(s_vch, nch)
                    v.tensor_scalar(sm_mean[:, p:p + 1], sm_sumx[:, p:p + 1], 1.0 / D,
                                    None, ALU.mult).then_inc(s_vch, 1); nch += 1
                    v.wait_ge(s_vch, nch)
                    v.tensor_scalar(sm_msqe[:, p:p + 1], sm_mean[:, p:p + 1],
                                    sm_mean[:, p:p + 1], EPS, ALU.mult, ALU.subtract).then_inc(s_vch, 1); nch += 1
                    v.wait_ge(s_sq, ti + 1)
                    v.wait_ge(s_vch, nch)
                    v.tensor_scalar(sm_varp[:, p:p + 1], sm_sumsq[:, p:p + 1], 1.0 / D,
                                    sm_msqe[:, p:p + 1], ALU.mult, ALU.subtract).then_inc(s_var, 1)
                if tt >= 1:
                    ti, p = tt - 1, (tt - 1) % 2
                    v.wait_ge(s_rstd, ti + 1)
                    if ti >= 2:
                        v.wait_ge(s_tr, 2 * (ti - 1))
                    v.tensor_scalar(xnb[p][:], xbuf[ti % 3][:], sm_mean[:, p:p + 1],
                                    sm_rstd[:, p:p + 1], ALU.subtract, ALU.mult).then_inc(s_xnb, 1)

        @blk.scalar
        def _(s):
            for tt in range(NTOT + 1):
                if tt < NTOT:
                    ti, p = tt, tt % 2
                    s.wait_ge(s_x[ti % 3], 16 * (ti // 3 + 1))
                    s.activation(xsq[:], xbuf[ti % 3][:], AF.Square,
                                 accum_out=sm_sumsq[:, p:p + 1]).then_inc(s_sq, 1)
                    s.wait_ge(s_var, ti + 1)
                    s.activation(sm_lnv[:, p:p + 1], sm_varp[:, p:p + 1], AF.Ln).then_inc(s_sch, 1)
                    s.wait_ge(s_sch, ti + 1)
                    s.activation(sm_rstd[:, p:p + 1], sm_lnv[:, p:p + 1], AF.Exp,
                                 scale=-0.5).then_inc(s_rstd, 1)
                if tt >= 1:
                    ti, p = tt - 1, (tt - 1) % 2
                    s.wait_ge(s_tr, 2 * ti + 1)
                    if ti < NT_ROW:
                        s.copy(xns[p][:, 0:1024], ptr[(2 * ti) % 4][:]).then_inc(s_xe, 1)
                        s.wait_ge(s_tr, 2 * ti + 2)
                        s.copy(xns[p][:, 1024:2048], ptr[(2 * ti + 1) % 4][:]).then_inc(s_xe, 1)
                        s.wait_ge(s_kv, ti + 1)
                        s.copy(kctile[p][:], pkv[p][:, 0:128]).then_inc(s_kce, 1)
                        s.copy(vc[:, ti * 128:(ti + 1) * 128], pkv[p][:, 128:256]).then_inc(s_vce, 1)
                        s.wait_ge(s_kt, ti + 1)
                        s.copy(kcT[:, ti * 128:(ti + 1) * 128], pkt[p][:, 0:128]).then_inc(s_kte, 1)
                    else:
                        o = ti - NT_ROW
                        xv = xnT[:].rearrange("p (j t) -> p j t", t=NOWN)
                        s.copy(xv[:, 0:8, o * 128:(o + 1) * 128],
                               ptr[(2 * ti) % 4][:].rearrange("p (j t) -> p j t", t=128)).then_inc(s_xe, 1)
                        s.wait_ge(s_tr, 2 * ti + 2)
                        s.copy(xv[:, 8:16, o * 128:(o + 1) * 128],
                               ptr[(2 * ti + 1) % 4][:].rearrange("p (j t) -> p j t", t=128)).then_inc(s_xe, 1)

        @blk.tensor
        def _(t):
            for tt in range(1, NTOT + 1):
                ti, p = tt - 1, (tt - 1) % 2
                t.wait_ge(s_xnb, ti + 1)
                if ti >= 2:
                    t.wait_ge(s_xe, 2 * (ti - 1))
                for half8 in range(2):
                    bank = ptr[(2 * ti + half8) % 4]
                    for j in range(8):
                        jj = half8 * 8 + j
                        ins = t.transpose(bank[:, j * 128:(j + 1) * 128],
                                          xnb[p][:, jj * 128:(jj + 1) * 128], s_idb[:])
                        if j == 7:
                            ins.then_inc(s_tr, 1)
                if ti < NT_ROW:
                    t.wait_ge(s_xe, 2 * (ti + 1))
                    if ti >= 2:
                        t.wait_ge(s_kce, ti - 1)
                        t.wait_ge(s_vce, ti - 1)
                    for j in range(ND):
                        ins = t.matmul(pkv[p][:, 0:KV], xns[p][:, j * 128:(j + 1) * 128],
                                       wkvb[:, j * KV:(j + 1) * KV],
                                       start=(j == 0), stop=(j == ND - 1))
                    ins.then_inc(s_kv, 1)
                    t.wait_ge(s_kce, ti + 1)
                    if ti >= 2:
                        t.wait_ge(s_kte, ti - 1)
                    t.transpose(pkt[p][:, 0:128], kctile[p][:], s_idb[:]).then_inc(s_kt, 1)

    sc1.close()

    if DEBUG >= 1:
        s_dbg = sem("s_dbg")
        with nc.Block() as blk:
            @blk.sync
            def _(sync):
                sync.dma_start(dbg["d_kcT"], kcT[:]).then_inc(s_dbg, 16)
                sync.dma_start(dbg["d_vc"], vc[:]).then_inc(s_dbg, 16)
                sync.dma_start(dbg["d_xnT"], xnT[:]).then_inc(s_dbg, 16)
                sync.wait_ge(s_dbg, 48)
    if UPTO <= 1:
        return nc

    NTK = NOWN // 512
    sc2a = Scope()
    gwb = [sc2a.sbuf(nc, f"gwb{i}", [128, ND * 128], BF16) for i in range(2)]
    uwb = [sc2a.sbuf(nc, f"uwb{i}", [128, ND * 128], BF16) for i in range(2)]
    hgb = [sc2a.sbuf(nc, f"hgb{i}", [128, 512], BF16) for i in range(2)]
    ubf = [sc2a.sbuf(nc, f"ubf{i}", [128, 512], BF16) for i in range(2)]
    hdnb = [sc2a.sbuf(nc, f"hdnb{i}", [128, 512], BF16) for i in range(2)]
    pg = [sc2a.psum(nc, f"pg{i}", [128, 512], F32) for i in range(2)]
    pu = [sc2a.psum(nc, f"pu{i}", [128, 512], F32) for i in range(2)]

    s_gw = [sem("s_gw0"), sem("s_gw1")]
    s_uw = [sem("s_uw0"), sem("s_uw1")]
    s_gmm = sem("s_gmm")
    s_umm = sem("s_umm")
    s_sil = sem("s_sil")
    s_hdn = sem("s_hdn")
    s_hsp = [sem("s_hsp0"), sem("s_hsp1")]
    s_vc2 = sem("s_vc2")



    with nc.Block() as blk:
        @blk.gpsimd
        def _(gp):
            for i in range(NI):
                if i >= 2:
                    gp.wait_ge(s_gmm, NTK * (i - 1))
                    gp.wait_ge(s_umm, NTK * (i - 1))
                gp.dma_start(gwb[i % 2][:], gateT[i]).then_inc(s_gw[i % 2], 16)
                gp.dma_start(uwb[i % 2][:], upT[i]).then_inc(s_uw[i % 2], 16)

        @blk.tensor
        def _(t):
            for i in range(NI):
                t.wait_ge(s_gw[i % 2], 16 * (i // 2 + 1))
                t.wait_ge(s_uw[i % 2], 16 * (i // 2 + 1))
                for tk in range(NTK):
                    seq = i * NTK + tk
                    if seq >= 2:
                        t.wait_ge(s_sil, seq - 1)
                    for j in range(ND):
                        ins = t.matmul(pg[seq % 2][:], gwb[i % 2][:, j * 128:(j + 1) * 128],
                                       xnT[:, j * NOWN + tk * 512:j * NOWN + (tk + 1) * 512],
                                       start=(j == 0), stop=(j == ND - 1))
                    ins.then_inc(s_gmm, 1)
                    if seq >= 2:
                        t.wait_ge(s_hdn, seq - 1)
                    for j in range(ND):
                        ins = t.matmul(pu[seq % 2][:], uwb[i % 2][:, j * 128:(j + 1) * 128],
                                       xnT[:, j * NOWN + tk * 512:j * NOWN + (tk + 1) * 512],
                                       start=(j == 0), stop=(j == ND - 1))
                    ins.then_inc(s_umm, 1)

        @blk.scalar
        def _(s):
            load_act_set(nc, s, SET_SILU)
            for i in range(NI):
                for tk in range(NTK):
                    seq = i * NTK + tk
                    s.wait_ge(s_gmm, seq + 1)
                    if seq >= 2:
                        s.wait_ge(s_hdn, seq - 1)
                    s.activation(hgb[seq % 2][:], pg[seq % 2][:], AF.Silu,
                                 scale=gsv[:, i:i + 1]).then_inc(s_sil, 1)

        @blk.vector
        def _(v):
            nv = 0
            for i in range(NI):
                for tk in range(NTK):
                    seq = i * NTK + tk
                    v.wait_ge(s_umm, seq + 1)
                    if seq >= 2:
                        v.wait_ge(s_hsp[seq % 2], 16 * (seq // 2))
                    v.tensor_scalar(ubf[seq % 2][:], pu[seq % 2][:], usv[:, i:i + 1],
                                    None, ALU.mult).then_inc(s_vc2, 1); nv += 1
                    v.wait_ge(s_sil, seq + 1)
                    v.wait_ge(s_vc2, nv)
                    v.tensor_mul(hdnb[seq % 2][:], hgb[seq % 2][:],
                                 ubf[seq % 2][:]).then_inc(s_hdn, 1)

        @blk.sync
        def _(sync):
            for i in range(NI):
                for tk in range(NTK):
                    seq = i * NTK + tk
                    sync.wait_ge(s_hdn, seq + 1)
                    sync.dma_start(hdn_spill[i * 128:(i + 1) * 128, tk * 512:(tk + 1) * 512],
                                   hdnb[seq % 2][:]).then_inc(s_hsp[seq % 2], 16)

    sc2a.close()
    if UPTO <= 2:
        return nc

    sc2b = Scope()
    hdnT = sc2b.sbuf(nc, "hdnT", [128, NI * 512], BF16)
    dstrip = [sc2b.sbuf(nc, f"dstrip{i}", [128, NI * 128], BF16) for i in range(4)]
    ysb = [sc2b.sbuf(nc, f"ysb{i}", [128, 512], F32) for i in range(2)]
    xtb = [sc2b.sbuf(nc, f"xtb{i}", [128, 512], F32) for i in range(2)]
    py = [sc2b.psum(nc, f"py{i}", [128, 512], F32) for i in range(2)]

    s_hl = sem("s_hl")
    s_dw = [sem(f"s_dw{i}") for i in range(4)]
    s_dmm = sem("s_dmm")
    s_yv = sem("s_yv")
    s_yst = [sem("s_yst0"), sem("s_yst1")]
    s_xl = [sem("s_xl0"), sem("s_xl1")]
    s_vb2 = sem("s_vb2")

    spill3 = hdn_spill.rearrange("(j l) k -> l j k", l=128)

    with nc.Block() as blk:
        @blk.sync
        def _(sync):
            nsp = NI * NTK
            sync.wait_ge(s_hsp[0], 16 * ((nsp + 1) // 2))
            sync.wait_ge(s_hsp[1], 16 * (nsp // 2))
            for tc in range(NTK):
                if tc >= 1:
                    sync.wait_ge(s_dmm, ND * tc)
                for part in range(4):
                    j0 = (NI * part) // 4
                    j1 = (NI * (part + 1)) // 4
                    if j1 > j0:
                        sync.dma_start(
                            hdnT[:].rearrange("p (j k) -> p j k", k=512)[:, j0:j1, :],
                            spill3[:, j0:j1, tc * 512:(tc + 1) * 512]).then_inc(s_hl, 16)
                for hb in range(ND):
                    seq = tc * ND + hb
                    if seq >= 2:
                        sync.wait_ge(s_yv, seq - 1)
                    sync.dma_start(xtb[seq % 2][:],
                                   xT_own[hb * 128:(hb + 1) * 128, tc * 512:(tc + 1) * 512]
                                   ).then_inc(s_xl[seq % 2], 16)
                    sync.wait_ge(s_yv, seq + 1)
                    sync.dma_start(out[hb * 128:(hb + 1) * 128, tc * 512:(tc + 1) * 512],
                                   ysb[seq % 2][:]).then_inc(s_yst[seq % 2], 16)

        @blk.gpsimd
        def _(gp):
            for tc in range(NTK):
                for hb in range(ND):
                    seq = tc * ND + hb
                    if seq >= 4:
                        gp.wait_ge(s_dmm, seq - 3)
                    gp.dma_start(dstrip[seq % 4][:], downT[hb]).then_inc(s_dw[seq % 4], 16)
            gp.wait_ge(s_yst[0], 16 * ((NTK * ND + 1) // 2))
            gp.wait_ge(s_yst[1], 16 * ((NTK * ND) // 2))

        @blk.tensor
        def _(t):
            for tc in range(NTK):
                t.wait_ge(s_hl, 16 * 4 * (tc + 1))
                for hb in range(ND):
                    seq = tc * ND + hb
                    t.wait_ge(s_dw[seq % 4], 16 * (seq // 4 + 1))
                    if seq >= 2:
                        t.wait_ge(s_yv, seq - 1)
                    for jb in range(NI):
                        ins = t.matmul(py[seq % 2][:], dstrip[seq % 4][:, jb * 128:(jb + 1) * 128],
                                       hdnT[:, jb * 512:(jb + 1) * 512],
                                       start=(jb == 0), stop=(jb == NI - 1))
                    ins.then_inc(s_dmm, 1)

        @blk.vector
        def _(v):
            for tc in range(NTK):
                for hb in range(ND):
                    seq = tc * ND + hb
                    v.wait_ge(s_dmm, seq + 1)
                    if seq >= 2:
                        v.wait_ge(s_yst[seq % 2], 16 * (seq // 2))
                    v.tensor_scalar(ysb[seq % 2][:], py[seq % 2][:], dsv[:, hb:hb + 1],
                                    None, ALU.mult).then_inc(s_vb2, 1)
                    v.wait_ge(s_vb2, seq + 1)
                    v.wait_ge(s_xl[seq % 2], 16 * (seq // 2 + 1))
                    v.tensor_add(ysb[seq % 2][:], ysb[seq % 2][:],
                                 xtb[seq % 2][:]).then_inc(s_yv, 1)

    sc2b.close()
    if UPTO <= 3:
        return nc

    scheds = (sched_for(0), sched_for(1))
    NSLOT = len(HALF_CHUNKS[0])
    PQ = max(L for sch in scheds for L, _ in sch)
    NQC = NOWN // 512

    scQ = Scope()
    QT = scQ.sbuf(nc, "QT", [128, H * NOWN], BF16)
    wvball = scQ.sbuf(nc, "wvball", [128, H * 128], BF16)

    sc3 = Scope()
    wqb = [sc3.sbuf(nc, f"wqb{i}", [128, ND * 128], BF16) for i in range(2)]
    qta = sc3.sbuf(nc, "qta", [32, 512], F32)
    qtb = sc3.sbuf(nc, "qtb", [32, 512], F32)
    qtc = sc3.sbuf(nc, "qtc", [32, 512], F32)
    qtd = sc3.sbuf(nc, "qtd", [32, 512], F32)
    vrowq = sc3.sbuf(nc, "vrowq", [1, 128], F32)
    pq3 = [sc3.psum(nc, f"pq3{i}", [128, 512], F32) for i in range(2)]
    pbw = sc3.psum(nc, "pbw", [128, 256], F32)

    s_wq = [sem("s_wq0"), sem("s_wq1")]
    s_qmm = sem("s_qmm")
    s_qe = sem("s_qe")
    s_qrope = sem("s_qrope")
    s_v3 = sem("s_v3")
    s_bt = sem("s_bt")
    s_bv = sem("s_bv")
    s_bm = sem("s_bm")
    s_bw = sem("s_bw")

    with nc.Block() as blk:
        @blk.gpsimd
        def _(gp):
            for h in range(H):
                if h >= 2:
                    gp.wait_ge(s_qmm, NQC * (h - 1))
                gp.dma_start(wqb[h % 2][:], wqT[h]).then_inc(s_wq[h % 2], 16)

        @blk.tensor
        def _(t):
            for h in range(H):
                t.wait_ge(s_wq[h % 2], 16 * (h // 2 + 1))
                for nk in range(NQC):
                    seq = h * NQC + nk
                    if seq >= 2:
                        t.wait_ge(s_qe, seq - 1)
                        t.wait_ge(s_qrope, seq - 1)
                    for j in range(ND):
                        ins = t.matmul(pq3[seq % 2][:], wqb[h % 2][:, j * 128:(j + 1) * 128],
                                       xnT[:, j * NOWN + nk * 512:j * NOWN + (nk + 1) * 512],
                                       start=(j == 0), stop=(j == ND - 1))
                    ins.then_inc(s_qmm, 1)
            t.wait_ge(s_c1, 3)
            for h in range(H):
                if h >= 1:
                    t.wait_ge(s_bv, h)
                    t.wait_ge(s_bw, h)
                t.transpose(pbw[0:1, 0:128], wvs[:, h:h + 1], s_idf[:]).then_inc(s_bt, 1)
                t.wait_ge(s_bv, h + 1)
                if h >= 1:
                    t.wait_ge(s_bw, h)
                t.matmul(pbw[:, 128:256], s_ones[0:1, 0:128], vrowq[:],
                         start=True, stop=True).then_inc(s_bm, 1)

        @blk.scalar
        def _(sx):
            for h in range(H):
                for nk in range(NQC):
                    seq = h * NQC + nk
                    sx.wait_ge(s_qmm, seq + 1)
                    sx.activation(QT[64:128, h * NOWN + nk * 512:h * NOWN + (nk + 1) * 512],
                                  pq3[seq % 2][64:128, :], AF.Copy,
                                  scale=SCL).then_inc(s_qe, 1)
            for h in range(H):
                sx.wait_ge(s_bt, h + 1)
                sx.copy(vrowq[:], pbw[0:1, 0:128]).then_inc(s_bv, 1)
                sx.wait_ge(s_bm, h + 1)
                sx.copy(wvball[:, h * 128:(h + 1) * 128], pbw[:, 128:256]).then_inc(s_bw, 1)

        @blk.vector
        def _(v):
            nch = 0
            for h in range(H):
                for nk in range(NQC):
                    seq = h * NQC + nk
                    v.wait_ge(s_qmm, seq + 1)
                    v.wait_ge(s_v3, nch)
                    v.wait_ge(s_qrope, seq)
                    pa = pq3[seq % 2]
                    ca, sa = pa[0:32, :], pa[32:64, :]
                    ck = s_cosq[:, nk * 512:(nk + 1) * 512]
                    rq = s_srotq[:, nk * 512:(nk + 1) * 512]
                    qcol = h * NOWN + nk * 512
                    v.tensor_mul(qta[:], ca, ck[0:32, :]).then_inc(s_v3, 1); nch += 1
                    v.tensor_mul(qtb[:], sa, rq[32:64, :]).then_inc(s_v3, 1); nch += 1
                    v.tensor_mul(qtc[:], sa, ck[32:64, :]).then_inc(s_v3, 1); nch += 1
                    v.tensor_mul(qtd[:], ca, rq[0:32, :]).then_inc(s_v3, 1); nch += 1
                    v.wait_ge(s_v3, nch)
                    v.tensor_sub(QT[0:32, qcol:qcol + 512], qta[:], qtb[:]).then_inc(s_v3, 1)
                    nch += 1
                    v.tensor_add(QT[32:64, qcol:qcol + 512], qtc[:], qtd[:]).then_inc(s_qrope, 1)

    sc3.close()
    scX.close()

    sc4 = Scope()
    kbuf = [sc4.sbuf(nc, f"kbuf{i}", [128, S], BF16) for i in range(2)]
    vbuf = [sc4.sbuf(nc, f"vbuf{i}", [128, S], BF16) for i in range(2)]
    ktmp = sc4.sbuf(nc, "ktmp", [64, S], BF16)
    pts = [sc4.sbuf(nc, f"pts{i}", [128, 512], BF16) for i in range(3)]
    atst = [sc4.sbuf(nc, f"atst{i}", [128, 512], BF16) for i in range(2)]
    mxc = sc4.sbuf(nc, "mxc", [128, 8], F32)
    nmq = sc4.sbuf(nc, "nmq", [128, 8], F32)
    nmrow = [sc4.sbuf(nc, f"nmrow{i}", [1, 512], BF16) for i in range(2)]
    srow = sc4.sbuf(nc, "srow", [1, 512], F32)
    rrow = sc4.sbuf(nc, "rrow", [1, 512], BF16)
    rbsb = sc4.sbuf(nc, "rbsb", [128, 512], BF16)

    sp1 = [sc4.psum(nc, f"sp1{i}", [128, 512], F32) for i in range(3)]
    sp2 = [sc4.psum(nc, f"sp2{i}", [128, 512], F32) for i in range(2)]
    pvp = sc4.psum(nc, "pvp", [128, 512], F32)
    prow = sc4.psum(nc, "prow", [128, 512], F32)
    rbank = sc4.psum(nc, "rbank", [128, 512], F32)

    s_m1 = sem("s_m1")
    s_mx = sem("s_mx")
    s_cmb = sem("s_cmb")
    s_prt = sem("s_prt")
    s_nr = sem("s_nr")
    s_m2 = sem("s_m2")
    s_msk = sem("s_msk")
    s_ex = sem("s_ex")
    s_z = sem("s_z")
    s_pv = sem("s_pv")
    s_sg = sem("s_sg")
    s_rr = sem("s_rr")
    s_rb = sem("s_rb")
    s_rbe = sem("s_rbe")
    s_ae = sem("s_ae")
    s_asp = [sem("s_asp0"), sem("s_asp1")]
    s_kvb = sem("s_kvb")
    s_kb2 = sem("s_kb2")
    s_v4 = sem("s_v4")

    syncq = nc.sync
    for idx in (range(P4_NH * NSLOT) if P4_STAGE >= 3 else ()):
        syncq.wait_ge(s_ae, idx + 1)
        h, k = idx // NSLOT, idx % NSLOT
        syncq.dma_start(at_spill[h * 128:(h + 1) * 128, k * 512:(k + 1) * 512],
                        atst[idx % 2][:]).then_inc(s_asp[idx % 2], 16)

    def emit_p4(sched, cnt):
        t, v, s = nc.tensor, nc.vector, nc.scalar

        def vch(ins):
            ins.then_inc(s_v4, 1)
            cnt["v4"] += 1
            return ins

        def vwait():
            v.wait_ge(s_v4, cnt["v4"])

        a_guard = [[], [], []]
        b_guard = [[], []]
        load_act_set(nc, s, SET_LNEXP)
        if DEBUG >= 3:
            v.memset(mxc[:], 0.0)
            vch(v.memset(nmq[:], 0.0))

        def emit_kv(g, gm1, gm2):
            gp_ = g % 2
            if g >= 1:
                v.wait_ge(s_m1, gm1)
                v.wait_ge(s_m2, gm2)
            vch(v.tensor_scalar(kbuf[gp_][:], kcT[:], wks[:, g:g + 1], None, ALU.mult))
            yield
            vwait()
            vch(v.tensor_mul(ktmp[0:32, :], kbuf[gp_][32:64, :], s_srotk[32:64, :]))
            yield
            vch(v.tensor_mul(ktmp[32:64, :], kbuf[gp_][0:32, :], s_srotk[0:32, :]))
            yield
            vwait()
            vch(v.tensor_mul(kbuf[gp_][0:32, :], kbuf[gp_][0:32, :], s_cosk[0:32, :]))
            yield
            vch(v.tensor_mul(kbuf[gp_][32:64, :], kbuf[gp_][32:64, :], s_cosk[32:64, :]))
            yield
            vwait()
            vch(v.tensor_sub(kbuf[gp_][0:32, :], kbuf[gp_][0:32, :], ktmp[0:32, :]))
            yield
            v.tensor_add(kbuf[gp_][32:64, :], kbuf[gp_][32:64, :],
                         ktmp[32:64, :]).then_inc(s_kb2, 1)

        def emit_v(g, gpv):
            gp_ = g % 2
            if g == 0:
                v.wait_ge(s_bw, H)
            if g >= 2:
                v.wait_ge(s_pv, gpv)
            wvb_b = wvball[:, g * 128:(g + 1) * 128].rearrange(
                "p (j k) -> p j k", j=1).broadcast_to((128, NT_ROW, 128))
            v.tensor_tensor(vbuf[gp_][:].rearrange("p (j k) -> p j k", k=128),
                            vc[:].rearrange("p (j k) -> p j k", k=128),
                            wvb_b, ALU.mult).then_inc(s_kvb, 1)

        def emit_A(h, k, sp, L, tqs):
            hp = h % 2
            qs0 = h * NOWN + k * 512
            for q in range(4):
                tq = tqs[q]
                nch = (tq + 511) // 512
                v.wait_ge(s_cmb, cnt["cmb"])
                for c in range(nch):
                    w = min(512, tq - c * 512)
                    ar = cnt["m1"] % 3
                    for smv in a_guard[ar]:
                        t.wait_ge(*smv)
                    a_guard[ar] = []
                    t.matmul(sp1[ar][:, 0:w], QT[:, qs0 + q * 128:qs0 + (q + 1) * 128],
                             kbuf[hp][:, c * 512:c * 512 + w],
                             start=True, stop=True).then_inc(s_m1, 1)
                    cnt["m1"] += 1
                    v.wait_ge(s_m1, cnt["m1"])
                    if c == nch - 1:
                        vch(v.tensor_tensor(sp1[ar][:, w - 128:w], sp1[ar][:, w - 128:w],
                                            s_mask[:], ALU.add))
                        vwait()
                    v.reduce_max(mxc[:, c:c + 1], sp1[ar][:, 0:w],
                                 axis=AX.X).then_inc(s_mx, 1)
                    cnt["mx"] += 1
                    a_guard[ar] = [(s_mx, cnt["mx"])]
                    yield
                v.wait_ge(s_prt, cnt["prt_par"][sp])
                v.wait_ge(s_mx, cnt["mx"])
                v.reduce_max(nmq[:, sp * 4 + q:sp * 4 + q + 1], mxc[:, 0:nch],
                             axis=AX.X, negate=True).then_inc(s_cmb, 1)
                cnt["cmb"] += 1
            for q in range(4):
                t.wait_ge(s_cmb, cnt["cmb"] - 3 + q)
                if q == 0:
                    t.wait_ge(s_nr, cnt["nr"])
                t.transpose(prow[0:1, q * 128:(q + 1) * 128],
                            nmq[:, sp * 4 + q:sp * 4 + q + 1],
                            s_idf[:]).then_inc(s_prt, 1)
                cnt["prt"] += 1
                yield
            cnt["prt_par"][sp] = cnt["prt"]

        def emit_sigpv(hp, ntb, tb, ex_cnt, z_cnt, g):
            ptsb = pts[g % 3]
            t.wait_ge(s_ex, ex_cnt)
            if z_cnt is not None:
                t.wait_ge(s_z, z_cnt)
            if tb == 0:
                t.wait_ge(s_rbe, cnt["rbe"])
                t.wait_ge(s_ae, cnt["ae"])
            ins = t.matmul(rbank[0:1, 0:512], s_onesb[:, 0:1], ptsb[:],
                           start=(tb == 0), stop=(tb == ntb - 1))
            if tb == ntb - 1:
                ins.then_inc(s_sg, 1)
                cnt["sg"] += 1
            t.matmul(pvp[:], vbuf[hp][:, tb * 128:(tb + 1) * 128], ptsb[:],
                     start=(tb == 0), stop=(tb == ntb - 1)).then_inc(s_pv, 1)
            cnt["pv"] += 1
            cnt["pv_hist"][g] = cnt["pv"]

        def emit_CD(h, k, sp, L, tqs, prtv):
            hp = h % 2
            ntb = L // 128
            qs0 = h * NOWN + k * 512
            s.wait_ge(s_prt, prtv)
            s.wait_ge(s_m2, cnt["m2_sl"][sp])
            s.copy(nmrow[sp][:], prow[0:1, 0:512]).then_inc(s_nr, 1)
            cnt["nr"] += 1
            if k == 0:
                t.wait_ge(s_kvb, h + 1)
            t.wait_ge(s_nr, cnt["nr"])
            pend = None
            for tb in range(ntb):
                g2 = cnt["m2"] % 2
                for smv in b_guard[g2]:
                    t.wait_ge(*smv)
                b_guard[g2] = []
                t.matmul(sp2[g2][:], kbuf[hp][:, tb * 128:(tb + 1) * 128],
                         QT[:, qs0:qs0 + 512], start=True, stop=False)
                t.matmul(sp2[g2][:], s_onesb[0:1, 0:128], nmrow[sp][:],
                         start=False, stop=True).then_inc(s_m2, 1)
                cnt["m2"] += 1
                dq = tb - (ntb - 4)
                if 0 <= dq < 4:
                    v.wait_ge(s_m2, cnt["m2"])
                    v.tensor_tensor(sp2[g2][:, dq * 128:(dq + 1) * 128],
                                    sp2[g2][:, dq * 128:(dq + 1) * 128],
                                    s_maskt[:], ALU.add).then_inc(s_msk, 1)
                    cnt["msk"] += 1
                g = cnt["gtb"]
                cnt["gtb"] += 1
                ptsb = pts[g % 3]
                s.wait_ge(s_m2, cnt["m2"])
                if 0 <= dq < 4:
                    s.wait_ge(s_msk, cnt["msk"])
                if g >= 3:
                    s.wait_ge(s_pv, cnt["pv_hist"][g - 3])
                s.activation(ptsb[:], sp2[g2][:], AF.Exp).then_inc(s_ex, 1)
                cnt["ex"] += 1
                b_guard[g2] = [(s_ex, cnt["ex"])]
                zq = tb - (ntb - 3)
                if zq >= 0:
                    v.wait_ge(s_ex, cnt["ex"])
                    nz = min(zq, 2) + 1
                    for q in range(nz):
                        ins = v.memset(ptsb[:, q * 128:(q + 1) * 128], 0.0)
                        if q < nz - 1:
                            ins.then_inc(s_v4, 1)
                            cnt["v4"] += 1
                    ins.then_inc(s_z, 1)
                    cnt["z"] += 1
                if pend is not None:
                    emit_sigpv(hp, ntb, *pend)
                pend = (tb, cnt["ex"], cnt["z"] if zq >= 0 else None, g)
                yield
            emit_sigpv(hp, ntb, *pend)
            s.wait_ge(s_sg, cnt["sg"])
            s.activation(srow[:], rbank[0:1, 0:512], AF.Ln).then_inc(s_rr, 1)
            cnt["rr"] += 1
            s.wait_ge(s_rr, cnt["rr"])
            s.activation(rrow[:], srow[:], AF.Exp, scale=-1.0).then_inc(s_rr, 1)
            cnt["rr"] += 1
            t.wait_ge(s_rr, cnt["rr"])
            t.matmul(rbank[:, 0:512], s_onesb[0:1, 0:128], rrow[:],
                     start=True, stop=True).then_inc(s_rb, 1)
            cnt["rb"] += 1
            s.wait_ge(s_rb, cnt["rb"])
            s.copy(rbsb[:], rbank[:, 0:512]).then_inc(s_rbe, 1)
            cnt["rbe"] += 1
            ai = cnt["ae"]
            v.wait_ge(s_rbe, cnt["rbe"])
            v.wait_ge(s_pv, cnt["pv"])
            if ai >= 2:
                v.wait_ge(s_asp[ai % 2], 16 * (ai // 2))
            v.tensor_tensor(atst[ai % 2][:], pvp[:], rbsb[:], ALU.mult).then_inc(s_ae, 1)
            cnt["ae"] += 1
            cnt["m2_sl"][sp] = cnt["m2"]

        for _ in emit_kv(0, 0, 0):
            pass
        emit_v(0, 0)
        if P4_STAGE < 3:
            for g in range(1, P4_NH):
                for _ in emit_kv(g, 0, 0):
                    pass
                emit_v(g, 0)
            return

        state = dict(pending=None, kvgen=None, vhead=0)

        def step_bg():
            if state["kvgen"] is not None:
                try:
                    next(state["kvgen"])
                except StopIteration:
                    state["kvgen"] = None
                    emit_v(state["vhead"], cnt["pv"])

        def step_pend():
            if state["pending"] is None:
                return False
            try:
                next(state["pending"])
                return True
            except StopIteration:
                state["pending"] = None
                return False

        kv_created = 0
        for h in range(P4_NH):
            while state["kvgen"] is not None:
                step_bg()
            t.wait_ge(s_kb2, h + 1)
            for k in range(NSLOT):
                L, tqs = sched[k]
                sp = cnt["sl"] % 2
                cnt["sl"] += 1
                gA = emit_A(h, k, sp, L, tqs)
                for _ in gA:
                    step_pend()
                    step_bg()
                while step_pend():
                    step_bg()
                if k == 0 and h + 1 < P4_NH and kv_created == h:
                    state["kvgen"] = emit_kv(h + 1, cnt["m1"], cnt["m2"])
                    state["vhead"] = h + 1
                    kv_created = h + 1
                state["pending"] = emit_CD(h, k, sp, L, tqs, cnt["prt"])
        while state["pending"] is not None or state["kvgen"] is not None:
            step_pend()
            step_bg()

    def fresh_cnt():
        return dict(m1=0, mx=0, cmb=0, prt=0, nr=0, m2=0, msk=0, ex=0, z=0,
                    pv=0, sg=0, rr=0, rb=0, rbe=0, ae=0, v4=0, sl=0, gtb=0,
                    prt_par=[0, 0], m2_sl=[0, 0], pv_hist={})

    engs = [nc.tensor, nc.vector, nc.scalar]
    idxs = [e.partition_id() % 2 for e in engs]
    for ib in nc.Switch(engines=engs, index=idxs, n=2):
        emit_p4(scheds[ib], fresh_cnt())

    if DEBUG >= 3:
        dbg3 = {}
        for nm, tns in (("d_wks", wks), ("d_wvs", wvs),
                        ("d_vbuf", vbuf[(P4_NH - 1) % 2]), ("d_kbuf", kbuf[(P4_NH - 1) % 2]),
                        ("d_qh", QT), ("d_pbuf", pts[0]),
                        ("d_pbuf1", pts[1]), ("d_pbuf2", pts[2]),
                        ("d_nmrow", nmrow[0]), ("d_srow", srow),
                        ("d_rbsb", rbsb),
                        ("d_nmx", nmq), ("d_mxc", mxc)):
            shp = list(tns[:].shape)
            dt = tns[:].dtype
            dbg3[nm] = (nc.dram_tensor(nm, shp, dt, kind="ExternalOutput").ap(), tns)
        s_d3 = sem("s_d3")
        sy = nc.sync
        sy.wait_ge(s_ae, P4_NH * NSLOT)
        nd3 = 0
        for nm, (dap, tns) in dbg3.items():
            sy.dma_start(dap, tns[:]).then_inc(s_d3, 16)
            nd3 += 16
        sy.wait_ge(s_d3, nd3)

    if UPTO <= 4:
        sc4.close()
        scQ.close()
        return nc

    with nc.Block():
        pass
    sc4.close()
    scQ.close()
    sc5 = Scope()
    atsb = sc5.sbuf(nc, "atsb", [128, ND * NOWN], BF16)
    wob = [sc5.sbuf(nc, f"wob{i}", [128, ND * 128], BF16) for i in range(2)]
    oysb = [sc5.sbuf(nc, f"oysb{i}", [128, 512], F32) for i in range(2)]
    pyo = [sc5.psum(nc, f"pyo{i}", [128, 512], F32) for i in range(2)]

    s_al = sem("s_al")
    s_wo = [sem("s_wo0"), sem("s_wo1")]
    s_omm = sem("s_omm")
    s_oye = sem("s_oye")
    s_yo = [sem("s_yo0"), sem("s_yo1")]

    at3 = at_spill.rearrange("(j l) k -> l j k", l=128)

    with nc.Block() as blk:
        @blk.sync
        def _(sync):
            sync.wait_ge(s_asp[0], 16 * ((H * NSLOT + 1) // 2))
            sync.wait_ge(s_asp[1], 16 * ((H * NSLOT) // 2))
            for part in range(4):
                j0, j1 = (ND * part) // 4, (ND * (part + 1)) // 4
                sync.dma_start(atsb[:].rearrange("p (j k) -> p j k", k=NOWN)[:, j0:j1, :],
                               at3[:, j0:j1, :]).then_inc(s_al, 16)

        @blk.gpsimd
        def _(gp):
            NQC5 = NOWN // 512
            nacc = 0

            def emit_acc(gp, seq):
                gp.wait_ge(s_oye, seq + 1)
                ob_, tc_ = seq // NQC5, seq % NQC5
                gp.dma_start(out[ob_ * 128:(ob_ + 1) * 128, tc_ * 512:(tc_ + 1) * 512],
                             oysb[seq % 2][:], accum_op=ALU.add).then_inc(s_yo[seq % 2], 16)

            for ob in range(ND):
                if ob >= 2:
                    while nacc < NQC5 * (ob - 1) - 1:
                        emit_acc(gp, nacc)
                        nacc += 1
                    gp.wait_ge(s_omm, NQC5 * (ob - 1))
                gp.dma_start(wob[ob % 2][:], woT[ob]).then_inc(s_wo[ob % 2], 16)
            while nacc < ND * NQC5:
                emit_acc(gp, nacc)
                nacc += 1
            gp.wait_ge(s_yo[0], 16 * ((ND * NQC5 + 1) // 2))
            gp.wait_ge(s_yo[1], 16 * ((ND * NQC5) // 2))

        @blk.tensor
        def _(t):
            t.wait_ge(s_al, 16 * 4)
            for ob in range(ND):
                t.wait_ge(s_wo[ob % 2], 16 * (ob // 2 + 1))
                for tc in range(NOWN // 512):
                    seq = ob * (NOWN // 512) + tc
                    if seq >= 2:
                        t.wait_ge(s_oye, seq - 1)
                    for j in range(ND):
                        ins = t.matmul(pyo[seq % 2][:], wob[ob % 2][:, j * 128:(j + 1) * 128],
                                       atsb[:, j * NOWN + tc * 512:j * NOWN + (tc + 1) * 512],
                                       start=(j == 0), stop=(j == ND - 1))
                    ins.then_inc(s_omm, 1)

        @blk.scalar
        def _(s):
            for seq in range(ND * (NOWN // 512)):
                s.wait_ge(s_omm, seq + 1)
                if seq >= 2:
                    s.wait_ge(s_yo[seq % 2], 16 * (seq // 2))
                s.copy(oysb[seq % 2][:], pyo[seq % 2][:]).then_inc(s_oye, 1)

    sc5.close()
    return nc



def kernel(L0_attn_x, L0_attn_W_q, L0_attn_W_kv, L0_attn_W_k_heads,
           L0_attn_W_v_heads, L0_attn_W_o, L0_ffn_gate_w, L0_ffn_up_w,
           L0_ffn_down_w, L0_ffn_gate_s, L0_ffn_up_s, L0_ffn_down_s):
    x = np.ascontiguousarray(L0_attn_x, dtype=np.float32)
    tabs = host_tables()
    def tile_blocks(wT, nblk):
        din, dout = wT.shape
        nj = din // 128
        out = np.empty((nblk, 128, din), dtype=wT.dtype)
        w3 = wT.reshape(nj, 128, dout)
        for b in range(nblk):
            out[b] = w3[:, :, b * 128:(b + 1) * 128].transpose(1, 0, 2).reshape(128, din)
        return out

    shared = dict(
        wqT=tile_blocks(np.ascontiguousarray(np.asarray(L0_attn_W_q).T), H),
        wkvT=np.ascontiguousarray(np.asarray(L0_attn_W_kv).T),
        wkh=np.ascontiguousarray(L0_attn_W_k_heads),
        wvh=np.ascontiguousarray(L0_attn_W_v_heads),
        woT=tile_blocks(np.ascontiguousarray(np.asarray(L0_attn_W_o).T), ND),
        gateT=tile_blocks(np.ascontiguousarray(np.asarray(L0_ffn_gate_w).T), NI),
        upT=tile_blocks(np.ascontiguousarray(np.asarray(L0_ffn_up_w).T), NI),
        downT=tile_blocks(np.ascontiguousarray(np.asarray(L0_ffn_down_w).T), ND),
        gates=np.ascontiguousarray(np.asarray(L0_ffn_gate_s).reshape(NI, 128)),
        ups=np.ascontiguousarray(np.asarray(L0_ffn_up_s).reshape(NI, 128)),
        downs=np.ascontiguousarray(np.asarray(L0_ffn_down_s).reshape(ND, 128)),
        cosk=tabs["cosk"], srotk=tabs["srotk"],
        identbf=tabs["identbf"], identf=tabs["identf"], maskf=tabs["maskf"],
        masktf=tabs["masktf"],
        onesf=tabs["onesf"],
    )
    in_maps = []
    for c in range(8):
        b, half = c // 2, c % 2
        rows = own_rows(half)
        m = dict(shared)
        m["x_row"] = np.ascontiguousarray(x[b])
        m["x_own"] = np.ascontiguousarray(x[b][rows])
        m["xT_own"] = np.ascontiguousarray(x[b][rows].T)
        m["cosq"] = np.ascontiguousarray(tabs["cosq_full"][:, rows])
        m["srotq"] = np.ascontiguousarray(tabs["srotq_full"][:, rows])
        in_maps.append(m)

    nc = build_nc()
    res = run_bass_kernel_spmd(nc, in_maps, core_ids=list(range(8)))

    full = np.empty((B, S, D), dtype=np.float32)
    for c in range(8):
        b, half = c // 2, c % 2
        full[b, own_rows(half)] = np.asarray(res.results[c]["out"]).T
    return full


# version 18 (final)
# speedup vs baseline: 1.2611x; 1.0308x over previous
import math

import numpy as np
import ml_dtypes
import concourse.bass as bass
import concourse.mybir as mybir
from concourse.bass_utils import run_bass_kernel_spmd
from concourse.hw_specs import get_activation_tables

F32 = mybir.dt.float32
BF16 = mybir.dt.bfloat16
I32 = mybir.dt.int32
AF = mybir.ActivationFunctionType
ALU = mybir.AluOpType
AX = mybir.AxisListType

B, S, D, H, DH, KV, INTER = 4, 4096, 2048, 16, 128, 256, 5632
NT_ROW = S // 128
NT_OWN = 16
ND = D // 128
NI = INTER // 128
HALF_CHUNKS = ([0, 1, 6, 7], [2, 3, 4, 5])
NOWN = 2048
SCL = 1.0 / math.sqrt(DH)
EPS = 1e-5
NEG = -1e30

DEBUG = 0
P4_NH = 16
P4_STAGE = 4
UPTO = 5


def act_set_idx(*need):
    import concourse.bass as _b
    nc = _b.Bass("TRN2", target_bir_lowering=False)
    for i, funcs in enumerate(get_activation_tables(nc.m.arch).values()):
        names = {f.name for f in funcs}
        if set(need) <= names:
            return i
    raise ValueError(need)


SET_LNEXP = 6
SET_SILU = 18


def load_act_set(nc, s, idx):
    s.add_instruction(mybir.InstLoadActFuncSet(
        name=nc.get_next_instruction_name(), act_func_set_id=idx, ins=[], outs=[]))


def host_tables():
    rope_half = DH // 4
    freqs = 1.0 / (10000.0 ** (np.arange(rope_half, dtype=np.float32) / np.float32(rope_half)))
    ang = np.outer(np.arange(S, dtype=np.float32), freqs)
    cos = np.repeat(np.cos(ang), 2, axis=-1)
    sin = np.repeat(np.sin(ang), 2, axis=-1)
    bf = ml_dtypes.bfloat16
    return {
        "cosk": np.ascontiguousarray(cos.T).astype(bf),
        "sink": np.ascontiguousarray(sin.T).astype(bf),
        "srotk": np.ascontiguousarray(np.concatenate([sin[:, 32:], sin[:, :32]], 1).T).astype(bf),
        "cosq_full": np.ascontiguousarray((cos * SCL).T).astype(bf),
        "sinq_full": np.ascontiguousarray((sin * SCL).T).astype(bf),
        "srotq_full": np.ascontiguousarray(
            np.concatenate([sin[:, 32:] * SCL, sin[:, :32] * SCL], 1).T).astype(bf),
        "identbf": np.eye(128, dtype=bf),
        "identf": np.eye(128, dtype=np.float32),
        "maskf": np.triu(np.full((128, 128), NEG, np.float32), 1),
        "masktf": np.tril(np.full((128, 128), NEG, np.float32), -1),
        "onesf": np.ones((1, 128), dtype=np.float32),
    }


def own_rows(half):
    return np.concatenate([np.arange(c * 512, (c + 1) * 512) for c in HALF_CHUNKS[half]])


def sched_for(half):
    out = []
    for c in HALF_CHUNKS[half]:
        L = (c + 1) * 512
        out.append((L, [L - 384 + 128 * q for q in range(4)]))
    return out


class Scope:

    def __init__(self):
        self.cms = []

    def sbuf(self, nc, name, shape, dtype, side=None):
        cm = nc.sbuf_tensor(name, shape, dtype, side=side)
        self.cms.append(cm)
        return cm.__enter__()

    def psum(self, nc, name, shape, dtype):
        cm = nc.psum_tensor(name, shape, dtype)
        self.cms.append(cm)
        return cm.__enter__()

    def close(self):
        for cm in reversed(self.cms):
            cm.__exit__(None, None, None)
        self.cms = []


def build_nc():
    nc = bass.Bass("TRN2", target_bir_lowering=False)

    x_row = nc.dram_tensor("x_row", [S, D], F32, kind="ExternalInput").ap()
    x_own = nc.dram_tensor("x_own", [NOWN, D], F32, kind="ExternalInput").ap()
    xT_own = nc.dram_tensor("xT_own", [D, NOWN], F32, kind="ExternalInput").ap()
    wqT = nc.dram_tensor("wqT", [H, 128, D], F32, kind="ExternalInput").ap()
    wkvT = nc.dram_tensor("wkvT", [D, KV], F32, kind="ExternalInput").ap()
    wkh = nc.dram_tensor("wkh", [H, 128, 128], F32, kind="ExternalInput").ap()
    wvh = nc.dram_tensor("wvh", [H, 128, 128], F32, kind="ExternalInput").ap()
    woT = nc.dram_tensor("woT", [ND, 128, D], F32, kind="ExternalInput").ap()
    gateT = nc.dram_tensor("gateT", [NI, 128, D], F32, kind="ExternalInput").ap()
    upT = nc.dram_tensor("upT", [NI, 128, D], F32, kind="ExternalInput").ap()
    downT = nc.dram_tensor("downT", [ND, 128, INTER], F32, kind="ExternalInput").ap()
    gates = nc.dram_tensor("gates", [NI, 128], F32, kind="ExternalInput").ap()
    ups = nc.dram_tensor("ups", [NI, 128], F32, kind="ExternalInput").ap()
    downs = nc.dram_tensor("downs", [ND, 128], F32, kind="ExternalInput").ap()
    cosk = nc.dram_tensor("cosk", [64, S], BF16, kind="ExternalInput").ap()
    srotk = nc.dram_tensor("srotk", [64, S], BF16, kind="ExternalInput").ap()
    cosq = nc.dram_tensor("cosq", [64, NOWN], BF16, kind="ExternalInput").ap()
    srotq = nc.dram_tensor("srotq", [64, NOWN], BF16, kind="ExternalInput").ap()
    identbf = nc.dram_tensor("identbf", [128, 128], BF16, kind="ExternalInput").ap()
    identf = nc.dram_tensor("identf", [128, 128], F32, kind="ExternalInput").ap()
    maskf = nc.dram_tensor("maskf", [128, 128], F32, kind="ExternalInput").ap()
    masktf = nc.dram_tensor("masktf", [128, 128], F32, kind="ExternalInput").ap()
    onesf = nc.dram_tensor("onesf", [1, 128], F32, kind="ExternalInput").ap()

    out = nc.dram_tensor("out", [D, NOWN], F32, kind="ExternalOutput").ap()

    spillkind = "ExternalOutput" if DEBUG >= 2 else "Internal"
    hdn_spill = nc.dram_tensor("hdn_spill", [INTER, NOWN], BF16, kind=spillkind).ap()
    at_spill = nc.dram_tensor("at_spill", [D, NOWN], BF16, kind=spillkind).ap()

    dbg = {}
    if DEBUG:
        dbg["d_kcT"] = nc.dram_tensor("d_kcT", [128, S], BF16, kind="ExternalOutput").ap()
        dbg["d_vc"] = nc.dram_tensor("d_vc", [128, S], BF16, kind="ExternalOutput").ap()
        dbg["d_xnT"] = nc.dram_tensor("d_xnT", [128, ND * NOWN], BF16, kind="ExternalOutput").ap()

    sb = nc.alloc_sbuf_tensor
    scX = Scope()
    xnT = scX.sbuf(nc, "xnT", [128, ND * NOWN], BF16, side="right")
    kcT = sb("kcT", [128, S], BF16)
    vc = sb("vc", [128, S], BF16)
    s_cosk = sb("s_cosk", [64, S], BF16)
    s_srotk = sb("s_srotk", [64, S], BF16)
    s_cosq = scX.sbuf(nc, "s_cosq", [64, NOWN], BF16, side="right")
    s_srotq = scX.sbuf(nc, "s_srotq", [64, NOWN], BF16, side="right")
    s_idb = sb("s_idb", [128, 128], BF16)
    s_idf = sb("s_idf", [128, 128], F32)
    s_mask = sb("s_mask", [128, 128], F32)
    s_maskt = sb("s_maskt", [128, 128], F32)
    s_onesb = sb("s_onesb", [128, 128], BF16)
    s_ones = sb("s_ones", [1, 128], F32)
    wks = sb("wks", [128, H], F32)
    wvs = sb("wvs", [128, H], F32)
    gsv = sb("gsv", [128, NI], F32)
    usv = sb("usv", [128, NI], F32)
    dsv = sb("dsv", [128, ND], F32)

    sem = nc.alloc_semaphore
    sc1 = Scope()
    wkvb = sc1.sbuf(nc, "wkvb", [128, ND * KV], BF16)

    sc0 = Scope()
    whk = sc0.sbuf(nc, "whk", [128, H * 128], F32)
    scr44 = sc0.sbuf(nc, "scr44", [NI, 128], F32)
    scr44b = sc0.sbuf(nc, "scr44b", [NI, 128], F32)
    scr16 = sc0.sbuf(nc, "scr16", [ND, 128], F32)
    p0a = sc0.psum(nc, "p0a", [128, 512], F32)
    s_c0 = sem("s_c0")
    s_g0 = sem("s_g0")
    s_c1 = sem("s_c1")

    with nc.Block() as blk:
        @blk.sync
        def _(sync):
            for dst, src in (
                (s_cosk[:], cosk), (s_cosq[:], cosq),
                (s_srotk[:], srotk), (s_srotq[:], srotq),
                (s_idb[:], identbf), (s_idf[:], identf),
                (s_mask[:], maskf), (s_maskt[:], masktf), (s_ones[:], onesf),
                (scr44[:], gates), (scr44b[:], ups),
                (scr16[:], downs),
                (whk[:].rearrange("p (h k) -> p h k", k=128),
                 wkh.rearrange("h l k -> l h k")),
            ):
                sync.dma_start(dst, src).then_inc(s_c0, 16)

        @blk.gpsimd
        def _(gp):
            gp.memset(s_onesb[:], 1.0)
            gp.dma_start(wkvb[:].rearrange("p (j k) -> p j k", k=KV),
                         wkvT.rearrange("(j l) k -> l j k", l=128)).then_inc(s_g0, 16)

        @blk.vector
        def _(v):
            v.wait_ge(s_c0, 13 * 16)
            v.wait_ge(s_g0, 16)
            for h in range(H):
                v.reduce_sum(wks[:, h:h + 1], whk[:, h * 128:(h + 1) * 128], axis=AX.X)
            v.nop().then_inc(s_c1, 1)

        @blk.tensor
        def _(t):
            t.wait_ge(s_c0, 13 * 16)
            t.transpose(p0a[0:128, 0:NI], scr44[:], s_idf[0:NI, 0:NI])
            t.transpose(p0a[0:128, 64:64 + NI], scr44b[:], s_idf[0:NI, 0:NI])
            t.transpose(p0a[0:128, 128:128 + ND], scr16[:], s_idf[0:ND, 0:ND]).then_inc(s_c1, 1)

        @blk.scalar
        def _(s):
            load_act_set(nc, s, SET_LNEXP)
            s.wait_ge(s_c1, 2)
            s.copy(gsv[:], p0a[0:128, 0:NI])
            s.copy(usv[:], p0a[0:128, 64:64 + NI])
            s.copy(dsv[:], p0a[0:128, 128:128 + ND])

    with nc.Block() as blk:
        @blk.sync
        def _(sync):
            sync.dma_start(whk[:].rearrange("p (h k) -> p h k", k=128),
                           wvh.rearrange("h l k -> l h k")).then_inc(s_c0, 16)

        @blk.vector
        def _(v):
            v.wait_ge(s_c0, 14 * 16)
            for h in range(H):
                ins = v.reduce_sum(wvs[:, h:h + 1], whk[:, h * 128:(h + 1) * 128], axis=AX.X)
            ins.then_inc(s_c1, 1)

    sc0.close()

    xbuf = [sc1.sbuf(nc, f"xbuf{i}", [128, D], F32) for i in range(3)]
    xsq = sc1.sbuf(nc, "xsq", [128, D], BF16)
    xnb = [sc1.sbuf(nc, f"xnb{i}", [128, D], BF16) for i in range(2)]
    xns = [sc1.sbuf(nc, f"xns{i}", [128, ND * 128], BF16) for i in range(2)]
    kctile = [sc1.sbuf(nc, f"kctile{i}", [128, 128], BF16) for i in range(2)]
    sm_sumx = sc1.sbuf(nc, "sm_sumx", [128, 2], F32)
    sm_sumsq = sc1.sbuf(nc, "sm_sumsq", [128, 2], F32)
    sm_mean = sc1.sbuf(nc, "sm_mean", [128, 2], F32)
    sm_msqe = sc1.sbuf(nc, "sm_msqe", [128, 2], F32)
    sm_varp = sc1.sbuf(nc, "sm_varp", [128, 2], F32)
    sm_lnv = sc1.sbuf(nc, "sm_lnv", [128, 2], F32)
    sm_rstd = sc1.sbuf(nc, "sm_rstd", [128, 2], F32)
    ptr = [sc1.psum(nc, f"ptr{i}", [128, 1024], BF16) for i in range(4)]
    pkv = [sc1.psum(nc, f"pkv{i}", [128, 512], F32) for i in range(2)]
    pkt = [sc1.psum(nc, f"pkt{i}", [128, 1024], BF16) for i in range(2)]

    s_x = [sem(f"s_x{i}") for i in range(3)]
    s_sq = sem("s_sq")
    s_var = sem("s_var")
    s_rstd = sem("s_rstd")
    s_xnb = sem("s_xnb")
    s_tr = sem("s_tr")
    s_xe = sem("s_xe")
    s_kv = sem("s_kv")
    s_kce = sem("s_kce")
    s_vce = sem("s_vce")
    s_kt = sem("s_kt")
    s_kte = sem("s_kte")
    s_vch = sem("s_vch")
    s_sch = sem("s_sch")

    NTOT = NT_ROW + NT_OWN

    with nc.Block() as blk:
        @blk.sync
        def _(sync):
            for ti in range(NTOT):
                if ti >= 3:
                    sync.wait_ge(s_sq, ti - 2)
                    sync.wait_ge(s_xnb, ti - 2)
                src = x_row[ti * 128:(ti + 1) * 128, :] if ti < NT_ROW else \
                    x_own[(ti - NT_ROW) * 128:(ti - NT_ROW + 1) * 128, :]
                sync.dma_start(xbuf[ti % 3][:], src).then_inc(s_x[ti % 3], 16)

        @blk.vector
        def _(v):
            nch = 0
            for tt in range(NTOT + 1):
                if tt < NTOT:
                    ti, p = tt, tt % 2
                    v.wait_ge(s_x[ti % 3], 16 * (ti // 3 + 1))
                    v.reduce_sum(sm_sumx[:, p:p + 1], xbuf[ti % 3][:], axis=AX.X).then_inc(s_vch, 1); nch += 1
                    v.wait_ge(s_vch, nch)
                    v.tensor_scalar(sm_mean[:, p:p + 1], sm_sumx[:, p:p + 1], 1.0 / D,
                                    None, ALU.mult).then_inc(s_vch, 1); nch += 1
                    v.wait_ge(s_vch, nch)
                    v.tensor_scalar(sm_msqe[:, p:p + 1], sm_mean[:, p:p + 1],
                                    sm_mean[:, p:p + 1], EPS, ALU.mult, ALU.subtract).then_inc(s_vch, 1); nch += 1
                    v.wait_ge(s_sq, ti + 1)
                    v.wait_ge(s_vch, nch)
                    v.tensor_scalar(sm_varp[:, p:p + 1], sm_sumsq[:, p:p + 1], 1.0 / D,
                                    sm_msqe[:, p:p + 1], ALU.mult, ALU.subtract).then_inc(s_var, 1)
                if tt >= 1:
                    ti, p = tt - 1, (tt - 1) % 2
                    v.wait_ge(s_rstd, ti + 1)
                    if ti >= 2:
                        v.wait_ge(s_tr, 2 * (ti - 1))
                    v.tensor_scalar(xnb[p][:], xbuf[ti % 3][:], sm_mean[:, p:p + 1],
                                    sm_rstd[:, p:p + 1], ALU.subtract, ALU.mult).then_inc(s_xnb, 1)

        @blk.scalar
        def _(s):
            for tt in range(NTOT + 1):
                if tt < NTOT:
                    ti, p = tt, tt % 2
                    s.wait_ge(s_x[ti % 3], 16 * (ti // 3 + 1))
                    s.activation(xsq[:], xbuf[ti % 3][:], AF.Square,
                                 accum_out=sm_sumsq[:, p:p + 1]).then_inc(s_sq, 1)
                    s.wait_ge(s_var, ti + 1)
                    s.activation(sm_lnv[:, p:p + 1], sm_varp[:, p:p + 1], AF.Ln).then_inc(s_sch, 1)
                    s.wait_ge(s_sch, ti + 1)
                    s.activation(sm_rstd[:, p:p + 1], sm_lnv[:, p:p + 1], AF.Exp,
                                 scale=-0.5).then_inc(s_rstd, 1)
                if tt >= 1:
                    ti, p = tt - 1, (tt - 1) % 2
                    s.wait_ge(s_tr, 2 * ti + 1)
                    if ti < NT_ROW:
                        s.copy(xns[p][:, 0:1024], ptr[(2 * ti) % 4][:]).then_inc(s_xe, 1)
                        s.wait_ge(s_tr, 2 * ti + 2)
                        s.copy(xns[p][:, 1024:2048], ptr[(2 * ti + 1) % 4][:]).then_inc(s_xe, 1)
                        s.wait_ge(s_kv, ti + 1)
                        s.copy(kctile[p][:], pkv[p][:, 0:128]).then_inc(s_kce, 1)
                        s.copy(vc[:, ti * 128:(ti + 1) * 128], pkv[p][:, 128:256]).then_inc(s_vce, 1)
                        s.wait_ge(s_kt, ti + 1)
                        s.copy(kcT[:, ti * 128:(ti + 1) * 128], pkt[p][:, 0:128]).then_inc(s_kte, 1)
                    else:
                        o = ti - NT_ROW
                        xv = xnT[:].rearrange("p (j t) -> p j t", t=NOWN)
                        s.copy(xv[:, 0:8, o * 128:(o + 1) * 128],
                               ptr[(2 * ti) % 4][:].rearrange("p (j t) -> p j t", t=128)).then_inc(s_xe, 1)
                        s.wait_ge(s_tr, 2 * ti + 2)
                        s.copy(xv[:, 8:16, o * 128:(o + 1) * 128],
                               ptr[(2 * ti + 1) % 4][:].rearrange("p (j t) -> p j t", t=128)).then_inc(s_xe, 1)

        @blk.tensor
        def _(t):
            for tt in range(1, NTOT + 1):
                ti, p = tt - 1, (tt - 1) % 2
                t.wait_ge(s_xnb, ti + 1)
                if ti >= 2:
                    t.wait_ge(s_xe, 2 * (ti - 1))
                for half8 in range(2):
                    bank = ptr[(2 * ti + half8) % 4]
                    for j in range(8):
                        jj = half8 * 8 + j
                        ins = t.transpose(bank[:, j * 128:(j + 1) * 128],
                                          xnb[p][:, jj * 128:(jj + 1) * 128], s_idb[:])
                        if j == 7:
                            ins.then_inc(s_tr, 1)
                if ti < NT_ROW:
                    t.wait_ge(s_xe, 2 * (ti + 1))
                    if ti >= 2:
                        t.wait_ge(s_kce, ti - 1)
                        t.wait_ge(s_vce, ti - 1)
                    for j in range(ND):
                        ins = t.matmul(pkv[p][:, 0:KV], xns[p][:, j * 128:(j + 1) * 128],
                                       wkvb[:, j * KV:(j + 1) * KV],
                                       start=(j == 0), stop=(j == ND - 1))
                    ins.then_inc(s_kv, 1)
                    t.wait_ge(s_kce, ti + 1)
                    if ti >= 2:
                        t.wait_ge(s_kte, ti - 1)
                    t.transpose(pkt[p][:, 0:128], kctile[p][:], s_idb[:]).then_inc(s_kt, 1)

    sc1.close()

    if DEBUG >= 1:
        s_dbg = sem("s_dbg")
        with nc.Block() as blk:
            @blk.sync
            def _(sync):
                sync.dma_start(dbg["d_kcT"], kcT[:]).then_inc(s_dbg, 16)
                sync.dma_start(dbg["d_vc"], vc[:]).then_inc(s_dbg, 16)
                sync.dma_start(dbg["d_xnT"], xnT[:]).then_inc(s_dbg, 16)
                sync.wait_ge(s_dbg, 48)
    if UPTO <= 1:
        return nc

    NTK = NOWN // 512
    sc2a = Scope()
    gwb = [sc2a.sbuf(nc, f"gwb{i}", [128, ND * 128], BF16) for i in range(2)]
    uwb = [sc2a.sbuf(nc, f"uwb{i}", [128, ND * 128], BF16) for i in range(2)]
    hgb = [sc2a.sbuf(nc, f"hgb{i}", [128, 512], BF16) for i in range(2)]
    ubf = [sc2a.sbuf(nc, f"ubf{i}", [128, 512], BF16) for i in range(2)]
    hdnb = [sc2a.sbuf(nc, f"hdnb{i}", [128, 512], BF16) for i in range(2)]
    pg = [sc2a.psum(nc, f"pg{i}", [128, 512], F32) for i in range(2)]
    pu = [sc2a.psum(nc, f"pu{i}", [128, 512], F32) for i in range(2)]

    s_gw = [sem("s_gw0"), sem("s_gw1")]
    s_uw = [sem("s_uw0"), sem("s_uw1")]
    s_gmm = sem("s_gmm")
    s_umm = sem("s_umm")
    s_sil = sem("s_sil")
    s_hdn = sem("s_hdn")
    s_hsp = [sem("s_hsp0"), sem("s_hsp1")]
    s_vc2 = sem("s_vc2")



    with nc.Block() as blk:
        @blk.gpsimd
        def _(gp):
            for i in range(NI):
                if i >= 2:
                    gp.wait_ge(s_gmm, NTK * (i - 1))
                    gp.wait_ge(s_umm, NTK * (i - 1))
                gp.dma_start(gwb[i % 2][:], gateT[i]).then_inc(s_gw[i % 2], 16)
                gp.dma_start(uwb[i % 2][:], upT[i]).then_inc(s_uw[i % 2], 16)

        @blk.tensor
        def _(t):
            for i in range(NI):
                t.wait_ge(s_gw[i % 2], 16 * (i // 2 + 1))
                t.wait_ge(s_uw[i % 2], 16 * (i // 2 + 1))
                for tk in range(NTK):
                    seq = i * NTK + tk
                    if seq >= 2:
                        t.wait_ge(s_sil, seq - 1)
                    for j in range(ND):
                        ins = t.matmul(pg[seq % 2][:], gwb[i % 2][:, j * 128:(j + 1) * 128],
                                       xnT[:, j * NOWN + tk * 512:j * NOWN + (tk + 1) * 512],
                                       start=(j == 0), stop=(j == ND - 1))
                    ins.then_inc(s_gmm, 1)
                    if seq >= 2:
                        t.wait_ge(s_hdn, seq - 1)
                    for j in range(ND):
                        ins = t.matmul(pu[seq % 2][:], uwb[i % 2][:, j * 128:(j + 1) * 128],
                                       xnT[:, j * NOWN + tk * 512:j * NOWN + (tk + 1) * 512],
                                       start=(j == 0), stop=(j == ND - 1))
                    ins.then_inc(s_umm, 1)

        @blk.scalar
        def _(s):
            load_act_set(nc, s, SET_SILU)
            for i in range(NI):
                for tk in range(NTK):
                    seq = i * NTK + tk
                    s.wait_ge(s_gmm, seq + 1)
                    if seq >= 2:
                        s.wait_ge(s_hdn, seq - 1)
                    s.activation(hgb[seq % 2][:], pg[seq % 2][:], AF.Silu,
                                 scale=gsv[:, i:i + 1]).then_inc(s_sil, 1)

        @blk.vector
        def _(v):
            nv = 0
            for i in range(NI):
                for tk in range(NTK):
                    seq = i * NTK + tk
                    v.wait_ge(s_umm, seq + 1)
                    if seq >= 2:
                        v.wait_ge(s_hsp[seq % 2], 16 * (seq // 2))
                    v.tensor_scalar(ubf[seq % 2][:], pu[seq % 2][:], usv[:, i:i + 1],
                                    None, ALU.mult).then_inc(s_vc2, 1); nv += 1
                    v.wait_ge(s_sil, seq + 1)
                    v.wait_ge(s_vc2, nv)
                    v.tensor_mul(hdnb[seq % 2][:], hgb[seq % 2][:],
                                 ubf[seq % 2][:]).then_inc(s_hdn, 1)

        @blk.sync
        def _(sync):
            for i in range(NI):
                for tk in range(NTK):
                    seq = i * NTK + tk
                    sync.wait_ge(s_hdn, seq + 1)
                    sync.dma_start(hdn_spill[i * 128:(i + 1) * 128, tk * 512:(tk + 1) * 512],
                                   hdnb[seq % 2][:]).then_inc(s_hsp[seq % 2], 16)

    sc2a.close()
    if UPTO <= 2:
        return nc

    sc2b = Scope()
    hdnT = sc2b.sbuf(nc, "hdnT", [128, NI * 512], BF16)
    dstrip = [sc2b.sbuf(nc, f"dstrip{i}", [128, NI * 128], BF16) for i in range(4)]
    ysb = [sc2b.sbuf(nc, f"ysb{i}", [128, 512], F32) for i in range(2)]
    xtb = [sc2b.sbuf(nc, f"xtb{i}", [128, 512], F32) for i in range(2)]
    py = [sc2b.psum(nc, f"py{i}", [128, 512], F32) for i in range(2)]

    s_hl = sem("s_hl")
    s_dw = [sem(f"s_dw{i}") for i in range(4)]
    s_dmm = sem("s_dmm")
    s_yv = sem("s_yv")
    s_yst = [sem("s_yst0"), sem("s_yst1")]
    s_xl = [sem("s_xl0"), sem("s_xl1")]
    s_vb2 = sem("s_vb2")

    spill3 = hdn_spill.rearrange("(j l) k -> l j k", l=128)

    with nc.Block() as blk:
        @blk.sync
        def _(sync):
            nsp = NI * NTK
            sync.wait_ge(s_hsp[0], 16 * ((nsp + 1) // 2))
            sync.wait_ge(s_hsp[1], 16 * (nsp // 2))
            for tc in range(NTK):
                if tc >= 1:
                    sync.wait_ge(s_dmm, ND * tc)
                for part in range(4):
                    j0 = (NI * part) // 4
                    j1 = (NI * (part + 1)) // 4
                    if j1 > j0:
                        sync.dma_start(
                            hdnT[:].rearrange("p (j k) -> p j k", k=512)[:, j0:j1, :],
                            spill3[:, j0:j1, tc * 512:(tc + 1) * 512]).then_inc(s_hl, 16)
                for hb in range(ND):
                    seq = tc * ND + hb
                    if seq >= 2:
                        sync.wait_ge(s_yv, seq - 1)
                    sync.dma_start(xtb[seq % 2][:],
                                   xT_own[hb * 128:(hb + 1) * 128, tc * 512:(tc + 1) * 512]
                                   ).then_inc(s_xl[seq % 2], 16)
                    sync.wait_ge(s_yv, seq + 1)
                    sync.dma_start(out[hb * 128:(hb + 1) * 128, tc * 512:(tc + 1) * 512],
                                   ysb[seq % 2][:]).then_inc(s_yst[seq % 2], 16)

        @blk.gpsimd
        def _(gp):
            for tc in range(NTK):
                for hb in range(ND):
                    seq = tc * ND + hb
                    if seq >= 4:
                        gp.wait_ge(s_dmm, seq - 3)
                    gp.dma_start(dstrip[seq % 4][:], downT[hb]).then_inc(s_dw[seq % 4], 16)
            gp.wait_ge(s_yst[0], 16 * ((NTK * ND + 1) // 2))
            gp.wait_ge(s_yst[1], 16 * ((NTK * ND) // 2))

        @blk.tensor
        def _(t):
            for tc in range(NTK):
                t.wait_ge(s_hl, 16 * 4 * (tc + 1))
                for hb in range(ND):
                    seq = tc * ND + hb
                    t.wait_ge(s_dw[seq % 4], 16 * (seq // 4 + 1))
                    if seq >= 2:
                        t.wait_ge(s_yv, seq - 1)
                    for jb in range(NI):
                        ins = t.matmul(py[seq % 2][:], dstrip[seq % 4][:, jb * 128:(jb + 1) * 128],
                                       hdnT[:, jb * 512:(jb + 1) * 512],
                                       start=(jb == 0), stop=(jb == NI - 1))
                    ins.then_inc(s_dmm, 1)

        @blk.vector
        def _(v):
            for tc in range(NTK):
                for hb in range(ND):
                    seq = tc * ND + hb
                    v.wait_ge(s_dmm, seq + 1)
                    if seq >= 2:
                        v.wait_ge(s_yst[seq % 2], 16 * (seq // 2))
                    v.tensor_scalar(ysb[seq % 2][:], py[seq % 2][:], dsv[:, hb:hb + 1],
                                    None, ALU.mult).then_inc(s_vb2, 1)
                    v.wait_ge(s_vb2, seq + 1)
                    v.wait_ge(s_xl[seq % 2], 16 * (seq // 2 + 1))
                    v.tensor_add(ysb[seq % 2][:], ysb[seq % 2][:],
                                 xtb[seq % 2][:]).then_inc(s_yv, 1)

    sc2b.close()
    if UPTO <= 3:
        return nc

    scheds = (sched_for(0), sched_for(1))
    NSLOT = len(HALF_CHUNKS[0])
    PQ = max(L for sch in scheds for L, _ in sch)
    NQC = NOWN // 512

    scQ = Scope()
    QT = scQ.sbuf(nc, "QT", [128, H * NOWN], BF16)
    wvball = scQ.sbuf(nc, "wvball", [128, H * 128], BF16)

    sc3 = Scope()
    wqb = [sc3.sbuf(nc, f"wqb{i}", [128, ND * 128], BF16) for i in range(2)]
    qta = sc3.sbuf(nc, "qta", [32, 512], F32)
    qtb = sc3.sbuf(nc, "qtb", [32, 512], F32)
    qtc = sc3.sbuf(nc, "qtc", [32, 512], F32)
    qtd = sc3.sbuf(nc, "qtd", [32, 512], F32)
    vrowq = sc3.sbuf(nc, "vrowq", [1, 128], F32)
    pq3 = [sc3.psum(nc, f"pq3{i}", [128, 512], F32) for i in range(2)]
    pbw = sc3.psum(nc, "pbw", [128, 256], F32)

    s_wq = [sem("s_wq0"), sem("s_wq1")]
    s_qmm = sem("s_qmm")
    s_qe = sem("s_qe")
    s_qrope = sem("s_qrope")
    s_v3 = sem("s_v3")
    s_bt = sem("s_bt")
    s_bv = sem("s_bv")
    s_bm = sem("s_bm")
    s_bw = sem("s_bw")

    with nc.Block() as blk:
        @blk.gpsimd
        def _(gp):
            for h in range(H):
                if h >= 2:
                    gp.wait_ge(s_qmm, NQC * (h - 1))
                gp.dma_start(wqb[h % 2][:], wqT[h]).then_inc(s_wq[h % 2], 16)

        @blk.tensor
        def _(t):
            for h in range(H):
                t.wait_ge(s_wq[h % 2], 16 * (h // 2 + 1))
                for nk in range(NQC):
                    seq = h * NQC + nk
                    if seq >= 2:
                        t.wait_ge(s_qe, seq - 1)
                        t.wait_ge(s_qrope, seq - 1)
                    for j in range(ND):
                        ins = t.matmul(pq3[seq % 2][:], wqb[h % 2][:, j * 128:(j + 1) * 128],
                                       xnT[:, j * NOWN + nk * 512:j * NOWN + (nk + 1) * 512],
                                       start=(j == 0), stop=(j == ND - 1))
                    ins.then_inc(s_qmm, 1)
            t.wait_ge(s_c1, 3)
            for h in range(H):
                if h >= 1:
                    t.wait_ge(s_bv, h)
                    t.wait_ge(s_bw, h)
                t.transpose(pbw[0:1, 0:128], wvs[:, h:h + 1], s_idf[:]).then_inc(s_bt, 1)
                t.wait_ge(s_bv, h + 1)
                if h >= 1:
                    t.wait_ge(s_bw, h)
                t.matmul(pbw[:, 128:256], s_ones[0:1, 0:128], vrowq[:],
                         start=True, stop=True).then_inc(s_bm, 1)

        @blk.scalar
        def _(sx):
            for h in range(H):
                for nk in range(NQC):
                    seq = h * NQC + nk
                    sx.wait_ge(s_qmm, seq + 1)
                    sx.activation(QT[64:128, h * NOWN + nk * 512:h * NOWN + (nk + 1) * 512],
                                  pq3[seq % 2][64:128, :], AF.Copy,
                                  scale=SCL).then_inc(s_qe, 1)
            for h in range(H):
                sx.wait_ge(s_bt, h + 1)
                sx.copy(vrowq[:], pbw[0:1, 0:128]).then_inc(s_bv, 1)
                sx.wait_ge(s_bm, h + 1)
                sx.copy(wvball[:, h * 128:(h + 1) * 128], pbw[:, 128:256]).then_inc(s_bw, 1)

        @blk.vector
        def _(v):
            nch = 0
            for h in range(H):
                for nk in range(NQC):
                    seq = h * NQC + nk
                    v.wait_ge(s_qmm, seq + 1)
                    v.wait_ge(s_v3, nch)
                    v.wait_ge(s_qrope, seq)
                    pa = pq3[seq % 2]
                    ca, sa = pa[0:32, :], pa[32:64, :]
                    ck = s_cosq[:, nk * 512:(nk + 1) * 512]
                    rq = s_srotq[:, nk * 512:(nk + 1) * 512]
                    qcol = h * NOWN + nk * 512
                    v.tensor_mul(qta[:], ca, ck[0:32, :]).then_inc(s_v3, 1); nch += 1
                    v.tensor_mul(qtb[:], sa, rq[32:64, :]).then_inc(s_v3, 1); nch += 1
                    v.tensor_mul(qtc[:], sa, ck[32:64, :]).then_inc(s_v3, 1); nch += 1
                    v.tensor_mul(qtd[:], ca, rq[0:32, :]).then_inc(s_v3, 1); nch += 1
                    v.wait_ge(s_v3, nch)
                    v.tensor_sub(QT[0:32, qcol:qcol + 512], qta[:], qtb[:]).then_inc(s_v3, 1)
                    nch += 1
                    v.tensor_add(QT[32:64, qcol:qcol + 512], qtc[:], qtd[:]).then_inc(s_qrope, 1)

    sc3.close()
    scX.close()

    sc4 = Scope()
    kbuf = [sc4.sbuf(nc, f"kbuf{i}", [128, S], BF16) for i in range(2)]
    vbuf = [sc4.sbuf(nc, f"vbuf{i}", [128, S], BF16) for i in range(2)]
    ktmp = sc4.sbuf(nc, "ktmp", [64, S], BF16)
    pts = [sc4.sbuf(nc, f"pts{i}", [128, 512], BF16) for i in range(3)]
    atst = [sc4.sbuf(nc, f"atst{i}", [128, 512], BF16) for i in range(2)]
    mxc = sc4.sbuf(nc, "mxc", [128, 8], F32)
    nmq = sc4.sbuf(nc, "nmq", [128, 8], F32)
    nmrow = [sc4.sbuf(nc, f"nmrow{i}", [1, 512], BF16) for i in range(2)]
    srow = sc4.sbuf(nc, "srow", [1, 512], F32)
    rrow = sc4.sbuf(nc, "rrow", [1, 512], BF16)
    rbsb = sc4.sbuf(nc, "rbsb", [128, 512], BF16)

    sp1 = [sc4.psum(nc, f"sp1{i}", [128, 512], F32) for i in range(3)]
    sp2 = [sc4.psum(nc, f"sp2{i}", [128, 512], F32) for i in range(2)]
    pvp = sc4.psum(nc, "pvp", [128, 512], F32)
    prow = sc4.psum(nc, "prow", [128, 512], F32)
    rbank = sc4.psum(nc, "rbank", [128, 512], F32)

    s_m1 = sem("s_m1")
    s_mx = sem("s_mx")
    s_cmb = sem("s_cmb")
    s_prt = sem("s_prt")
    s_nr = sem("s_nr")
    s_m2 = sem("s_m2")
    s_msk = sem("s_msk")
    s_ex = sem("s_ex")
    s_z = sem("s_z")
    s_pv = sem("s_pv")
    s_sg = sem("s_sg")
    s_rr = sem("s_rr")
    s_rb = sem("s_rb")
    s_rbe = sem("s_rbe")
    s_ae = sem("s_ae")
    s_asp = [sem("s_asp0"), sem("s_asp1")]
    s_kvb = sem("s_kvb")
    s_kb2 = sem("s_kb2")
    s_v4 = sem("s_v4")

    syncq = nc.sync
    for idx in (range(P4_NH * NSLOT) if P4_STAGE >= 3 else ()):
        syncq.wait_ge(s_ae, idx + 1)
        h, k = idx // NSLOT, idx % NSLOT
        syncq.dma_start(at_spill[h * 128:(h + 1) * 128, k * 512:(k + 1) * 512],
                        atst[idx % 2][:]).then_inc(s_asp[idx % 2], 16)

    def emit_p4(sched, cnt):
        t, v, s = nc.tensor, nc.vector, nc.scalar

        def vch(ins):
            ins.then_inc(s_v4, 1)
            cnt["v4"] += 1
            return ins

        def vwait():
            v.wait_ge(s_v4, cnt["v4"])

        a_guard = [[], [], []]
        b_guard = [[], []]
        load_act_set(nc, s, SET_LNEXP)
        if DEBUG >= 3:
            v.memset(mxc[:], 0.0)
            vch(v.memset(nmq[:], 0.0))

        def emit_kv(g, gm1, gm2):
            gp_ = g % 2
            if g >= 1:
                v.wait_ge(s_m1, gm1)
                v.wait_ge(s_m2, gm2)
            vch(v.tensor_scalar(kbuf[gp_][:], kcT[:], wks[:, g:g + 1], None, ALU.mult))
            yield
            vwait()
            vch(v.tensor_mul(ktmp[0:32, :], kbuf[gp_][32:64, :], s_srotk[32:64, :]))
            yield
            vch(v.tensor_mul(ktmp[32:64, :], kbuf[gp_][0:32, :], s_srotk[0:32, :]))
            yield
            vwait()
            vch(v.tensor_mul(kbuf[gp_][0:32, :], kbuf[gp_][0:32, :], s_cosk[0:32, :]))
            yield
            vch(v.tensor_mul(kbuf[gp_][32:64, :], kbuf[gp_][32:64, :], s_cosk[32:64, :]))
            yield
            vwait()
            vch(v.tensor_sub(kbuf[gp_][0:32, :], kbuf[gp_][0:32, :], ktmp[0:32, :]))
            yield
            v.tensor_add(kbuf[gp_][32:64, :], kbuf[gp_][32:64, :],
                         ktmp[32:64, :]).then_inc(s_kb2, 1)

        def emit_v(g, gpv):
            gp_ = g % 2
            if g == 0:
                v.wait_ge(s_bw, H)
            if g >= 2:
                v.wait_ge(s_pv, gpv)
            wvb_b = wvball[:, g * 128:(g + 1) * 128].rearrange(
                "p (j k) -> p j k", j=1).broadcast_to((128, NT_ROW, 128))
            v.tensor_tensor(vbuf[gp_][:].rearrange("p (j k) -> p j k", k=128),
                            vc[:].rearrange("p (j k) -> p j k", k=128),
                            wvb_b, ALU.mult).then_inc(s_kvb, 1)

        def emit_A(h, k, sp, L, tqs):
            hp = h % 2
            qs0 = h * NOWN + k * 512
            for q in range(4):
                tq = tqs[q]
                nch = (tq + 511) // 512
                v.wait_ge(s_cmb, cnt["cmb"])
                for c in range(nch):
                    w = min(512, tq - c * 512)
                    ar = cnt["m1"] % 3
                    for smv in a_guard[ar]:
                        t.wait_ge(*smv)
                    a_guard[ar] = []
                    t.matmul(sp1[ar][:, 0:w], QT[:, qs0 + q * 128:qs0 + (q + 1) * 128],
                             kbuf[hp][:, c * 512:c * 512 + w],
                             start=True, stop=True).then_inc(s_m1, 1)
                    cnt["m1"] += 1
                    v.wait_ge(s_m1, cnt["m1"])
                    if c == nch - 1:
                        vch(v.tensor_tensor(sp1[ar][:, w - 128:w], sp1[ar][:, w - 128:w],
                                            s_mask[:], ALU.add))
                        vwait()
                    v.reduce_max(mxc[:, c:c + 1], sp1[ar][:, 0:w],
                                 axis=AX.X).then_inc(s_mx, 1)
                    cnt["mx"] += 1
                    a_guard[ar] = [(s_mx, cnt["mx"])]
                    yield
                v.wait_ge(s_prt, cnt["prt_par"][sp])
                v.wait_ge(s_mx, cnt["mx"])
                v.reduce_max(nmq[:, sp * 4 + q:sp * 4 + q + 1], mxc[:, 0:nch],
                             axis=AX.X, negate=True).then_inc(s_cmb, 1)
                cnt["cmb"] += 1
            for q in range(4):
                t.wait_ge(s_cmb, cnt["cmb"] - 3 + q)
                if q == 0:
                    t.wait_ge(s_nr, cnt["nr"])
                t.transpose(prow[0:1, q * 128:(q + 1) * 128],
                            nmq[:, sp * 4 + q:sp * 4 + q + 1],
                            s_idf[:]).then_inc(s_prt, 1)
                cnt["prt"] += 1
                yield
            cnt["prt_par"][sp] = cnt["prt"]
            s.wait_ge(s_prt, cnt["prt"])
            s.wait_ge(s_m2, cnt["m2_sl"][sp])
            s.copy(nmrow[sp][:], prow[0:1, 0:512]).then_inc(s_nr, 1)
            cnt["nr"] += 1

        def emit_sigpv(hp, ntb, tb, ex_cnt, z_cnt, g):
            ptsb = pts[g % 3]
            t.wait_ge(s_ex, ex_cnt)
            if z_cnt is not None:
                t.wait_ge(s_z, z_cnt)
            if tb == 0:
                t.wait_ge(s_rbe, cnt["rbe"])
                t.wait_ge(s_ae, cnt["ae"])
            ins = t.matmul(rbank[0:1, 0:512], s_onesb[:, 0:1], ptsb[:],
                           start=(tb == 0), stop=(tb == ntb - 1))
            if tb == ntb - 1:
                ins.then_inc(s_sg, 1)
                cnt["sg"] += 1
            t.matmul(pvp[:], vbuf[hp][:, tb * 128:(tb + 1) * 128], ptsb[:],
                     start=(tb == 0), stop=(tb == ntb - 1)).then_inc(s_pv, 1)
            cnt["pv"] += 1
            cnt["pv_hist"][g] = cnt["pv"]

        def emit_CD(h, k, sp, L, tqs, prtv):
            hp = h % 2
            ntb = L // 128
            qs0 = h * NOWN + k * 512
            if k == 0:
                t.wait_ge(s_kvb, h + 1)
            t.wait_ge(s_nr, cnt["nr"])
            pends = []
            for tb in range(ntb):
                g2 = cnt["m2"] % 2
                for smv in b_guard[g2]:
                    t.wait_ge(*smv)
                b_guard[g2] = []
                t.matmul(sp2[g2][:], kbuf[hp][:, tb * 128:(tb + 1) * 128],
                         QT[:, qs0:qs0 + 512], start=True, stop=False)
                t.matmul(sp2[g2][:], s_onesb[0:1, 0:128], nmrow[sp][:],
                         start=False, stop=True).then_inc(s_m2, 1)
                cnt["m2"] += 1
                dq = tb - (ntb - 4)
                if 0 <= dq < 4:
                    v.wait_ge(s_m2, cnt["m2"])
                    v.tensor_tensor(sp2[g2][:, dq * 128:(dq + 1) * 128],
                                    sp2[g2][:, dq * 128:(dq + 1) * 128],
                                    s_maskt[:], ALU.add).then_inc(s_msk, 1)
                    cnt["msk"] += 1
                g = cnt["gtb"]
                cnt["gtb"] += 1
                ptsb = pts[g % 3]
                s.wait_ge(s_m2, cnt["m2"])
                if 0 <= dq < 4:
                    s.wait_ge(s_msk, cnt["msk"])
                if g >= 3:
                    s.wait_ge(s_pv, cnt["pv_hist"][g - 3])
                s.activation(ptsb[:], sp2[g2][:], AF.Exp).then_inc(s_ex, 1)
                cnt["ex"] += 1
                b_guard[g2] = [(s_ex, cnt["ex"])]
                zq = tb - (ntb - 3)
                if zq >= 0:
                    v.wait_ge(s_ex, cnt["ex"])
                    nz = min(zq, 2) + 1
                    for q in range(nz):
                        ins = v.memset(ptsb[:, q * 128:(q + 1) * 128], 0.0)
                        if q < nz - 1:
                            ins.then_inc(s_v4, 1)
                            cnt["v4"] += 1
                    ins.then_inc(s_z, 1)
                    cnt["z"] += 1
                if len(pends) == 2:
                    emit_sigpv(hp, ntb, *pends.pop(0))
                pends.append((tb, cnt["ex"], cnt["z"] if zq >= 0 else None, g))
                yield
            for p in pends:
                emit_sigpv(hp, ntb, *p)
            s.wait_ge(s_sg, cnt["sg"])
            s.activation(srow[:], rbank[0:1, 0:512], AF.Ln).then_inc(s_rr, 1)
            cnt["rr"] += 1
            s.wait_ge(s_rr, cnt["rr"])
            s.activation(rrow[:], srow[:], AF.Exp, scale=-1.0).then_inc(s_rr, 1)
            cnt["rr"] += 1
            t.wait_ge(s_rr, cnt["rr"])
            t.matmul(rbank[:, 0:512], s_onesb[0:1, 0:128], rrow[:],
                     start=True, stop=True).then_inc(s_rb, 1)
            cnt["rb"] += 1
            s.wait_ge(s_rb, cnt["rb"])
            s.copy(rbsb[:], rbank[:, 0:512]).then_inc(s_rbe, 1)
            cnt["rbe"] += 1
            ai = cnt["ae"]
            v.wait_ge(s_rbe, cnt["rbe"])
            v.wait_ge(s_pv, cnt["pv"])
            if ai >= 2:
                v.wait_ge(s_asp[ai % 2], 16 * (ai // 2))
            v.tensor_tensor(atst[ai % 2][:], pvp[:], rbsb[:], ALU.mult).then_inc(s_ae, 1)
            cnt["ae"] += 1
            cnt["m2_sl"][sp] = cnt["m2"]

        for _ in emit_kv(0, 0, 0):
            pass
        emit_v(0, 0)
        if P4_STAGE < 3:
            for g in range(1, P4_NH):
                for _ in emit_kv(g, 0, 0):
                    pass
                emit_v(g, 0)
            return

        state = dict(pending=None, kvgen=None, vhead=0)

        def step_bg():
            if state["kvgen"] is not None:
                try:
                    next(state["kvgen"])
                except StopIteration:
                    state["kvgen"] = None
                    emit_v(state["vhead"], cnt["pv"])

        def step_pend():
            if state["pending"] is None:
                return False
            try:
                next(state["pending"])
                return True
            except StopIteration:
                state["pending"] = None
                return False

        kv_created = 0
        for h in range(P4_NH):
            while state["kvgen"] is not None:
                step_bg()
            t.wait_ge(s_kb2, h + 1)
            for k in range(NSLOT):
                L, tqs = sched[k]
                sp = cnt["sl"] % 2
                cnt["sl"] += 1
                gA = emit_A(h, k, sp, L, tqs)
                zc = 0
                for _ in gA:
                    step_pend()
                    zc += 1
                    if zc % 3 == 0:
                        step_bg()
                while step_pend():
                    zc += 1
                    if zc % 3 == 0:
                        step_bg()
                if k == 0 and h + 1 < P4_NH and kv_created == h:
                    state["kvgen"] = emit_kv(h + 1, cnt["m1"], cnt["m2"])
                    state["vhead"] = h + 1
                    kv_created = h + 1
                state["pending"] = emit_CD(h, k, sp, L, tqs, cnt["prt"])
        while state["pending"] is not None or state["kvgen"] is not None:
            step_pend()
            step_bg()

    def fresh_cnt():
        return dict(m1=0, mx=0, cmb=0, prt=0, nr=0, m2=0, msk=0, ex=0, z=0,
                    pv=0, sg=0, rr=0, rb=0, rbe=0, ae=0, v4=0, sl=0, gtb=0,
                    prt_par=[0, 0], m2_sl=[0, 0], pv_hist={})

    engs = [nc.tensor, nc.vector, nc.scalar]
    idxs = [e.partition_id() % 2 for e in engs]
    for ib in nc.Switch(engines=engs, index=idxs, n=2):
        emit_p4(scheds[ib], fresh_cnt())

    if DEBUG >= 3:
        dbg3 = {}
        for nm, tns in (("d_wks", wks), ("d_wvs", wvs),
                        ("d_vbuf", vbuf[(P4_NH - 1) % 2]), ("d_kbuf", kbuf[(P4_NH - 1) % 2]),
                        ("d_qh", QT), ("d_pbuf", pts[0]),
                        ("d_pbuf1", pts[1]), ("d_pbuf2", pts[2]),
                        ("d_nmrow", nmrow[0]), ("d_srow", srow),
                        ("d_rbsb", rbsb),
                        ("d_nmx", nmq), ("d_mxc", mxc)):
            shp = list(tns[:].shape)
            dt = tns[:].dtype
            dbg3[nm] = (nc.dram_tensor(nm, shp, dt, kind="ExternalOutput").ap(), tns)
        s_d3 = sem("s_d3")
        sy = nc.sync
        sy.wait_ge(s_ae, P4_NH * NSLOT)
        nd3 = 0
        for nm, (dap, tns) in dbg3.items():
            sy.dma_start(dap, tns[:]).then_inc(s_d3, 16)
            nd3 += 16
        sy.wait_ge(s_d3, nd3)

    if UPTO <= 4:
        sc4.close()
        scQ.close()
        return nc

    with nc.Block():
        pass
    sc4.close()
    scQ.close()
    sc5 = Scope()
    atsb = sc5.sbuf(nc, "atsb", [128, ND * NOWN], BF16)
    wob = [sc5.sbuf(nc, f"wob{i}", [128, ND * 128], BF16) for i in range(2)]
    oysb = [sc5.sbuf(nc, f"oysb{i}", [128, 512], F32) for i in range(2)]
    pyo = [sc5.psum(nc, f"pyo{i}", [128, 512], F32) for i in range(2)]

    s_al = sem("s_al")
    s_wo = [sem("s_wo0"), sem("s_wo1")]
    s_omm = sem("s_omm")
    s_oye = sem("s_oye")
    s_yo = [sem("s_yo0"), sem("s_yo1")]

    at3 = at_spill.rearrange("(j l) k -> l j k", l=128)

    with nc.Block() as blk:
        @blk.sync
        def _(sync):
            sync.wait_ge(s_asp[0], 16 * ((H * NSLOT + 1) // 2))
            sync.wait_ge(s_asp[1], 16 * ((H * NSLOT) // 2))
            for part in range(4):
                j0, j1 = (ND * part) // 4, (ND * (part + 1)) // 4
                sync.dma_start(atsb[:].rearrange("p (j k) -> p j k", k=NOWN)[:, j0:j1, :],
                               at3[:, j0:j1, :]).then_inc(s_al, 16)

        @blk.gpsimd
        def _(gp):
            NQC5 = NOWN // 512
            nacc = 0

            def emit_acc(gp, seq):
                gp.wait_ge(s_oye, seq + 1)
                ob_, tc_ = seq // NQC5, seq % NQC5
                gp.dma_start(out[ob_ * 128:(ob_ + 1) * 128, tc_ * 512:(tc_ + 1) * 512],
                             oysb[seq % 2][:], accum_op=ALU.add).then_inc(s_yo[seq % 2], 16)

            for ob in range(ND):
                if ob >= 2:
                    while nacc < NQC5 * (ob - 1) - 1:
                        emit_acc(gp, nacc)
                        nacc += 1
                    gp.wait_ge(s_omm, NQC5 * (ob - 1))
                gp.dma_start(wob[ob % 2][:], woT[ob]).then_inc(s_wo[ob % 2], 16)
            while nacc < ND * NQC5:
                emit_acc(gp, nacc)
                nacc += 1
            gp.wait_ge(s_yo[0], 16 * ((ND * NQC5 + 1) // 2))
            gp.wait_ge(s_yo[1], 16 * ((ND * NQC5) // 2))

        @blk.tensor
        def _(t):
            t.wait_ge(s_al, 16 * 4)
            for ob in range(ND):
                t.wait_ge(s_wo[ob % 2], 16 * (ob // 2 + 1))
                for tc in range(NOWN // 512):
                    seq = ob * (NOWN // 512) + tc
                    if seq >= 2:
                        t.wait_ge(s_oye, seq - 1)
                    for j in range(ND):
                        ins = t.matmul(pyo[seq % 2][:], wob[ob % 2][:, j * 128:(j + 1) * 128],
                                       atsb[:, j * NOWN + tc * 512:j * NOWN + (tc + 1) * 512],
                                       start=(j == 0), stop=(j == ND - 1))
                    ins.then_inc(s_omm, 1)

        @blk.scalar
        def _(s):
            for seq in range(ND * (NOWN // 512)):
                s.wait_ge(s_omm, seq + 1)
                if seq >= 2:
                    s.wait_ge(s_yo[seq % 2], 16 * (seq // 2))
                s.copy(oysb[seq % 2][:], pyo[seq % 2][:]).then_inc(s_oye, 1)

    sc5.close()
    return nc



def kernel(L0_attn_x, L0_attn_W_q, L0_attn_W_kv, L0_attn_W_k_heads,
           L0_attn_W_v_heads, L0_attn_W_o, L0_ffn_gate_w, L0_ffn_up_w,
           L0_ffn_down_w, L0_ffn_gate_s, L0_ffn_up_s, L0_ffn_down_s):
    x = np.ascontiguousarray(L0_attn_x, dtype=np.float32)
    tabs = host_tables()
    def tile_blocks(wT, nblk):
        din, dout = wT.shape
        nj = din // 128
        out = np.empty((nblk, 128, din), dtype=wT.dtype)
        w3 = wT.reshape(nj, 128, dout)
        for b in range(nblk):
            out[b] = w3[:, :, b * 128:(b + 1) * 128].transpose(1, 0, 2).reshape(128, din)
        return out

    shared = dict(
        wqT=tile_blocks(np.ascontiguousarray(np.asarray(L0_attn_W_q).T), H),
        wkvT=np.ascontiguousarray(np.asarray(L0_attn_W_kv).T),
        wkh=np.ascontiguousarray(L0_attn_W_k_heads),
        wvh=np.ascontiguousarray(L0_attn_W_v_heads),
        woT=tile_blocks(np.ascontiguousarray(np.asarray(L0_attn_W_o).T), ND),
        gateT=tile_blocks(np.ascontiguousarray(np.asarray(L0_ffn_gate_w).T), NI),
        upT=tile_blocks(np.ascontiguousarray(np.asarray(L0_ffn_up_w).T), NI),
        downT=tile_blocks(np.ascontiguousarray(np.asarray(L0_ffn_down_w).T), ND),
        gates=np.ascontiguousarray(np.asarray(L0_ffn_gate_s).reshape(NI, 128)),
        ups=np.ascontiguousarray(np.asarray(L0_ffn_up_s).reshape(NI, 128)),
        downs=np.ascontiguousarray(np.asarray(L0_ffn_down_s).reshape(ND, 128)),
        cosk=tabs["cosk"], srotk=tabs["srotk"],
        identbf=tabs["identbf"], identf=tabs["identf"], maskf=tabs["maskf"],
        masktf=tabs["masktf"],
        onesf=tabs["onesf"],
    )
    in_maps = []
    for c in range(8):
        b, half = c // 2, c % 2
        rows = own_rows(half)
        m = dict(shared)
        m["x_row"] = np.ascontiguousarray(x[b])
        m["x_own"] = np.ascontiguousarray(x[b][rows])
        m["xT_own"] = np.ascontiguousarray(x[b][rows].T)
        m["cosq"] = np.ascontiguousarray(tabs["cosq_full"][:, rows])
        m["srotq"] = np.ascontiguousarray(tabs["srotq_full"][:, rows])
        in_maps.append(m)

    nc = build_nc()
    res = run_bass_kernel_spmd(nc, in_maps, core_ids=list(range(8)))

    full = np.empty((B, S, D), dtype=np.float32)
    for c in range(8):
        b, half = c // 2, c % 2
        full[b, own_rows(half)] = np.asarray(res.results[c]["out"]).T
    return full
